# version 18
# baseline (speedup 1.0000x reference)
import numpy as np
import ml_dtypes
import bass_rust
import concourse.bass as bass
import concourse.mybir as mybir
from concourse.bass_utils import run_bass_kernel_spmd
from contextlib import ExitStack

F32 = mybir.dt.float32
BF16 = mybir.dt.bfloat16
U8 = mybir.dt.uint8
AF = mybir.ActivationFunctionType
ALU = mybir.AluOpType
AX = mybir.AxisListType

NCORES = 8
SEQ = 2048
DM = 1024
SEQ_PER_CORE = 2
TOK = SEQ * SEQ_PER_CORE
NTS = SEQ // 128
NE = DM // 128
EPS = 1e-5
NEG = -1.0e30


def custom_ap(base_ap, offset_elems, pairs):
    a = base_ap.copy()
    a.offset = offset_elems
    a.ap = bass_rust.VecI64Pair([list(x) for x in pairs])
    return a


class Prog:

    def __init__(self, nc, ctx, extra=()):
        self.nc = nc
        self.ctx = ctx
        self.ten = nc.tensor
        self.vec = nc.vector
        self.sca = nc.scalar
        self.gps = nc.gpsimd
        self.syn = nc.sync
        self.sem = {}
        self.val = {}
        for name in ("t", "v", "s", "g", "d", "d0", "d1", "r0", "r1", "x", "e0", "e1") + tuple(extra):
            self.sem[name] = ctx.enter_context(nc.semaphore("sem_" + name))
            self.val[name] = 0

    def inc(self, instr, name, step=1):
        instr.then_inc(self.sem[name], step)
        self.val[name] += step
        return self.val[name]

    def wait(self, engine, name, v=None):
        if v is None:
            v = self.val[name]
        if v > 0:
            engine.wait_ge(self.sem[name], v)

    def barrier(self, skip=("dc",)):
        for eng in (self.ten, self.vec, self.sca, self.gps, self.syn):
            for name in self.sem:
                if name in skip:
                    continue
                self.wait(eng, name)


class LNBufs:

    def __init__(self, nc, ctx, nslots, pf=""):
        E = ctx.enter_context
        self.stats = E(nc.sbuf_tensor(pf + "ln_stats", [128, 12], F32))
        self.mv = E(nc.sbuf_tensor(pf + "ln_mv", [128, 3 * nslots], F32))
        self.rstd = E(nc.sbuf_tensor(pf + "ln_rstd", [128, nslots], F32))
        self.negmur = E(nc.sbuf_tensor(pf + "ln_negmur", [128, nslots], F32))
        self.free_s = [0] * nslots
        self.free_v = [0] * nslots
        self.eps = E(nc.sbuf_tensor(pf + "ln_eps", [128, 1], F32))

    def init(self, p):
        p.inc(p.vec.memset(self.eps[:], EPS), "v")


def emit_ln_stats(p, lb, slot, src_ap, w):
    vec, sca = p.vec, p.sca
    mv = lb.mv[:, 3 * slot : 3 * slot + 3]
    rstd = lb.rstd[:, slot : slot + 1]
    negmur = lb.negmur[:, slot : slot + 1]
    p.wait(vec, "s", lb.free_s[slot])
    p.wait(sca, "v", lb.free_v[slot])
    nchunk = (w + 511) // 512
    ins = None
    for c in range(nchunk):
        lo = c * 512
        hi = min(w, lo + 512)
        ins = vec.bn_stats(lb.stats[:, 6 * c : 6 * c + 6], src_ap[:, lo:hi])
    p.inc(ins, "v")
    p.wait(vec, "v")
    p.inc(vec.bn_aggr(mv[:, 0:2], lb.stats[:, 0 : 6 * nchunk]), "v")
    p.wait(sca, "v")
    p.inc(sca.activation(rstd, mv[:, 1:2], AF.Ln, bias=lb.eps[:]), "s")
    p.wait(sca, "s")
    s_tok = p.inc(sca.activation(rstd, rstd, AF.Exp, scale=-0.5), "s")
    p.wait(vec, "s", s_tok)
    v_tok = p.inc(
        vec.tensor_scalar(negmur, mv[:, 0:1], rstd, -1.0, ALU.mult, ALU.mult), "v"
    )
    lb.free_s[slot] = s_tok
    lb.free_v[slot] = v_tok
    return v_tok, s_tok, rstd, negmur



TC = 512
NCH = TOK // TC
NF1 = 4096 // 128
NF2 = 4096 // 128
RPAD = 160
TC2 = 512
NCH2 = TOK // TC2
NEXP = 8
NH = 2048 // 128
NK2 = NEXP * NH
EPS32 = float(np.finfo(np.float32).eps)


def P_ap(big_sb, off, w):
    return big_sb[:, off : off + w]


def host_consts():
    ones = np.ones((1, 128), dtype=np.float32)
    idf = np.eye(128, dtype=np.float32)
    cmask = np.zeros((4, 128, 512), dtype=np.float32)
    for r in range(4):
        jl = np.arange(128)[:, None]
        il = np.arange(512)[None, :]
        cmask[r][il < jl + 128 * r] = NEG
    return ones, idf, cmask


_DRAM_CACHE = {}


def dram_cached(nc, name, shape, dtype, kind="Internal"):
    key = (id(nc), name)
    if key not in _DRAM_CACHE:
        _DRAM_CACHE[key] = nc.dram_tensor(name, shape, dtype, kind=kind)
    return _DRAM_CACHE[key]


def emit_l0_attn(nc, ctx, p, x, y):
    Wq = dram_cached(nc, "Wq", [DM, DM], F32, kind="ExternalInput").ap()
    Wk = dram_cached(nc, "Wk", [DM, DM], F32, kind="ExternalInput").ap()
    Wv = dram_cached(nc, "Wv", [DM, DM], F32, kind="ExternalInput").ap()
    bq = dram_cached(nc, "bq", [DM], F32, kind="ExternalInput").ap()
    bk = dram_cached(nc, "bk", [DM], F32, kind="ExternalInput").ap()
    bv = dram_cached(nc, "bv", [DM], F32, kind="ExternalInput").ap()
    ones = dram_cached(nc, "ones", [1, 128], F32, kind="ExternalInput").ap()
    idf = dram_cached(nc, "idf", [128, 128], F32, kind="ExternalInput").ap()
    cmask = dram_cached(nc, "cmask", [4, 128, 512], F32, kind="ExternalInput").ap()
    wstage = dram_cached(nc, "wstage", [3, DM, DM], BF16, kind="Internal").ap()
    xnd = dram_cached(nc, "xnd", [SEQ_PER_CORE, SEQ, DM], BF16, kind="Internal").ap()

    p_off = []
    acc = 0
    for jt in range(NTS):
        bi0 = (128 * jt) // 512
        p_off.append((acc, 512 * bi0, SEQ - 512 * bi0))
        acc += SEQ - 512 * bi0

    with ExitStack() as ctx:
        E = ctx.enter_context
        ten, vec, sca, gps, syn = p.ten, p.vec, p.sca, p.gps, p.syn
        lb = LNBufs(nc, ctx, 4, "a_")
        lb.init(p)

        ones_sb = E(nc.sbuf_tensor("a_" + "ones_sb", [1, 128], F32))
        id8_sb = E(nc.sbuf_tensor("a_" + "id8_sb", [8, 8], F32))
        braw_sb = E(nc.sbuf_tensor("a_" + "braw_sb", [8, 256], F32))
        bcol_sb = E(nc.sbuf_tensor("a_" + "bcol_sb", [128, 16], F32))
        bvrow_sb = E(nc.sbuf_tensor("a_" + "bvrow_sb", [1, DM], F32))
        mask_sb = E(nc.sbuf_tensor("a_" + "mask_sb", [128, 4 * 512], F32))
        sm_sb = E(nc.sbuf_tensor("a_" + "sm_sb", [128, 8], F32))
        xst_sb = E(nc.sbuf_tensor("a_" + "xst_sb", [128, 2 * DM], F32))
        xnst_sb = E(nc.sbuf_tensor("a_" + "xnst_sb", [128, 2 * DM], BF16))
        x2st_sb = E(nc.sbuf_tensor("a_" + "x2st_sb", [128, 2 * DM], F32))
        tmp_sb = E(nc.sbuf_tensor("a_" + "tmp_sb", [128, 2 * DM], F32))
        out_sb = E(nc.sbuf_tensor("a_" + "out_sb", [128, 2 * DM], F32))
        qT_sb = E(nc.sbuf_tensor("a_" + "qT_sb", [128, NE * SEQ], BF16))
        kT_sb = E(nc.sbuf_tensor("a_" + "kT_sb", [128, NE * SEQ], BF16))
        v_sb = E(nc.sbuf_tensor("a_" + "v_sb", [128, NTS * DM], BF16))
        st_sb = E(nc.sbuf_tensor("a_" + "st_sb", [128, SEQ], F32))
        big_sb = E(nc.sbuf_tensor("a_" + "big_sb", [128, 24576], BF16))
        xnT = big_sb[:, 0:16384]
        wTf = big_sb[:, 16384:24576]

        ps_a = E(nc.psum_tensor("a_" + "ps_a", [128, 512], F32))
        ps_b = E(nc.psum_tensor("a_" + "ps_b", [128, 512], F32))
        ps_c = E(nc.psum_tensor("a_" + "ps_c", [128, 512], F32))
        ps_o = [E(nc.psum_tensor("a_" + f"ps_o{i}", [128, DM], F32)) for i in range(2)]


        p.inc(syn.dma_start(ones_sb[:], ones), "d", 16)
        p.inc(syn.dma_start(id8_sb[:], idf[0:8, 0:8]), "d", 16)
        p.inc(syn.dma_start(braw_sb[:, 0:128], bq.rearrange("(a b) -> a b", a=8)), "d", 16)
        p.inc(syn.dma_start(braw_sb[:, 128:256], bk.rearrange("(a b) -> a b", a=8)), "d", 16)
        p.inc(syn.dma_start(bvrow_sb[:], bv.rearrange("(a b) -> a b", a=1)), "d", 16)
        for r in range(4):
            p.inc(syn.dma_start(mask_sb[:, r * 512 : (r + 1) * 512], cmask[r]), "d", 16)
        p.wait(ten, "d")
        p.inc(ten.transpose(ps_c[:, 0:8], braw_sb[0:8, 0:128], id8_sb[:]), "t")
        p.wait(sca, "t")
        p.inc(sca.activation(bcol_sb[:, 0:8], ps_c[:, 0:8], AF.Identity, scale=1.0 / 32.0), "s")
        p.wait(ten, "s")
        p.inc(ten.transpose(ps_c[:, 8:16], braw_sb[0:8, 128:256], id8_sb[:]), "t")
        p.wait(sca, "t")
        p.inc(sca.activation(bcol_sb[:, 8:16], ps_c[:, 8:16], AF.Identity), "s")
        p.barrier()

        a_stage = [(0, 0, 0)] * 2
        a_era_tok = {}
        a_era_half = {}

        def emit_era_tile_A(s_, i_):
            b = i_ % 2
            vf, sf, ef = a_stage[b]
            p.wait(syn, "v", vf)
            p.wait(syn, "s", sf)
            p.wait(syn, "e%d" % b, ef)
            xs = xst_sb[:, b * DM : (b + 1) * DM]
            xns = xnst_sb[:, b * DM : (b + 1) * DM]
            p.wait(syn, "d%d" % b)
            dtok = p.inc(
                syn.dma_start(xs, x[s_ * SEQ + i_ * 128 : s_ * SEQ + (i_ + 1) * 128, :]),
                "d%d" % b, 16,
            )
            p.wait(vec, "d%d" % b, dtok)
            vtok, stok, rstd, negmur = emit_ln_stats(p, lb, b, xs, DM)
            p.wait(vec, "v")
            p.wait(vec, "e%d" % b, ef)
            aptok = p.inc(
                vec.tensor_scalar(xns, xs, rstd, negmur, ALU.mult, ALU.add), "v"
            )
            lb.free_v[b] = aptok
            p.wait(syn, "v", aptok)
            p.wait(syn, "e%d" % b)
            etok = p.inc(
                syn.dma_start(xnd[s_, i_ * 128 : (i_ + 1) * 128, :], xns), "e%d" % b, 16
            )
            a_stage[b] = (aptok, stok, etok)
            a_era_tok[s_] = (p.val["e0"], p.val["e1"])
            if i_ == 7 or i_ == 15:
                a_era_half[(s_, i_ // 8)] = (p.val["e0"], p.val["e1"])

        for i_ in range(NTS):
            emit_era_tile_A(0, i_)

        for s in range(SEQ_PER_CORE):
            xseq = x[s * SEQ : (s + 1) * SEQ, :]
            yseq = y[s * SEQ : (s + 1) * SEQ, :]

            xhalf = [0, 0]
            hsem = ["x", "xs"]
            for half in range(2):
                e0t, e1t = a_era_half[(s, half)]
                p.wait(syn, "e0", e0t)
                p.wait(syn, "e1", e1t)
                p.wait(syn, hsem[half])
                for et in range(NE):
                    p.inc(
                        syn.dma_start(
                            xnT[:, et * SEQ + half * 1024 : et * SEQ + half * 1024 + 1024],
                            xnd[s, half * 1024 : (half + 1) * 1024, et * 128 : (et + 1) * 128],
                            transpose=True,
                        ),
                        hsem[half],
                        16,
                    )
                xhalf[half] = p.val[hsem[half]]

            t_qkv_prev = 0
            evac_parity = {}
            for m in range(3):
                p.wait(syn, "t", t_qkv_prev)
                for et in range(NE):
                    p.inc(
                        syn.dma_start(
                            wTf[:, et * DM : (et + 1) * DM],
                            wstage[m, :, et * 128 : (et + 1) * 128],
                            transpose=True,
                        ),
                        "u0",
                        16,
                    )
                u_m = p.val["u0"]
                if m < 2:
                    dst = qT_sb if m == 0 else kT_sb
                    for tb in range(SEQ // 512):
                        for ft in range(NE):
                            key = (ft * 4 + tb) % 2
                            ps = ps_a if key == 0 else ps_b
                            if key in evac_parity:
                                p.wait(ten, "s", evac_parity[key])
                            if ft == 0:
                                p.wait(ten, ["x", "xs"][tb // 2], xhalf[tb // 2])
                                p.wait(ten, "u0", u_m)
                            last = None
                            for et in range(NE):
                                last = ten.matmul(
                                    ps[:],
                                    wTf[:, et * DM + ft * 128 : et * DM + ft * 128 + 128],
                                    xnT[:, et * SEQ + tb * 512 : et * SEQ + tb * 512 + 512],
                                    start=(et == 0),
                                    stop=(et == NE - 1),
                                )
                            ttok = p.inc(last, "t")
                            p.wait(sca, "t", ttok)
                            kw = (
                                dict(scale=1.0 / 32.0, bias=bcol_sb[:, ft : ft + 1])
                                if m == 0
                                else dict(bias=bcol_sb[:, 8 + ft : 9 + ft])
                            )
                            ins = sca.activation(
                                dst[:, ft * SEQ + tb * 512 : ft * SEQ + tb * 512 + 512],
                                ps[:],
                                AF.Identity,
                                **kw,
                            )
                            evac_parity[key] = p.inc(ins, "s")
                    t_qkv_prev = p.val["t"]
                else:
                    for tt in range(NTS):
                        for fb in range(2):
                            key = (tt * 2 + fb) % 2
                            ps = ps_a if key == 0 else ps_b
                            if key in evac_parity:
                                p.wait(ten, "s", evac_parity[key])
                            if fb == 0:
                                p.wait(ten, ["x", "xs"][tt // 8], xhalf[tt // 8])
                                p.wait(ten, "u0", u_m)
                            for et in range(NE):
                                ten.matmul(
                                    ps[:],
                                    xnT[:, et * SEQ + tt * 128 : et * SEQ + tt * 128 + 128],
                                    wTf[:, et * DM + fb * 512 : et * DM + fb * 512 + 512],
                                    start=(et == 0),
                                    stop=False,
                                )
                            last = ten.matmul(
                                ps[:],
                                ones_sb[:],
                                bvrow_sb[:, fb * 512 : fb * 512 + 512],
                                start=False,
                                stop=True,
                            )
                            ttok = p.inc(last, "t")
                            p.wait(sca, "t", ttok)
                            ins = sca.activation(
                                v_sb[:, tt * DM + fb * 512 : tt * DM + fb * 512 + 512],
                                ps[:],
                                AF.Identity,
                            )
                            evac_parity[key] = p.inc(ins, "s")
                    t_qkv_prev = p.val["t"]
            p.wait(ten, "s")

            prev_exp_s = 0
            ps_free = {0: ("s", 0), 1: ("s", 0)}
            for jt in range(NTS):
                if s == 0:
                    emit_era_tile_A(1, jt)
                if jt % 2 == 0:
                    p.pump(1)
                off, i0, wc = p_off[jt]
                bi0 = i0 // 512
                evac_tok = []
                for ib in range(bi0, 4):
                    key = ib % 2
                    ps = ps_a if key == 0 else ps_b
                    nm, tok = ps_free[key]
                    p.wait(ten, nm, tok)
                    last = None
                    for et in range(NE):
                        last = ten.matmul(
                            ps[:],
                            kT_sb[:, et * SEQ + jt * 128 : et * SEQ + jt * 128 + 128],
                            qT_sb[:, et * SEQ + ib * 512 : et * SEQ + ib * 512 + 512],
                            start=(et == 0),
                            stop=(et == NE - 1),
                        )
                    ttok = p.inc(last, "t")
                    if ib == bi0:
                        r = jt - 4 * bi0
                        p.wait(vec, "t", ttok)
                        p.wait(vec, "s", prev_exp_s)
                        tok = p.inc(
                            vec.tensor_tensor(
                                st_sb[:, ib * 512 : ib * 512 + 512],
                                ps[:],
                                mask_sb[:, r * 512 : r * 512 + 512],
                                ALU.add,
                            ),
                            "v",
                        )
                        evac_tok.append(("v", tok))
                        ps_free[key] = ("v", tok)
                    else:
                        p.wait(sca, "t", ttok)
                        p.wait(sca, "s", prev_exp_s)
                        tok = p.inc(
                            sca.activation(
                                st_sb[:, ib * 512 : ib * 512 + 512], ps[:], AF.Identity
                            ),
                            "s",
                        )
                        evac_tok.append(("s", tok))
                        ps_free[key] = ("s", tok)
                for nm, tok in evac_tok:
                    p.wait(vec, nm, tok)
                p.wait(vec, "s", prev_exp_s)
                p.inc(
                    vec.reduce_max(sm_sb[:, 0:1], st_sb[:, i0:SEQ], axis=AX.X, negate=True),
                    "v",
                )
                for nm, tok in evac_tok:
                    p.wait(sca, nm, tok)
                p.wait(sca, "v")
                prev_exp_s = p.inc(
                    sca.activation(
                        P_ap(big_sb, off, wc),
                        st_sb[:, i0:SEQ],
                        AF.Exp,
                        bias=sm_sb[:, 0:1],
                        accum_out=sm_sb[:, 1:2],
                    ),
                    "s",
                )
                p.wait(vec, "s", prev_exp_s)
                p.inc(vec.reciprocal(sm_sb[:, 2:3], sm_sb[:, 1:2]), "v")
                p.wait(vec, "v")
                p.inc(
                    vec.tensor_scalar(
                        v_sb[:, jt * DM : (jt + 1) * DM],
                        v_sb[:, jt * DM : (jt + 1) * DM],
                        sm_sb[:, 2:3],
                        None,
                        ALU.mult,
                    ),
                    "v",
                )
            p.barrier()

            ofree = [(0, 0, 0)] * 2
            for it in range(NTS):
                if it % 2 == 0:
                    p.pump(1)
                b = it % 2
                vf, sf, ef = ofree[b]
                p.wait(syn, "v", vf)
                x2s = x2st_sb[:, b * DM : (b + 1) * DM]
                dres = p.inc(syn.dma_start(x2s, xseq[it * 128 : (it + 1) * 128, :]), "r%d" % b, 16)
                p.wait(ten, "v", vf)
                p.wait(ten, "s", sf)
                pso = ps_o[b]
                for eb in range(2):
                    first = True
                    last = None
                    for jt in range(it + 1):
                        off, i0, wc = p_off[jt]
                        col = it * 128 - i0
                        last = ten.matmul(
                            pso[:, eb * 512 : eb * 512 + 512],
                            P_ap(big_sb, off + col, 128),
                            v_sb[:, jt * DM + eb * 512 : jt * DM + eb * 512 + 512],
                            start=first,
                            stop=(jt == it),
                        )
                        first = False
                    if eb == 1:
                        ttok = p.inc(last, "t")
                p.wait(vec, "t", ttok)
                vtok, stok, rstd, negmur = emit_ln_stats(p, lb, 2 + b, pso, DM)
                tmps = tmp_sb[:, b * DM : (b + 1) * DM]
                outs = out_sb[:, b * DM : (b + 1) * DM]
                p.wait(sca, "v", vtok)
                p.wait(sca, "s", stok)
                stok2 = p.inc(
                    sca.activation(tmps, pso[:], AF.Identity, bias=negmur, scale=rstd), "s"
                )
                lb.free_s[2 + b] = stok2
                p.wait(vec, "s", stok2)
                p.wait(vec, "r%d" % b, dres)
                p.wait(vec, "e%d" % b, ef)
                vtok2 = p.inc(vec.tensor_tensor(outs, tmps, x2s, ALU.add), "v")
                p.wait(syn, "v", vtok2)
                etok = p.inc(syn.dma_start(yseq[it * 128 : (it + 1) * 128, :], outs), "e%d" % b, 16)
                ofree[b] = (vtok2, stok2, etok)
            p.barrier()

        p.wait(syn, "e0")
        p.wait(syn, "e1")
        p.wait(syn, "d")



def emit_l0_ffn(nc, ctx, p, x, y):
    k0 = dram_cached(nc, "k0", [2 * DM, 4096], U8, kind="ExternalInput").ap()
    k1 = dram_cached(nc, "k1", [4096, 4096], U8, kind="ExternalInput").ap()
    k2 = dram_cached(nc, "k2", [4096, DM], U8, kind="ExternalInput").ap()
    s0 = dram_cached(nc, "s0", [1, 1], F32, kind="ExternalInput").ap()
    s1 = dram_cached(nc, "s1", [1, 1], F32, kind="ExternalInput").ap()
    s2 = dram_cached(nc, "s2", [1, 1], F32, kind="ExternalInput").ap()
    ones = dram_cached(nc, "ones", [1, 128], F32, kind="ExternalInput").ap()
    w0d = dram_cached(nc, "w0d", [DM, 4096], BF16, kind="Internal").ap()
    w1d = dram_cached(nc, "w1d", [4096, 4096], BF16, kind="Internal").ap()
    w2d = dram_cached(nc, "w2d", [4096, DM], BF16, kind="Internal").ap()
    xnd = dram_cached(nc, "xnd_b", [TOK, DM], BF16, kind="Internal").ap()

    with ExitStack() as ctx:
        E = ctx.enter_context
        ten, vec, sca, gps, syn = p.ten, p.vec, p.sca, p.gps, p.syn
        lb = LNBufs(nc, ctx, 4, "b_")
        lb.init(p)

        ones_sb = E(nc.sbuf_tensor("b_ones_sb", [1, 128], F32))
        svec_sb = E(nc.sbuf_tensor("b_svec_sb", [1, 4], F32))
        sc_sb = E(nc.sbuf_tensor("b_sc_sb", [128, 4], F32))
        xst_sb = E(nc.sbuf_tensor("b_xst_sb", [128, 2 * DM], F32))
        xnst_sb = E(nc.sbuf_tensor("b_xnst_sb", [128, 2 * DM], BF16))
        x2st_sb = E(nc.sbuf_tensor("b_x2st_sb", [128, 2 * DM], F32))
        tmp_sb = E(nc.sbuf_tensor("b_tmp_sb", [128, 2 * DM], F32))
        out_sb = E(nc.sbuf_tensor("b_out_sb", [128, 2 * DM], F32))
        y_sb = E(nc.sbuf_tensor("b_y_sb", [128, 2 * DM], F32))
        xnT_sb = E(nc.sbuf_tensor("b_xnT_sb", [128, 2 * NE * TC], BF16))
        h1_sb = E(nc.sbuf_tensor("b_h1_sb", [128, NF1 * TC], BF16))
        h2_sb = E(nc.sbuf_tensor("b_h2_sb", [128, NF2 * TC], BF16))
        cvt_sb = h1_sb
        cvo_sb = h2_sb
        col_sb = E(nc.sbuf_tensor("b_col_sb", [128, 4 * NF1 * 128], BF16))
        w2t_sb = E(nc.sbuf_tensor("b_w2t_sb", [128, 4 * DM], BF16))

        ps = [E(nc.psum_tensor(f"b_ps{i}", [128, 512], F32)) for i in range(8)]

        p.inc(syn.dma_start(ones_sb[:], ones), "d", 16)
        p.inc(syn.dma_start(svec_sb[:, 0:1], s0), "d", 16)
        p.inc(syn.dma_start(svec_sb[:, 1:2], s1), "d", 16)
        p.inc(syn.dma_start(svec_sb[:, 2:3], s2), "d", 16)
        p.wait(ten, "d")
        last = ten.matmul(ps[0][:, 0:3], ones_sb[:], svec_sb[:, 0:3], start=True, stop=True)
        p.inc(last, "t")
        p.wait(sca, "t")
        p.inc(sca.activation(sc_sb[:, 0:1], ps[0][:, 0:1], AF.Identity, scale=4.0), "s")
        p.inc(sca.activation(sc_sb[:, 1:2], ps[0][:, 1:2], AF.Identity, scale=2.0), "s")
        p.inc(sca.activation(sc_sb[:, 2:3], ps[0][:, 2:3], AF.Identity), "s")

        cvt8 = cvt_sb.bitcast(U8)
        conv_jobs = []
        for et in range(NE):
            conv_jobs.append(("w0", et))
        for r in range(NF1):
            conv_jobs.append(("k1", r))
        for r in range(NF2):
            conv_jobs.append(("k2", r))

        slot_tt = [("v", 0)] * 4
        slot_st = [0] * 4
        st_sem = ["k0", "k1", "k2", "k3"]
        ld_sem = ["d0", "d1", "r0", "r1"]
        milestones = {}

        def conv_round(ridx, kind, arg):
            sl = ridx % 4
            eng = vec if ridx % 2 == 0 else gps
            ename = "v" if ridx % 2 == 0 else "gc"
            ci = cvt8[:, sl * 8192 : sl * 8192 + 4096]
            co = cvo_sb[:, sl * 4096 : sl * 4096 + 4096]
            nm, tok = slot_tt[sl]
            p.wait(syn, nm, tok)
            p.wait(syn, ld_sem[sl])
            if kind == "w0":
                et = arg
                t0 = cvt8[:, sl * 8192 : sl * 8192 + 4096]
                t1 = cvt8[:, sl * 8192 + 4096 : sl * 8192 + 8192]
                p.inc(syn.dma_start(t0, k0[et * 128 : et * 128 + 128, :]), ld_sem[sl], 16)
                ltok = p.inc(
                    syn.dma_start(t1, k0[DM + et * 128 : DM + et * 128 + 128, :]),
                    ld_sem[sl], 16,
                )
            else:
                kk, r = (k1, arg) if kind == "k1" else (k2, arg)
                cols = 4096 if kind == "k1" else DM
                nblk = 4096 // cols
                if nblk == 1:
                    ap_in = kk[r * 128 : (r + 1) * 128, :]
                else:
                    ap_in = custom_ap(kk, r * nblk * 128 * cols,
                                      [[cols, 128], [128 * cols, nblk], [1, cols]])
                ltok = p.inc(syn.dma_start(ci, ap_in), ld_sem[sl], 16)
            p.wait(eng, ld_sem[sl], ltok)
            p.wait(eng, st_sem[sl], slot_st[sl])
            if kind == "w0":
                t0 = cvt8[:, sl * 8192 : sl * 8192 + 4096]
                t1 = cvt8[:, sl * 8192 + 4096 : sl * 8192 + 8192]
                vtok = p.inc(eng.tensor_tensor(co, t0, t1, ALU.subtract), ename)
            else:
                vtok = p.inc(
                    eng.tensor_scalar(co, ci, 2.0, -1.0, ALU.mult, ALU.add), ename
                )
            slot_tt[sl] = (ename, vtok)
            p.wait(sca, ename, vtok)
            p.wait(sca, st_sem[sl])
            if kind == "w0":
                dst = w0d[arg * 128 : (arg + 1) * 128, :]
            elif kind == "k1":
                dst = w1d[arg * 128 : (arg + 1) * 128, :]
            else:
                r = arg
                nblk = 4
                dst = custom_ap(w2d, r * nblk * 128 * DM,
                                [[DM, 128], [128 * DM, nblk], [1, DM]])
            slot_st[sl] = p.inc(sca.dma_start(dst, co), st_sem[sl], 16)

        conv_jobs = []
        for et in range(NE):
            conv_jobs.append(("w0", et))
        for r in range(NF1):
            conv_jobs.append(("k1", r))
        for r in range(NF2 // 4):
            conv_jobs.append(("k2", r))

        stage_free = [(0, 0, 0)] * 2
        era_store_tok = {}

        def emit_era_tile(i):
            b = i % 2
            vf, sf, ef = stage_free[b]
            p.wait(syn, "v", vf)
            p.wait(syn, "s", sf)
            p.wait(syn, "e%d" % b, ef)
            xs = xst_sb[:, b * DM : (b + 1) * DM]
            xns = xnst_sb[:, b * DM : (b + 1) * DM]
            row = i * 128
            p.wait(syn, "d%d" % b)
            dtok = p.inc(syn.dma_start(xs, x[row : row + 128, :]), "d%d" % b, 16)
            p.wait(vec, "d%d" % b, dtok)
            vtok, stok, rstd, negmur = emit_ln_stats(p, lb, b, xs, DM)
            p.wait(vec, "v")
            p.wait(vec, "e%d" % b, ef)
            aptok = p.inc(
                vec.tensor_scalar(xns, xs, rstd, negmur, ALU.mult, ALU.add), "v"
            )
            lb.free_v[b] = aptok
            p.wait(syn, "v", aptok)
            p.wait(syn, "e%d" % b)
            etok = p.inc(syn.dma_start(xnd[row : row + 128, :], xns), "e%d" % b, 16)
            stage_free[b] = (aptok, stok, etok)
            ch = i // (TC // 128)
            era_store_tok[ch] = (p.val["e0"], p.val["e1"])


        ps_rd = [("s", 0)] * 8
        t_z0_prev = [0, 0]
        t_z1_prev = 0
        t_z2_prev = 0
        xtok = [0, 0]
        xsem = ["x", "xs"]
        colfree = [0, 0, 0, 0]
        ctok = [0, 0, 0, 0]
        w2free = [0, 0, 0, 0]
        utok = [0, 0, 0, 0]
        col_q = [syn, gps, syn, gps]
        col_sem = ["w0", "w1", "w2", "w3"]
        w2_q = [syn, gps, syn, gps]
        w2_sem = ["u0", "u1", "u2", "u3"]
        issued = [0]

        NSTR = 8 + NF2

        def stream_src(gidx):
            k = gidx % NSTR
            if k < 8:
                return custom_ap(w0d, k * 512, [[4096, 128], [4096 * 128, 8], [1, 512]]), NE * 512
            k -= 8
            return custom_ap(w1d, k * 128, [[4096, 128], [4096 * 128, 32], [1, 128]]), NF1 * 128

        def ensure_issued(upto):
            while issued[0] <= min(upto, NCH * NSTR - 1):
                gidx = issued[0]
                sl = gidx % 4
                q = col_q[sl]
                srcap, n = stream_src(gidx)
                p.wait(q, "t", colfree[sl])
                p.wait(q, col_sem[sl])
                if gidx < NSTR:
                    ms = milestones["w0_st" if gidx % NSTR < 8 else "k1_st"]
                    for snm, stok_ in zip(st_sem, ms):
                        p.wait(q, snm, stok_)
                ctok[sl] = p.inc(
                    q.dma_start(col_sb[:, sl * NF1 * 128 : sl * NF1 * 128 + n], srcap),
                    col_sem[sl],
                    16,
                )
                issued[0] += 1

        def issue_transposes(c):
            bx = c % 2
            tok0 = c * TC
            p.wait(sca, "t", t_z0_prev[bx])
            e0t, e1t = era_store_tok[c]
            p.wait(sca, "e0", e0t)
            p.wait(sca, "e1", e1t)
            for et in range(NE):
                p.inc(
                    sca.dma_start(
                        xnT_sb[:, bx * NE * TC + et * TC : bx * NE * TC + (et + 1) * TC],
                        xnd[tok0 : tok0 + TC, et * 128 : (et + 1) * 128],
                        transpose=True,
                    ),
                    xsem[bx],
                    16,
                )
            xtok[bx] = p.val[xsem[bx]]

        def issue_w2(gidx):
            sl = gidx % 4
            idx = gidx % NF2
            q = w2_q[sl]
            p.wait(q, "t", w2free[sl])
            if gidx < NF2:
                for snm, stok_ in zip(st_sem, milestones["all_st"]):
                    p.wait(q, snm, stok_)
            utok[sl] = p.inc(
                q.dma_start(w2t_sb[:, sl * DM : (sl + 1) * DM], w2d[idx * 128 : (idx + 1) * 128, :]),
                w2_sem[sl],
                16,
            )

        era_next = [0]

        def emit_some_era(n):
            while n > 0 and era_next[0] < TOK // 128:
                emit_era_tile(era_next[0])
                era_next[0] += 1
                n -= 1

        for ridx, (kind, arg) in enumerate(conv_jobs):
            conv_round(ridx, kind, arg)
            if ridx % 6 == 5:
                emit_some_era(1)
            if kind == "w0" and arg == NE - 1:
                milestones["w0_st"] = list(slot_st)
            if kind == "k1" and arg == NF1 - 1:
                milestones["k1_st"] = list(slot_st)
        milestones["all_st"] = list(slot_st)
        milestones["tt_v"] = p.val["v"]
        milestones["tt_gc"] = p.val["gc"]
        p.wait(syn, "d0")
        p.wait(syn, "d1")
        p.wait(syn, "r0")
        p.wait(syn, "r1")
        emit_some_era(max(0, 8 - era_next[0]))
        issue_transposes(0)
        ensure_issued(1)
        for c in range(NCH):
            bx = c % 2
            tok0 = c * TC
            if c + 1 < NCH:
                issue_transposes(c + 1)

            xbase = bx * NE * TC
            for f1t in range(NF1):
                gidx = c * NSTR + f1t // 4
                sl = gidx % 4
                loc = f1t % 4
                if f1t % 4 == 0:
                    p.pump(1)
                ensure_issued(gidx + 3)
                bank = f1t % 3
                pz = ps[bank]
                nm, tok = ps_rd[bank]
                p.wait(ten, nm, tok)
                if f1t == 0:
                    p.wait(ten, xsem[bx], xtok[bx])
                if loc == 0:
                    p.wait(ten, col_sem[sl], ctok[sl])
                last = None
                for et in range(NE):
                    last = ten.matmul(
                        pz[:],
                        col_sb[:, sl * NF1 * 128 + et * 512 + loc * 128 : sl * NF1 * 128 + et * 512 + loc * 128 + 128],
                        xnT_sb[:, xbase + et * TC : xbase + (et + 1) * TC],
                        start=(et == 0),
                        stop=(et == NE - 1),
                    )
                ttok = p.inc(last, "t")
                if loc == 3:
                    colfree[sl] = ttok
                if f1t == NF1 - 1:
                    t_z0_prev[bx] = ttok
                p.wait(sca, "t", ttok)
                if f1t == 0:
                    p.wait(sca, "t", t_z1_prev)
                if c == 0 and f1t == 0:
                    p.wait(sca, "v", milestones["tt_v"])
                    p.wait(sca, "gc", milestones["tt_gc"])
                ps_rd[bank] = (
                    "s",
                    p.inc(
                        sca.activation(
                            h1_sb[:, f1t * TC : (f1t + 1) * TC], pz[:], AF.Relu,
                            scale=sc_sb[:, 0:1],
                        ),
                        "s",
                    ),
                )
            s_h1_done = p.val["s"]
            emit_some_era(TC // 128)

            p.wait(ten, "s", s_h1_done)
            for f2t in range(NF2):
                gidx = c * NSTR + 8 + f2t
                sl = gidx % 4
                if f2t % 4 == 0:
                    p.pump(1)
                ensure_issued(gidx + 3)
                bank = 3 + f2t % 2
                pz = ps[bank]
                nm, tok = ps_rd[bank]
                p.wait(ten, nm, tok)
                p.wait(ten, col_sem[sl], ctok[sl])
                last = None
                for f1t in range(NF1):
                    last = ten.matmul(
                        pz[:],
                        col_sb[:, sl * NF1 * 128 + f1t * 128 : sl * NF1 * 128 + (f1t + 1) * 128],
                        h1_sb[:, f1t * TC : (f1t + 1) * TC],
                        start=(f1t == 0),
                        stop=(f1t == NF1 - 1),
                    )
                ttok = p.inc(last, "t")
                colfree[sl] = ttok
                if f2t == NF2 - 1:
                    t_z1_prev = ttok
                p.wait(sca, "t", ttok)
                if f2t == 0:
                    p.wait(sca, "t", t_z2_prev)
                if c == 0 and f2t == 0:
                    for snm, stok_ in zip(st_sem, milestones["all_st"]):
                        p.wait(sca, snm, stok_)
                ps_rd[bank] = (
                    "s",
                    p.inc(
                        sca.activation(
                            h2_sb[:, f2t * TC : (f2t + 1) * TC], pz[:], AF.Relu,
                            scale=sc_sb[:, 1:2],
                        ),
                        "s",
                    ),
                )
            s_h2_done = p.val["s"]

            if c == 0:
                for k in range(3):
                    issue_w2(k)
            p.wait(ten, "s", s_h2_done)
            gz_base = c * NF2
            for bank in range(8):
                nm, tok = ps_rd[bank]
                p.wait(ten, nm, tok)
            for f2t in range(NF2):
                g = gz_base + f2t
                sl = g % 4
                if g + 3 < NCH * NF2:
                    issue_w2(g + 3)
                p.wait(ten, w2_sem[sl], utok[sl])
                last = None
                for tt in range(TC // 128):
                    for dh in range(2):
                        last = ten.matmul(
                            ps[tt * 2 + dh][:],
                            h2_sb[:, f2t * TC + tt * 128 : f2t * TC + tt * 128 + 128],
                            w2t_sb[:, sl * DM + dh * 512 : sl * DM + dh * 512 + 512],
                            start=(f2t == 0),
                            stop=(f2t == NF2 - 1),
                        )
                ttok = p.inc(last, "t")
                w2free[sl] = ttok
                if f2t == NF2 - 1:
                    t_z2_prev = ttok
            tz2 = p.val["t"]

            evac_free = [(0, 0)] * 2
            if c > 0:
                evac_free = prev_evac_free
            for tt in range(TC // 128):
                b = tt % 2
                row = tok0 + tt * 128
                vf, sf = evac_free[b]
                p.wait(syn, "v", vf)
                x2s = x2st_sb[:, b * DM : (b + 1) * DM]
                dres = p.inc(syn.dma_start(x2s, x[row : row + 128, :]), "r%d" % b, 16)
                ys = y_sb[:, b * DM : (b + 1) * DM]
                p.wait(sca, "t", tz2)
                p.wait(sca, "v", vf)
                p.wait(sca, "s", sf)
                p.inc(sca.activation(ys[:, 0:512], ps[tt * 2][:], AF.Identity, scale=sc_sb[:, 2:3]), "s")
                p.wait(sca, "s")
                stok0 = p.inc(
                    sca.activation(ys[:, 512:1024], ps[tt * 2 + 1][:], AF.Identity, scale=sc_sb[:, 2:3]),
                    "s",
                )
                ps_rd[tt * 2] = ("s", stok0)
                ps_rd[tt * 2 + 1] = ("s", stok0)
                p.wait(vec, "s", stok0)
                vtok, stok, rstd, negmur = emit_ln_stats(p, lb, 2 + b, ys, DM)
                tmps = tmp_sb[:, b * DM : (b + 1) * DM]
                outs = out_sb[:, b * DM : (b + 1) * DM]
                p.wait(sca, "v", vtok)
                p.wait(sca, "s", stok)
                stok2 = p.inc(
                    sca.activation(tmps, ys, AF.Identity, bias=negmur, scale=rstd), "s"
                )
                lb.free_s[2 + b] = stok2
                prev_ys_rd = (vtok, stok2)
                p.d_prev_ys_rd = prev_ys_rd
                p.wait(vec, "s", stok2)
                p.wait(vec, "r%d" % b, dres)
                p.wait(vec, "e%d" % b)
                vtok2 = p.inc(vec.tensor_tensor(outs, tmps, x2s, ALU.add), "v")
                p.wait(syn, "v", vtok2)
                p.inc(syn.dma_start(y[row : row + 128, :], outs), "e%d" % b, 16)
                evac_free[b] = (vtok2, stok2)
            prev_evac_free = evac_free

        p.wait(syn, "e0")
        p.wait(syn, "e1")
        p.wait(syn, "d")

def emit_l1_attn(nc, ctx, p, x, y):
    W = dram_cached(nc, "W", [32, DM], F32, kind="ExternalInput").ap()
    bb = dram_cached(nc, "bb", [1, 32], F32, kind="ExternalInput").ap()
    ones = dram_cached(nc, "ones", [1, 128], F32, kind="ExternalInput").ap()
    idb = dram_cached(nc, "idb", [128, 128], BF16, kind="ExternalInput").ap()
    wstage = dram_cached(nc, "wstage_c", [32, DM], BF16, kind="Internal").ap()
    xnd = dram_cached(nc, "xnd_c", [SEQ_PER_CORE, SEQ, DM], BF16, kind="Internal").ap()
    WPROWS = SEQ + 2
    wpad_n = ((WPROWS * RPAD + 127) // 128) * 128
    wpad = dram_cached(nc, "wpad", [SEQ_PER_CORE, wpad_n], BF16, kind="Internal").ap()

    with ExitStack() as ctx:
        E = ctx.enter_context
        ten, vec, sca, gps, syn = p.ten, p.vec, p.sca, p.gps, p.syn
        lb = LNBufs(nc, ctx, 4, "c_")
        lb.init(p)

        ones_sb = E(nc.sbuf_tensor("c_" + "ones_sb", [1, 128], F32))
        idb_sb = E(nc.sbuf_tensor("c_" + "idb_sb", [128, 128], BF16))
        brow_sb = E(nc.sbuf_tensor("c_" + "brow_sb", [1, 32], F32))
        wT_sb = E(nc.sbuf_tensor("c_" + "wT_sb", [128, NE * 32], BF16))
        sm_sb = E(nc.sbuf_tensor("c_" + "sm_sb", [128, 8], F32))
        w_sb = E(nc.sbuf_tensor("c_" + "w_sb", [128, NTS * 32], F32))
        zer_sb = E(nc.sbuf_tensor("c_" + "zer_sb", [128, 2576], BF16))
        xst_sb = E(nc.sbuf_tensor("c_" + "xst_sb", [128, 2 * DM], F32))
        x2st_sb = E(nc.sbuf_tensor("c_" + "x2st_sb", [128, 2 * DM], F32))
        tmp_sb = E(nc.sbuf_tensor("c_" + "tmp_sb", [128, 2 * DM], F32))
        out_sb = E(nc.sbuf_tensor("c_" + "out_sb", [128, 2 * DM], F32))
        xn_sb = E(nc.sbuf_tensor("c_" + "xn_sb", [128, 2 * NTS * DM], BF16))
        xnT_sb = E(nc.sbuf_tensor("c_" + "xnT_sb", [128, NE * SEQ], BF16))
        PT_sb = E(nc.sbuf_tensor("c_" + "PT_sb", [128, 4 * 256], BF16))
        P_sb = E(nc.sbuf_tensor("c_" + "P_sb", [128, 2 * 256], BF16))

        ps_l = E(nc.psum_tensor("c_" + "ps_l", [128, 512], F32))
        psD = [E(nc.psum_tensor("c_" + f"psD{i}", [128, 1024], BF16)) for i in range(2)]
        ps_o = [E(nc.psum_tensor("c_" + f"ps_oc{i}", [128, DM], F32)) for i in range(2)]


        p.inc(syn.dma_start(ones_sb[:], ones), "d", 16)
        p.inc(syn.dma_start(idb_sb[:], idb), "d", 16)
        p.inc(syn.dma_start(brow_sb[:], bb), "d", 16)
        p.inc(gps.dma_start(wstage, W), "g", 16)
        p.wait(syn, "g")
        for et in range(NE):
            p.inc(
                syn.dma_start(
                    wT_sb[:, et * 32 : (et + 1) * 32],
                    wstage[0:32, et * 128 : (et + 1) * 128],
                    transpose=True,
                ),
                "x",
                16,
            )
        p.inc(vec.memset(zer_sb[:], 0.0), "v")
        p.inc(vec.memset(PT_sb[:], 0.0), "v")
        p.wait(syn, "v")
        for s in range(SEQ_PER_CORE):
            p.inc(
                syn.dma_start(
                    wpad[s].rearrange("(a b) -> a b", a=128), zer_sb[:, 0 : wpad_n // 128]
                ),
                "d",
                16,
            )
        p.barrier()

        c1_stage = [(0, 0) for _ in range(2)]
        c1_tok = {}
        c1_half = {}

        def emit_c1_tile(s_, i_):
            b = i_ % 2
            vf, sf = c1_stage[b]
            p.wait(syn, "v", vf)
            p.wait(syn, "s", sf)
            xs = xst_sb[:, b * DM : (b + 1) * DM]
            p.wait(syn, "d%d" % b)
            dtok = p.inc(
                syn.dma_start(xs, x[s_ * SEQ + i_ * 128 : s_ * SEQ + (i_ + 1) * 128, :]),
                "d%d" % b, 16,
            )
            p.wait(vec, "d%d" % b, dtok)
            vtok, stok, rstd, negmur = emit_ln_stats(p, lb, b, xs, DM)
            xns = xn_sb[:, (s_ * NTS + i_) * DM : (s_ * NTS + i_ + 1) * DM]
            p.wait(vec, "v")
            aptok = p.inc(
                vec.tensor_scalar(xns, xs, rstd, negmur, ALU.mult, ALU.add), "v"
            )
            lb.free_v[b] = aptok
            p.wait(syn, "v", aptok)
            csem = "xs" if i_ < 8 else "u2"
            p.inc(syn.dma_start(xnd[s_, i_ * 128 : (i_ + 1) * 128, :], xns), csem, 16)
            c1_stage[b] = (aptok, stok)
            if i_ == 7 or i_ == 15:
                c1_half[(s_, i_ // 8)] = p.val[csem]

        for i_ in range(NTS):
            emit_c1_tile(0, i_)

        for s in range(SEQ_PER_CORE):
            xseq = x[s * SEQ : (s + 1) * SEQ, :]
            yseq = y[s * SEQ : (s + 1) * SEQ, :]

            xhalfc = [0, 0]
            hsemc = ["x", "u0"]
            for half in range(2):
                p.wait(syn, ["xs", "u2"][half], c1_half[(s, half)])
                p.wait(syn, hsemc[half])
                for et in range(NE):
                    p.inc(
                        syn.dma_start(
                            xnT_sb[:, et * SEQ + half * 1024 : et * SEQ + half * 1024 + 1024],
                            xnd[s, half * 1024 : (half + 1) * 1024, et * 128 : (et + 1) * 128],
                            transpose=True,
                        ),
                        hsemc[half],
                        16,
                    )
                xhalfc[half] = p.val[hsemc[half]]

            prev_exp_s = 0
            psl_free = 0
            for tt in range(NTS):
                p.wait(ten, "s", psl_free)
                p.wait(ten, hsemc[tt // 8], xhalfc[tt // 8])
                last = None
                for et in range(NE):
                    last = ten.matmul(
                        ps_l[:, 0:32],
                        xnT_sb[:, et * SEQ + tt * 128 : et * SEQ + tt * 128 + 128],
                        wT_sb[:, et * 32 : (et + 1) * 32],
                        start=(et == 0),
                        stop=False,
                    )
                last = ten.matmul(
                    ps_l[:, 0:32], ones_sb[:], brow_sb[:], start=False, stop=True
                )
                ttok = p.inc(last, "t")
                p.wait(vec, "t", ttok)
                p.wait(vec, "s", prev_exp_s)
                p.inc(vec.reduce_max(sm_sb[:, 0:1], ps_l[:, 0:32], axis=AX.X, negate=True), "v")
                p.wait(sca, "v")
                prev_exp_s = p.inc(
                    sca.activation(
                        w_sb[:, tt * 32 : (tt + 1) * 32],
                        ps_l[:, 0:32],
                        AF.Exp,
                        bias=sm_sb[:, 0:1],
                        accum_out=sm_sb[:, 1:2],
                    ),
                    "s",
                )
                psl_free = prev_exp_s
                p.wait(vec, "s", prev_exp_s)
                p.inc(vec.reciprocal(sm_sb[:, 2:3], sm_sb[:, 1:2]), "v")
                p.wait(vec, "v")
                wtok = p.inc(
                    vec.tensor_scalar(
                        w_sb[:, tt * 32 : (tt + 1) * 32],
                        w_sb[:, tt * 32 : (tt + 1) * 32],
                        sm_sb[:, 2:3],
                        None,
                        ALU.mult,
                    ),
                    "v",
                )
                p.wait(gps, "v", wtok)
                dst = custom_ap(
                    wpad[s], (1 + tt * 128) * RPAD, [[RPAD, 128], [1, 32]]
                )
                p.inc(gps.dma_start(dst, w_sb[:, tt * 32 : (tt + 1) * 32]), "wp", 16)
            p.barrier()

            ofree = [(0, 0) for _ in range(2)]
            ptfree = [0, 0, 0, 0]
            pfree = [0, 0]
            psd_free = [0, 0]
            ktok = [0, 0, 0, 0]

            def issue_skew(lt):
                sl = lt % 4
                p.wait(syn, "t", ptfree[sl])
                srcap = custom_ap(
                    wpad[s], (1 + lt * 128) * RPAD, [[RPAD - 1, 128], [1, 159]]
                )
                ktok[sl] = p.inc(
                    syn.dma_start(PT_sb[:, sl * 256 : sl * 256 + 159], srcap), "k%d" % sl, 16
                )

            issue_skew(0)
            issue_skew(1)
            for lt in range(NTS):
                if s == 0:
                    emit_c1_tile(1, lt)
                p.pump(1)
                b = lt % 2
                sl = lt % 4
                if lt + 2 < NTS:
                    issue_skew(lt + 2)
                p.wait(ten, "k%d" % sl, ktok[sl])
                p.wait(ten, "s", psd_free[b])
                ten.transpose(
                    psD[b][:, 0:128], PT_sb[:, sl * 256 : sl * 256 + 128], idb_sb[:]
                )
                ttr = p.inc(
                    ten.transpose(
                        psD[b][:, 128:256], PT_sb[:, sl * 256 + 128 : sl * 256 + 256], idb_sb[:]
                    ),
                    "t",
                )
                ptfree[sl] = ttr
                p.wait(vec, "t", ttr)
                p.wait(vec, "t", pfree[b])
                p.wait(sca, "t", ttr)
                p.wait(sca, "t", pfree[b])
                ptok = p.inc(
                    sca.activation(P_sb[:, b * 256 : (b + 1) * 256], psD[b][:, 0:256], AF.Identity), "s"
                )
                psd_free[b] = ptok
                x2s = x2st_sb[:, b * DM : (b + 1) * DM]
                p.wait(syn, "gc", ofree[b][0])
                dres = p.inc(
                    syn.dma_start(x2s, xseq[lt * 128 : (lt + 1) * 128, :]), "r%d" % b, 16
                )
                vf, sf = ofree[b]
                p.wait(ten, "s", max(ptok, sf))
                p.wait(ten, "gc", vf)
                pso = ps_o[b]
                tmix = None
                for eb in range(2):
                    nsup = 2 if lt + 1 < NTS else 1
                    for kk in range(nsup):
                        lhs = P_sb[:, b * 256 + kk * 128 : b * 256 + kk * 128 + 128]
                        rhs = xn_sb[:, (s * NTS + lt + kk) * DM + eb * 512 : (s * NTS + lt + kk) * DM + eb * 512 + 512]
                        tmix = ten.matmul(
                            pso[:, eb * 512 : eb * 512 + 512], lhs, rhs,
                            start=(kk == 0), stop=(kk == nsup - 1),
                        )
                tmixtok = p.inc(tmix, "t")
                pfree[b] = tmixtok
                p.wait(vec, "t", tmixtok)
                vtok, stok, rstd, negmur = emit_ln_stats(p, lb, 2 + b, pso, DM)
                tmps = tmp_sb[:, b * DM : (b + 1) * DM]
                outs = out_sb[:, b * DM : (b + 1) * DM]
                p.wait(sca, "v", vtok)
                p.wait(sca, "s", stok)
                stok2 = p.inc(
                    sca.activation(tmps, pso[:], AF.Identity, bias=negmur, scale=rstd), "s"
                )
                lb.free_s[2 + b] = stok2
                p.wait(gps, "s", stok2)
                p.wait(gps, "r%d" % b, dres)
                p.wait(gps, "e%d" % b)
                vtok2 = p.inc(gps.tensor_tensor(outs, tmps, x2s, ALU.add), "gc")
                p.wait(syn, "gc", vtok2)
                p.inc(syn.dma_start(yseq[lt * 128 : (lt + 1) * 128, :], outs), "e%d" % b, 16)
                ofree[b] = (vtok2, stok2)
            p.barrier()

        p.wait(syn, "e0")
        p.wait(syn, "e1")



def emit_l1_ffn(nc, ctx, p, x, y):
    gw = dram_cached(nc, "gw", [NEXP, DM], F32, kind="ExternalInput").ap()
    gb = dram_cached(nc, "gb", [1, NEXP], F32, kind="ExternalInput").ap()
    rmsw = dram_cached(nc, "rmsw", [1, NEXP], F32, kind="ExternalInput").ap()
    w1 = dram_cached(nc, "w1", [NEXP, DM, 2048], F32, kind="ExternalInput").ap()
    b1 = dram_cached(nc, "b1", [NEXP * 2048], F32, kind="ExternalInput").ap()
    w2 = dram_cached(nc, "w2", [NEXP * 2048, DM], F32, kind="ExternalInput").ap()
    b2 = dram_cached(nc, "b2", [NEXP, DM], F32, kind="ExternalInput").ap()
    ones = dram_cached(nc, "ones", [1, 128], F32, kind="ExternalInput").ap()
    onescol = dram_cached(nc, "onescol", [128, 1], F32, kind="ExternalInput").ap()
    idf = dram_cached(nc, "idf", [128, 128], F32, kind="ExternalInput").ap()
    w1d = dram_cached(nc, "w1dd", [NEXP * NH * 128, DM], BF16, kind="Internal").ap()
    w2d = dram_cached(nc, "w2dd", [NEXP * 2048, DM], BF16, kind="Internal").ap()
    xnd = dram_cached(nc, "xnd_d", [TOK, DM], BF16, kind="Internal").ap()

    with ExitStack() as ctx:
        E = ctx.enter_context
        ten, vec, sca, gps, syn = p.ten, p.vec, p.sca, p.gps, p.syn
        lb = LNBufs(nc, ctx, 4, "d_")
        lb.init(p)

        ones_sb = E(nc.sbuf_tensor("d_ones_sb", [1, 128], F32))
        onescol_sb = E(nc.sbuf_tensor("d_onescol_sb", [128, 1], F32))
        idf_sb = E(nc.sbuf_tensor("d_idf_sb", [128, 128], F32))
        gw2_sb = E(nc.sbuf_tensor("d_gw2_sb", [128, NE * 16], BF16))
        gb2row_sb = E(nc.sbuf_tensor("d_gb2row_sb", [1, 16], F32))
        eps_sb = E(nc.sbuf_tensor("d_eps_sb", [128, 1], F32))
        gbrow_sb = E(nc.sbuf_tensor("d_gbrow_sb", [1, NEXP], F32))
        rmsrow_sb = E(nc.sbuf_tensor("d_rmsrow_sb", [1, NEXP], F32))
        rmsbc_sb = E(nc.sbuf_tensor("d_rmsbc_sb", [128, NEXP], F32))
        b1raw_sb = E(nc.sbuf_tensor("d_b1raw_sb", [128, 128], F32))
        b1col_sb = E(nc.sbuf_tensor("d_b1col_sb", [128, 128], F32))
        B2bc_sb = E(nc.sbuf_tensor("d_B2bc_sb", [128, DM], F32))
        gb_sb = E(nc.sbuf_tensor("d_gb_sb", [128, 2 * 16], F32))
        gb2_sb = E(nc.sbuf_tensor("d_gb2_sb", [128, 2 * 16], F32))
        ng_sb = E(nc.sbuf_tensor("d_ng_sb", [128, 32], F32))
        xst_sb = E(nc.sbuf_tensor("d_xst_sb", [128, 2 * DM], F32))
        xnst_sb = E(nc.sbuf_tensor("d_xnst_sb", [128, DM], BF16))
        x2st_sb = E(nc.sbuf_tensor("d_x2st_sb", [128, 2 * DM], F32))
        tmp_sb = E(nc.sbuf_tensor("d_tmp_sb", [128, 2 * DM], F32))
        out_sb = E(nc.sbuf_tensor("d_out_sb", [128, 2 * DM], F32))
        y_sb = E(nc.sbuf_tensor("d_y_sb", [128, DM], F32))
        xnT_sb = E(nc.sbuf_tensor("d_xnT_sb", [128, 2 * NE * TC2], BF16))
        col_sb = E(nc.sbuf_tensor("d_col_sb", [128, 4 * NE * 128], BF16))
        w2t_sb = E(nc.sbuf_tensor("d_w2t_sb", [128, 4 * DM], BF16))
        hall_sb = E(nc.sbuf_tensor("d_hall_sb", [128, NK2 * TC2], BF16))

        ps8 = [E(nc.psum_tensor(f"d_ps8_{i}", [128, 512], F32)) for i in range(8)]
        b2_sb = tmp_sb[0:8, 0:DM]
        b2row_sb = y_sb[0:1, 0:DM]
        graw_sb = xst_sb[0:8, 0:DM]


        p.inc(syn.dma_start(ones_sb[:], ones), "d", 16)
        p.inc(syn.dma_start(onescol_sb[:], onescol), "d", 16)
        p.inc(syn.dma_start(idf_sb[:], idf), "d", 16)
        p.inc(syn.dma_start(gbrow_sb[:], gb), "d", 16)
        p.inc(syn.dma_start(rmsrow_sb[:], rmsw), "d", 16)
        p.inc(syn.dma_start(b1raw_sb[:], b1.rearrange("(a b) -> a b", a=128)), "d", 16)
        p.inc(syn.dma_start(b2_sb, b2), "d", 16)
        p.inc(syn.dma_start(graw_sb, gw), "d", 16)
        p.pump(10**9)
        p.wait(ten, "d")
        p.inc(ten.matmul(ps8[3][:, 0:8], ones_sb[:], rmsrow_sb[:], start=True, stop=True), "t")
        p.wait(sca, "t")
        p.inc(sca.activation(rmsbc_sb[:], ps8[3][:, 0:8], AF.Identity), "s")
        p.inc(vec.memset(eps_sb[:], EPS32), "v")
        p.wait(vec, "d")
        p.inc(vec.tensor_copy(gb2row_sb[0:1, 0:8], gbrow_sb[0:1, 0:8]), "v")
        p.inc(vec.tensor_mul(gb2row_sb[0:1, 8:16], gbrow_sb[0:1, 0:8], rmsrow_sb[0:1, 0:8]), "v")
        for et in range(NE):
            p.inc(
                ten.transpose(
                    ps8[4][:, 0:8], graw_sb[0:8, et * 128 : (et + 1) * 128], idf_sb[0:8, 0:8]
                ),
                "t",
            )
            p.wait(sca, "t")
            p.wait(sca, "s")
            stok_g = p.inc(
                sca.activation(gw2_sb[:, et * 16 : et * 16 + 8], ps8[4][:, 0:8], AF.Identity),
                "s",
            )
            p.wait(vec, "s", stok_g)
            p.inc(
                vec.tensor_tensor(
                    gw2_sb[:, et * 16 + 8 : et * 16 + 16],
                    gw2_sb[:, et * 16 : et * 16 + 8],
                    rmsbc_sb[:],
                    ALU.mult,
                ),
                "v",
            )
            p.wait(ten, "s", stok_g)
        p.inc(ten.transpose(ps8[0][:, 0:128], b1raw_sb[:], idf_sb[:]), "t")
        p.wait(sca, "t")
        p.inc(sca.activation(b1col_sb[:], ps8[0][:, 0:128], AF.Identity), "s")
        p.wait(ten, "s")
        for dh in range(2):
            p.inc(
                ten.matmul(ps8[1][0:1, 0:512], onescol_sb[0:8, :], b2_sb[0:8, dh * 512 : dh * 512 + 512], start=True, stop=True),
                "t",
            )
            p.wait(sca, "t")
            p.inc(sca.activation(B2bc_sb[0:1, dh * 512 : dh * 512 + 512], ps8[1][0:1, 0:512], AF.Identity), "s")
            p.wait(ten, "s")
        p.barrier()

        stage_free = [(0, 0, 0)] * 2
        era_store_tok = {}
        era_next = [0]

        def emit_era_tile(i):
            b = i % 2
            vf, sf, ef = stage_free[b]
            p.wait(syn, "v", vf)
            p.wait(syn, "s", sf)
            p.wait(syn, "e%d" % b, ef)
            xs = xst_sb[:, b * DM : (b + 1) * DM]
            xns = xnst_sb[:, 0:DM]
            row = i * 128
            p.wait(syn, "d%d" % b)
            dtok = p.inc(syn.dma_start(xs, x[row : row + 128, :]), "d%d" % b, 16)
            p.wait(vec, "d%d" % b, dtok)
            vtok, stok, rstd, negmur = emit_ln_stats(p, lb, b, xs, DM)
            p.wait(vec, "v")
            p.wait(vec, "e0")
            p.wait(vec, "e1")
            aptok = p.inc(
                vec.tensor_scalar(xns, xs, rstd, negmur, ALU.mult, ALU.add), "v"
            )
            lb.free_v[b] = aptok
            p.wait(syn, "v", aptok)
            p.wait(syn, "e%d" % b)
            etok = p.inc(syn.dma_start(xnd[row : row + 128, :], xns), "e%d" % b, 16)
            stage_free[b] = (aptok, stok, etok)
            era_store_tok[i // (TC2 // 128)] = (p.val["e0"], p.val["e1"])

        def emit_some_era(n):
            while n > 0 and era_next[0] < TOK // 128:
                emit_era_tile(era_next[0])
                era_next[0] += 1
                n -= 1

        emit_some_era(2 * (TC2 // 128))

        ps_rd = [("s", 0)] * 8
        t_xnT_prev = [0, 0]
        t_mm2_prev = 0
        xtok = [0, 0]
        xsem = ["x", "xs"]
        colfree = [0] * 4
        ctok = [0] * 4
        w2free = [0] * 4
        utok = [0] * 4
        col_q = [syn, gps, syn, gps]
        col_sem = ["c0", "c1", "c2", "c3"]
        w2_q = [syn, gps, syn, gps]
        w2_sem = ["u0", "u1", "u2", "u3"]

        def issue_transposes(c):
            bx = c % 2
            tok0 = c * TC2
            p.wait(syn, "t", t_xnT_prev[bx])
            e0t, e1t = era_store_tok[c]
            p.wait(syn, "e0", e0t)
            p.wait(syn, "e1", e1t)
            for et in range(NE):
                p.inc(
                    syn.dma_start(
                        xnT_sb[:, bx * NE * TC2 + et * TC2 : bx * NE * TC2 + (et + 1) * TC2],
                        xnd[tok0 : tok0 + TC2, et * 128 : (et + 1) * 128],
                        transpose=True,
                    ),
                    xsem[bx],
                    16,
                )
            xtok[bx] = p.val[xsem[bx]]

        def issue_col(gidx):
            sl = gidx % 4
            kk = gidx % NK2
            q = col_q[sl]
            p.wait(q, "t", colfree[sl])
            ctok[sl] = p.inc(
                q.dma_start(
                    col_sb[:, sl * 1024 : (sl + 1) * 1024],
                    w1d[kk * 128 : (kk + 1) * 128, :],
                ),
                col_sem[sl],
                16,
            )

        def issue_w2(gidx):
            sl = gidx % 4
            kk = gidx % NK2
            q = w2_q[sl]
            p.wait(q, "t", w2free[sl])
            utok[sl] = p.inc(
                q.dma_start(w2t_sb[:, sl * DM : (sl + 1) * DM], w2d[kk * 128 : (kk + 1) * 128, :]),
                w2_sem[sl],
                16,
            )

        p.wait(syn, "dc")
        p.wait(gps, "dc")
        p.wait(sca, "dc")
        gate_slot_free = [(0, 0)] * 2
        issue_transposes(0)
        for k in range(3):
            issue_col(k)
        for k in range(3):
            issue_w2(k)
        prev_evac_free = [(0, 0)] * 2
        for c in range(NCH2):
            bx = c % 2
            xbase = bx * NE * TC2
            tok0 = c * TC2
            if c + 1 < NCH2:
                issue_transposes(c + 1)

            gslot_free = gate_slot_free

            def emit_gate_tile(i):
                gbi = i % 2
                gbank = 4 + i
                lg16 = gb_sb[:, gbi * 16 : gbi * 16 + 16]
                u2 = gb2_sb[:, gbi * 16 : gbi * 16 + 8]
                msq = gb2_sb[:, gbi * 16 + 8 : gbi * 16 + 9]
                negm2 = gb2_sb[:, gbi * 16 + 9 : gbi * 16 + 10]
                ssum = gb2_sb[:, gbi * 16 + 10 : gbi * 16 + 11]
                t2 = gb2_sb[:, gbi * 16 + 11 : gbi * 16 + 12]
                rr = gb2_sb[:, gbi * 16 + 12 : gbi * 16 + 13]
                nm, tok = ps_rd[gbank]
                p.wait(ten, nm, tok)
                if i == 0:
                    p.wait(ten, xsem[bx], xtok[bx])
                last = None
                for et in range(NE):
                    last = ten.matmul(
                        ps8[gbank][:, 0:16],
                        xnT_sb[:, xbase + et * TC2 + i * 128 : xbase + et * TC2 + i * 128 + 128],
                        gw2_sb[:, et * 16 : (et + 1) * 16],
                        start=(et == 0),
                        stop=False,
                    )
                last = ten.matmul(ps8[gbank][:, 0:16], ones_sb[:], gb2row_sb[:], start=False, stop=True)
                ttok = p.inc(last, "t")
                vfree, sfree = gslot_free[gbi]
                p.wait(sca, "t", ttok)
                p.wait(sca, "v", vfree)
                p.wait(sca, "s", sfree)
                p.inc(sca.activation(lg16, ps8[gbank][:, 0:16], AF.Identity), "s")
                p.wait(sca, "s")
                stok_sq = p.inc(
                    sca.activation(u2, lg16[:, 0:8], AF.Square, accum_out=msq), "s"
                )
                ps_rd[gbank] = ("s", stok_sq - 1)
                p.wait(sca, "s")
                p.inc(sca.activation(rr, msq, AF.Ln, scale=1.0 / 8.0, bias=eps_sb[:]), "s")
                p.wait(sca, "s")
                stok_r = p.inc(sca.activation(rr, rr, AF.Exp, scale=-0.5), "s")
                p.wait(vec, "s", stok_r)
                p.inc(vec.tensor_scalar(u2, lg16[:, 8:16], rr, 2.0, ALU.mult, ALU.mult), "v")
                p.wait(vec, "v")
                vtok_m = p.inc(vec.reduce_max(negm2, u2, axis=AX.X, negate=True), "v")
                p.wait(sca, "v", vtok_m)
                stok_e = p.inc(
                    sca.activation(u2, u2, AF.Exp, bias=negm2, accum_out=ssum), "s"
                )
                p.wait(vec, "s", stok_e)
                p.inc(vec.tensor_scalar(t2, ssum, 1e-6, 1.0, ALU.mult, ALU.add), "v")
                p.wait(vec, "v")
                vtok_n = p.inc(vec.reciprocal(ng_sb[:, (c % 2) * 4 + i : (c % 2) * 4 + i + 1], t2), "v")
                gslot_free[gbi] = (vtok_n, stok_e)

            for kk in range(NK2):
                if kk in (12, 24, 36, 48):
                    emit_gate_tile(kk // 12 - 1)
                gidx = c * NK2 + kk
                sl = gidx % 4
                if gidx + 3 < NCH2 * NK2:
                    issue_col(gidx + 3)
                bank = kk % 3
                nm, tok = ps_rd[bank]
                p.wait(ten, nm, tok)
                p.wait(ten, col_sem[sl], ctok[sl])
                if kk == 0:
                    p.wait(ten, xsem[bx], xtok[bx])
                last = None
                for et in range(NE):
                    last = ten.matmul(
                        ps8[bank][:],
                        col_sb[:, sl * 1024 + et * 128 : sl * 1024 + et * 128 + 128],
                        xnT_sb[:, xbase + et * TC2 : xbase + (et + 1) * TC2],
                        start=(et == 0),
                        stop=(et == NE - 1),
                    )
                ttok = p.inc(last, "t")
                colfree[sl] = ttok
                if kk == NK2 - 1:
                    t_xnT_prev[bx] = ttok
                p.wait(sca, "t", ttok)
                if kk == 0:
                    p.wait(sca, "t", t_mm2_prev)
                ps_rd[bank] = (
                    "s",
                    p.inc(
                        sca.activation(
                            hall_sb[:, kk * TC2 : (kk + 1) * TC2],
                            ps8[bank][:],
                            AF.Gelu,
                            bias=b1col_sb[:, kk : kk + 1],
                        ),
                        "s",
                    ),
                )
            s_hall_done = p.val["s"]
            emit_some_era(TC2 // 128)

            p.wait(ten, "s", s_hall_done)
            for bank in range(8):
                nm, tok = ps_rd[bank]
                p.wait(ten, nm, tok)
            for kk in range(NK2):
                gidx = c * NK2 + kk
                sl = gidx % 4
                if gidx + 3 < NCH2 * NK2:
                    issue_w2(gidx + 3)
                p.wait(ten, w2_sem[sl], utok[sl])
                last = None
                for tt in range(TC2 // 128):
                    for dh in range(2):
                        last = ten.matmul(
                            ps8[tt * 2 + dh][:],
                            hall_sb[:, kk * TC2 + tt * 128 : kk * TC2 + tt * 128 + 128],
                            w2t_sb[:, sl * DM + dh * 512 : sl * DM + dh * 512 + 512],
                            start=(kk == 0),
                            stop=False,
                        )
                ttok = p.inc(last, "t")
                w2free[sl] = ttok
                if kk == NK2 - 1:
                    t_mm2_prev = ttok
            last = None
            for tt in range(TC2 // 128):
                for dh in range(2):
                    last = ten.matmul(
                        ps8[tt * 2 + dh][:],
                        ones_sb[:],
                        B2bc_sb[0:1, dh * 512 : dh * 512 + 512],
                        start=False,
                        stop=True,
                    )
            p.inc(last, "t")
            tz2 = p.val["t"]

            evac_free = prev_evac_free
            prev_ys_rd = getattr(p, "d_prev_ys_rd", (0, 0))
            for tt in range(TC2 // 128):
                b = tt % 2
                gi = (c % 2) * 4 + tt
                row = tok0 + tt * 128
                vf, sf = evac_free[b]
                p.wait(syn, "v", vf)
                x2s = x2st_sb[:, b * DM : (b + 1) * DM]
                dres = p.inc(syn.dma_start(x2s, x[row : row + 128, :]), "r%d" % b, 16)
                ys = y_sb[:, 0:DM]
                p.wait(sca, "t", tz2)
                p.wait(sca, "v", prev_ys_rd[0])
                p.wait(sca, "s", prev_ys_rd[1])
                p.wait(sca, "v", vf)
                p.wait(sca, "s", sf)
                stk = 0
                for dh in range(2):
                    stk = p.inc(
                        sca.activation(
                            ys[:, dh * 512 : dh * 512 + 512],
                            ps8[tt * 2 + dh][:],
                            AF.Identity,
                            scale=ng_sb[:, gi : gi + 1],
                        ),
                        "s",
                    )
                    ps_rd[tt * 2 + dh] = ("s", stk)
                p.wait(vec, "s", stk)
                vtok, stok, rstd, negmur = emit_ln_stats(p, lb, 2 + b, ys, DM)
                tmps = tmp_sb[:, b * DM : (b + 1) * DM]
                outs = out_sb[:, b * DM : (b + 1) * DM]
                p.wait(sca, "v", vtok)
                p.wait(sca, "s", stok)
                stok2 = p.inc(
                    sca.activation(tmps, ys, AF.Identity, bias=negmur, scale=rstd), "s"
                )
                lb.free_s[2 + b] = stok2
                prev_ys_rd = (vtok, stok2)
                p.d_prev_ys_rd = prev_ys_rd
                p.wait(vec, "s", stok2)
                p.wait(vec, "r%d" % b, dres)
                p.wait(vec, "e%d" % b)
                vtok2 = p.inc(vec.tensor_tensor(outs, tmps, x2s, ALU.add), "v")
                p.wait(syn, "v", vtok2)
                p.inc(syn.dma_start(y[row : row + 128, :], outs), "e%d" % b, 16)
                evac_free[b] = (vtok2, stok2)
            prev_evac_free = evac_free

        p.wait(syn, "e0")
        p.wait(syn, "e1")

def build_fused():
    nc = bass.Bass("TRN2", target_bir_lowering=False)
    xin = nc.dram_tensor("xin", [TOK, DM], F32, kind="ExternalInput").ap()
    yout = nc.dram_tensor("yout", [TOK, DM], F32, kind="ExternalOutput").ap()
    x1d = nc.dram_tensor("x1d", [TOK, DM], F32, kind="Internal").ap()
    x2d = nc.dram_tensor("x2d", [TOK, DM], F32, kind="Internal").ap()
    x3d = nc.dram_tensor("x3d", [TOK, DM], F32, kind="Internal").ap()
    with ExitStack() as octx:
        p = Prog(nc, octx, extra=("w0", "w1", "w2", "w3", "u0", "u1", "u2", "u3",
                                  "g0", "g1", "k0", "k1", "k2", "k3", "xs", "wp",
                                  "c0", "c1", "c2", "c3", "c4", "c5", "dc", "gc"))
        blk = octx.enter_context(nc.Block())
        Wq_in = dram_cached(nc, "Wq", [DM, DM], F32, kind="ExternalInput").ap()
        Wk_in = dram_cached(nc, "Wk", [DM, DM], F32, kind="ExternalInput").ap()
        Wv_in = dram_cached(nc, "Wv", [DM, DM], F32, kind="ExternalInput").ap()
        wstage = dram_cached(nc, "wstage", [3, DM, DM], BF16, kind="Internal").ap()
        for m, W in enumerate((Wq_in, Wk_in, Wv_in)):
            p.inc(p.gps.dma_start(wstage[m], W), "g", 16)
        w1_in = dram_cached(nc, "w1", [NEXP, DM, 2048], F32, kind="ExternalInput").ap()
        w2_in = dram_cached(nc, "w2", [NEXP * 2048, DM], F32, kind="ExternalInput").ap()
        w1dd = dram_cached(nc, "w1dd", [NEXP * NH * 128, DM], BF16, kind="Internal").ap()
        w2dd = dram_cached(nc, "w2dd", [NEXP * 2048, DM], BF16, kind="Internal").ap()
        pieces = []
        for e in range(NEXP):
            for ht in range(NH):
                dstap = custom_ap(
                    w1dd,
                    (e * NH + ht) * 128 * DM,
                    [[DM, 128], [128, NE], [1, 128]],
                )
                srcap = custom_ap(
                    w1_in,
                    e * DM * 2048 + ht * 128,
                    [[2048, 128], [128 * 2048, NE], [1, 128]],
                )
                pieces.append((dstap, srcap))
        for r in range(0, NEXP * 2048, 512):
            pieces.append((w2dd[r : r + 512, :], w2_in[r : r + 512, :]))
        p.cast_pieces = pieces
        p.cast_idx = 0

        def pump(n=1):
            while n > 0 and p.cast_idx < len(p.cast_pieces):
                dst, srcp = p.cast_pieces[p.cast_idx]
                p.inc(p.gps.dma_start(dst, srcp), "dc", 16)
                p.cast_idx += 1
                n -= 1

        p.pump = pump
        emit_l0_attn(nc, octx, p, xin, x1d)
        p.barrier()
        emit_l0_ffn(nc, octx, p, x1d, x2d)
        p.barrier()
        emit_l1_attn(nc, octx, p, x2d, x3d)
        p.barrier()
        emit_l1_ffn(nc, octx, p, x3d, yout)
        p.barrier()
    return nc


def kernel(
    L0_attn_X,
    L0_attn_Wq,
    L0_attn_bq,
    L0_attn_Wk,
    L0_attn_bk,
    L0_attn_Wv,
    L0_attn_bv,
    L0_ffn_k0,
    L0_ffn_k1,
    L0_ffn_k2,
    L0_ffn_s0,
    L0_ffn_s1,
    L0_ffn_s2,
    L1_attn_W,
    L1_attn_b,
    L1_ffn_gate_w,
    L1_ffn_gate_b,
    L1_ffn_rms_w,
    L1_ffn_w1,
    L1_ffn_b1,
    L1_ffn_w2,
    L1_ffn_b2,
):
    X = np.asarray(L0_attn_X, dtype=np.float32)
    ones, idf, cmask = host_consts()
    idb = np.eye(128, dtype=np.float32).astype(ml_dtypes.bfloat16)
    onescol = np.ones((128, 1), dtype=np.float32)
    nc = build_fused()
    in_maps = []
    for c in range(NCORES):
        xs = np.ascontiguousarray(
            X[c * SEQ_PER_CORE : (c + 1) * SEQ_PER_CORE].reshape(TOK, DM)
        )
        in_maps.append(
            {
                "xin": xs,
                "Wq": np.asarray(L0_attn_Wq), "Wk": np.asarray(L0_attn_Wk),
                "Wv": np.asarray(L0_attn_Wv),
                "bq": np.asarray(L0_attn_bq), "bk": np.asarray(L0_attn_bk),
                "bv": np.asarray(L0_attn_bv),
                "ones": ones, "idf": idf, "cmask": cmask, "idb": idb,
                "onescol": onescol,
                "k0": np.asarray(L0_ffn_k0).view(np.uint8),
                "k1": np.asarray(L0_ffn_k1).view(np.uint8),
                "k2": np.asarray(L0_ffn_k2).view(np.uint8),
                "s0": np.asarray(L0_ffn_s0, dtype=np.float32).reshape(1, 1),
                "s1": np.asarray(L0_ffn_s1, dtype=np.float32).reshape(1, 1),
                "s2": np.asarray(L0_ffn_s2, dtype=np.float32).reshape(1, 1),
                "W": np.asarray(L1_attn_W),
                "bb": np.asarray(L1_attn_b).reshape(1, 32),
                "gw": np.asarray(L1_ffn_gate_w),
                "gb": np.asarray(L1_ffn_gate_b).reshape(1, 8),
                "rmsw": np.asarray(L1_ffn_rms_w).reshape(1, 8),
                "w1": np.asarray(L1_ffn_w1),
                "b1": np.asarray(L1_ffn_b1).reshape(-1),
                "w2": np.asarray(L1_ffn_w2).reshape(-1, DM),
                "b2": np.asarray(L1_ffn_b2),
            }
        )
    res = run_bass_kernel_spmd(nc, in_maps, core_ids=list(range(NCORES)))
    out = np.concatenate(
        [r["yout"].reshape(SEQ_PER_CORE, SEQ, DM) for r in res.results], axis=0
    )
    return np.ascontiguousarray(out.astype(np.float32))


# version 20
# speedup vs baseline: 1.1102x; 1.1102x over previous
import numpy as np
import ml_dtypes
import bass_rust
import concourse.bass as bass
import concourse.mybir as mybir
from concourse.bass_utils import run_bass_kernel_spmd
from contextlib import ExitStack

F32 = mybir.dt.float32
BF16 = mybir.dt.bfloat16
U8 = mybir.dt.uint8
AF = mybir.ActivationFunctionType
ALU = mybir.AluOpType
AX = mybir.AxisListType

NCORES = 8
SEQ = 2048
DM = 1024
SEQ_PER_CORE = 2
TOK = SEQ * SEQ_PER_CORE
NTS = SEQ // 128
NE = DM // 128
EPS = 1e-5
NEG = -1.0e30


def custom_ap(base_ap, offset_elems, pairs):
    a = base_ap.copy()
    a.offset = offset_elems
    a.ap = bass_rust.VecI64Pair([list(x) for x in pairs])
    return a


class Prog:

    def __init__(self, nc, ctx, extra=()):
        self.nc = nc
        self.ctx = ctx
        self.ten = nc.tensor
        self.vec = nc.vector
        self.sca = nc.scalar
        self.gps = nc.gpsimd
        self.syn = nc.sync
        self.sem = {}
        self.val = {}
        for name in ("t", "v", "s", "g", "d", "d0", "d1", "r0", "r1", "x", "e0", "e1") + tuple(extra):
            self.sem[name] = ctx.enter_context(nc.semaphore("sem_" + name))
            self.val[name] = 0

    def inc(self, instr, name, step=1):
        instr.then_inc(self.sem[name], step)
        self.val[name] += step
        return self.val[name]

    def wait(self, engine, name, v=None):
        if v is None:
            v = self.val[name]
        if v > 0:
            engine.wait_ge(self.sem[name], v)

    def barrier(self, skip=("dc",)):
        for eng in (self.ten, self.vec, self.sca, self.gps, self.syn):
            for name in self.sem:
                if name in skip:
                    continue
                self.wait(eng, name)


class LNBufs:

    def __init__(self, nc, ctx, nslots, pf=""):
        E = ctx.enter_context
        self.stats = E(nc.sbuf_tensor(pf + "ln_stats", [128, 12], F32))
        self.mv = E(nc.sbuf_tensor(pf + "ln_mv", [128, 3 * nslots], F32))
        self.rstd = E(nc.sbuf_tensor(pf + "ln_rstd", [128, nslots], F32))
        self.negmur = E(nc.sbuf_tensor(pf + "ln_negmur", [128, nslots], F32))
        self.free_s = [0] * nslots
        self.free_v = [0] * nslots
        self.eps = E(nc.sbuf_tensor(pf + "ln_eps", [128, 1], F32))

    def init(self, p):
        p.inc(p.vec.memset(self.eps[:], EPS), "v")


def emit_ln_stats(p, lb, slot, src_ap, w):
    vec, sca = p.vec, p.sca
    mv = lb.mv[:, 3 * slot : 3 * slot + 3]
    rstd = lb.rstd[:, slot : slot + 1]
    negmur = lb.negmur[:, slot : slot + 1]
    p.wait(vec, "s", lb.free_s[slot])
    p.wait(sca, "v", lb.free_v[slot])
    nchunk = (w + 511) // 512
    ins = None
    for c in range(nchunk):
        lo = c * 512
        hi = min(w, lo + 512)
        ins = vec.bn_stats(lb.stats[:, 6 * c : 6 * c + 6], src_ap[:, lo:hi])
    p.inc(ins, "v")
    p.wait(vec, "v")
    p.inc(vec.bn_aggr(mv[:, 0:2], lb.stats[:, 0 : 6 * nchunk]), "v")
    p.wait(sca, "v")
    p.inc(sca.activation(rstd, mv[:, 1:2], AF.Ln, bias=lb.eps[:]), "s")
    p.wait(sca, "s")
    s_tok = p.inc(sca.activation(rstd, rstd, AF.Exp, scale=-0.5), "s")
    p.wait(vec, "s", s_tok)
    v_tok = p.inc(
        vec.tensor_scalar(negmur, mv[:, 0:1], rstd, -1.0, ALU.mult, ALU.mult), "v"
    )
    lb.free_s[slot] = s_tok
    lb.free_v[slot] = v_tok
    return v_tok, s_tok, rstd, negmur



TC = 512
NCH = TOK // TC
NF1 = 4096 // 128
NF2 = 4096 // 128
RPAD = 160
TC2 = 512
NCH2 = TOK // TC2
NEXP = 8
NH = 2048 // 128
NK2 = NEXP * NH
EPS32 = float(np.finfo(np.float32).eps)


def P_ap(big_sb, off, w):
    return big_sb[:, off : off + w]


def host_consts():
    ones = np.ones((1, 128), dtype=np.float32)
    idf = np.eye(128, dtype=np.float32)
    cmask = np.zeros((4, 128, 512), dtype=np.float32)
    for r in range(4):
        jl = np.arange(128)[:, None]
        il = np.arange(512)[None, :]
        cmask[r][il < jl + 128 * r] = NEG
    return ones, idf, cmask


_DRAM_CACHE = {}


def dram_cached(nc, name, shape, dtype, kind="Internal"):
    key = (id(nc), name)
    if key not in _DRAM_CACHE:
        _DRAM_CACHE[key] = nc.dram_tensor(name, shape, dtype, kind=kind)
    return _DRAM_CACHE[key]


def emit_l0_attn(nc, ctx, p, x, y):
    Wq = dram_cached(nc, "Wq", [DM, DM], F32, kind="ExternalInput").ap()
    Wk = dram_cached(nc, "Wk", [DM, DM], F32, kind="ExternalInput").ap()
    Wv = dram_cached(nc, "Wv", [DM, DM], F32, kind="ExternalInput").ap()
    bq = dram_cached(nc, "bq", [DM], F32, kind="ExternalInput").ap()
    bk = dram_cached(nc, "bk", [DM], F32, kind="ExternalInput").ap()
    bv = dram_cached(nc, "bv", [DM], F32, kind="ExternalInput").ap()
    ones = dram_cached(nc, "ones", [1, 128], F32, kind="ExternalInput").ap()
    idf = dram_cached(nc, "idf", [128, 128], F32, kind="ExternalInput").ap()
    cmask = dram_cached(nc, "cmask", [4, 128, 512], F32, kind="ExternalInput").ap()
    wstage = dram_cached(nc, "wstage", [3, DM, DM], BF16, kind="Internal").ap()
    xnd = dram_cached(nc, "xnd", [SEQ_PER_CORE, SEQ, DM], BF16, kind="Internal").ap()

    p_off = []
    acc = 0
    for jt in range(NTS):
        bi0 = (128 * jt) // 512
        p_off.append((acc, 512 * bi0, SEQ - 512 * bi0))
        acc += SEQ - 512 * bi0

    with ExitStack() as ctx:
        E = ctx.enter_context
        ten, vec, sca, gps, syn = p.ten, p.vec, p.sca, p.gps, p.syn
        lb = LNBufs(nc, ctx, 4, "a_")
        lb.init(p)

        ones_sb = E(nc.sbuf_tensor("a_" + "ones_sb", [1, 128], F32))
        id8_sb = E(nc.sbuf_tensor("a_" + "id8_sb", [8, 8], F32))
        braw_sb = E(nc.sbuf_tensor("a_" + "braw_sb", [8, 256], F32))
        bcol_sb = E(nc.sbuf_tensor("a_" + "bcol_sb", [128, 16], F32))
        bvrow_sb = E(nc.sbuf_tensor("a_" + "bvrow_sb", [1, DM], F32))
        mask_sb = E(nc.sbuf_tensor("a_" + "mask_sb", [128, 4 * 512], F32))
        sm_sb = E(nc.sbuf_tensor("a_" + "sm_sb", [128, 8], F32))
        xst_sb = E(nc.sbuf_tensor("a_" + "xst_sb", [128, 2 * DM], F32))
        xnst_sb = E(nc.sbuf_tensor("a_" + "xnst_sb", [128, 2 * DM], BF16))
        x2st_sb = E(nc.sbuf_tensor("a_" + "x2st_sb", [128, 2 * DM], F32))
        tmp_sb = E(nc.sbuf_tensor("a_" + "tmp_sb", [128, 2 * DM], F32))
        out_sb = E(nc.sbuf_tensor("a_" + "out_sb", [128, 2 * DM], F32))
        qT_sb = E(nc.sbuf_tensor("a_" + "qT_sb", [128, NE * SEQ], BF16))
        kT_sb = E(nc.sbuf_tensor("a_" + "kT_sb", [128, NE * SEQ], BF16))
        v_sb = E(nc.sbuf_tensor("a_" + "v_sb", [128, NTS * DM], BF16))
        st_sb = E(nc.sbuf_tensor("a_" + "st_sb", [128, SEQ], F32))
        big_sb = E(nc.sbuf_tensor("a_" + "big_sb", [128, 24576], BF16))
        xnT = big_sb[:, 0:16384]
        wTf = big_sb[:, 16384:24576]

        ps_a = E(nc.psum_tensor("a_" + "ps_a", [128, 512], F32))
        ps_b = E(nc.psum_tensor("a_" + "ps_b", [128, 512], F32))
        ps_c = E(nc.psum_tensor("a_" + "ps_c", [128, 512], F32))
        ps_o = [E(nc.psum_tensor("a_" + f"ps_o{i}", [128, DM], F32)) for i in range(2)]


        p.inc(syn.dma_start(ones_sb[:], ones), "d", 16)
        p.inc(syn.dma_start(id8_sb[:], idf[0:8, 0:8]), "d", 16)
        p.inc(syn.dma_start(braw_sb[:, 0:128], bq.rearrange("(a b) -> a b", a=8)), "d", 16)
        p.inc(syn.dma_start(braw_sb[:, 128:256], bk.rearrange("(a b) -> a b", a=8)), "d", 16)
        p.inc(syn.dma_start(bvrow_sb[:], bv.rearrange("(a b) -> a b", a=1)), "d", 16)
        for r in range(4):
            p.inc(syn.dma_start(mask_sb[:, r * 512 : (r + 1) * 512], cmask[r]), "d", 16)
        p.wait(ten, "d")
        p.inc(ten.transpose(ps_c[:, 0:8], braw_sb[0:8, 0:128], id8_sb[:]), "t")
        p.wait(sca, "t")
        p.inc(sca.activation(bcol_sb[:, 0:8], ps_c[:, 0:8], AF.Identity, scale=1.0 / 32.0), "s")
        p.wait(ten, "s")
        p.inc(ten.transpose(ps_c[:, 8:16], braw_sb[0:8, 128:256], id8_sb[:]), "t")
        p.wait(sca, "t")
        p.inc(sca.activation(bcol_sb[:, 8:16], ps_c[:, 8:16], AF.Identity), "s")
        p.barrier()

        a_stage = [(0, 0, 0)] * 2
        a_era_tok = {}
        a_era_half = {}

        def emit_era_tile_A(s_, i_):
            b = i_ % 2
            vf, sf, ef = a_stage[b]
            p.wait(syn, "v", vf)
            p.wait(syn, "s", sf)
            p.wait(syn, "e%d" % b, ef)
            xs = xst_sb[:, b * DM : (b + 1) * DM]
            xns = xnst_sb[:, b * DM : (b + 1) * DM]
            p.wait(syn, "d%d" % b)
            dtok = p.inc(
                syn.dma_start(xs, x[s_ * SEQ + i_ * 128 : s_ * SEQ + (i_ + 1) * 128, :]),
                "d%d" % b, 16,
            )
            p.wait(vec, "d%d" % b, dtok)
            vtok, stok, rstd, negmur = emit_ln_stats(p, lb, b, xs, DM)
            p.wait(vec, "v")
            p.wait(vec, "e%d" % b, ef)
            aptok = p.inc(
                vec.tensor_scalar(xns, xs, rstd, negmur, ALU.mult, ALU.add), "v"
            )
            lb.free_v[b] = aptok
            p.wait(syn, "v", aptok)
            p.wait(syn, "e%d" % b)
            etok = p.inc(
                syn.dma_start(xnd[s_, i_ * 128 : (i_ + 1) * 128, :], xns), "e%d" % b, 16
            )
            a_stage[b] = (aptok, stok, etok)
            a_era_tok[s_] = (p.val["e0"], p.val["e1"])
            if i_ == 7 or i_ == 15:
                a_era_half[(s_, i_ // 8)] = (p.val["e0"], p.val["e1"])

        for i_ in range(NTS):
            emit_era_tile_A(0, i_)

        for s in range(SEQ_PER_CORE):
            xseq = x[s * SEQ : (s + 1) * SEQ, :]
            yseq = y[s * SEQ : (s + 1) * SEQ, :]

            xhalf = [0, 0]
            hsem = ["x", "xs"]
            for half in range(2):
                e0t, e1t = a_era_half[(s, half)]
                p.wait(syn, "e0", e0t)
                p.wait(syn, "e1", e1t)
                p.wait(syn, hsem[half])
                for et in range(NE):
                    p.inc(
                        syn.dma_start(
                            xnT[:, et * SEQ + half * 1024 : et * SEQ + half * 1024 + 1024],
                            xnd[s, half * 1024 : (half + 1) * 1024, et * 128 : (et + 1) * 128],
                            transpose=True,
                        ),
                        hsem[half],
                        16,
                    )
                xhalf[half] = p.val[hsem[half]]

            t_qkv_prev = 0
            evac_parity = {}
            for m in range(3):
                p.wait(syn, "t", t_qkv_prev)
                for et in range(NE):
                    p.inc(
                        syn.dma_start(
                            wTf[:, et * DM : (et + 1) * DM],
                            wstage[m, :, et * 128 : (et + 1) * 128],
                            transpose=True,
                        ),
                        "u0",
                        16,
                    )
                u_m = p.val["u0"]
                if m < 2:
                    dst = qT_sb if m == 0 else kT_sb
                    for tb in range(SEQ // 512):
                        for ft in range(NE):
                            key = (ft * 4 + tb) % 2
                            ps = ps_a if key == 0 else ps_b
                            if key in evac_parity:
                                p.wait(ten, "s", evac_parity[key])
                            if ft == 0:
                                p.wait(ten, ["x", "xs"][tb // 2], xhalf[tb // 2])
                                p.wait(ten, "u0", u_m)
                            last = None
                            for et in range(NE):
                                last = ten.matmul(
                                    ps[:],
                                    wTf[:, et * DM + ft * 128 : et * DM + ft * 128 + 128],
                                    xnT[:, et * SEQ + tb * 512 : et * SEQ + tb * 512 + 512],
                                    start=(et == 0),
                                    stop=(et == NE - 1),
                                )
                            ttok = p.inc(last, "t")
                            p.wait(sca, "t", ttok)
                            kw = (
                                dict(scale=1.0 / 32.0, bias=bcol_sb[:, ft : ft + 1])
                                if m == 0
                                else dict(bias=bcol_sb[:, 8 + ft : 9 + ft])
                            )
                            ins = sca.activation(
                                dst[:, ft * SEQ + tb * 512 : ft * SEQ + tb * 512 + 512],
                                ps[:],
                                AF.Identity,
                                **kw,
                            )
                            evac_parity[key] = p.inc(ins, "s")
                    t_qkv_prev = p.val["t"]
                else:
                    for tt in range(NTS):
                        for fb in range(2):
                            key = (tt * 2 + fb) % 2
                            ps = ps_a if key == 0 else ps_b
                            if key in evac_parity:
                                p.wait(ten, "s", evac_parity[key])
                            if fb == 0:
                                p.wait(ten, ["x", "xs"][tt // 8], xhalf[tt // 8])
                                p.wait(ten, "u0", u_m)
                            for et in range(NE):
                                ten.matmul(
                                    ps[:],
                                    xnT[:, et * SEQ + tt * 128 : et * SEQ + tt * 128 + 128],
                                    wTf[:, et * DM + fb * 512 : et * DM + fb * 512 + 512],
                                    start=(et == 0),
                                    stop=False,
                                )
                            last = ten.matmul(
                                ps[:],
                                ones_sb[:],
                                bvrow_sb[:, fb * 512 : fb * 512 + 512],
                                start=False,
                                stop=True,
                            )
                            ttok = p.inc(last, "t")
                            p.wait(sca, "t", ttok)
                            ins = sca.activation(
                                v_sb[:, tt * DM + fb * 512 : tt * DM + fb * 512 + 512],
                                ps[:],
                                AF.Identity,
                            )
                            evac_parity[key] = p.inc(ins, "s")
                    t_qkv_prev = p.val["t"]
            p.wait(ten, "s")

            prev_exp_s = 0
            ps_free = {0: ("s", 0), 1: ("s", 0)}
            for jt in range(NTS):
                if s == 0:
                    emit_era_tile_A(1, jt)
                if jt % 2 == 0:
                    p.pump(1)
                off, i0, wc = p_off[jt]
                bi0 = i0 // 512
                evac_tok = []
                for ib in range(bi0, 4):
                    key = ib % 2
                    ps = ps_a if key == 0 else ps_b
                    nm, tok = ps_free[key]
                    p.wait(ten, nm, tok)
                    last = None
                    for et in range(NE):
                        last = ten.matmul(
                            ps[:],
                            kT_sb[:, et * SEQ + jt * 128 : et * SEQ + jt * 128 + 128],
                            qT_sb[:, et * SEQ + ib * 512 : et * SEQ + ib * 512 + 512],
                            start=(et == 0),
                            stop=(et == NE - 1),
                        )
                    ttok = p.inc(last, "t")
                    if ib == bi0:
                        r = jt - 4 * bi0
                        p.wait(vec, "t", ttok)
                        p.wait(vec, "s", prev_exp_s)
                        tok = p.inc(
                            vec.tensor_tensor(
                                st_sb[:, ib * 512 : ib * 512 + 512],
                                ps[:],
                                mask_sb[:, r * 512 : r * 512 + 512],
                                ALU.add,
                            ),
                            "v",
                        )
                        evac_tok.append(("v", tok))
                        ps_free[key] = ("v", tok)
                    else:
                        p.wait(sca, "t", ttok)
                        p.wait(sca, "s", prev_exp_s)
                        tok = p.inc(
                            sca.activation(
                                st_sb[:, ib * 512 : ib * 512 + 512], ps[:], AF.Identity
                            ),
                            "s",
                        )
                        evac_tok.append(("s", tok))
                        ps_free[key] = ("s", tok)
                for nm, tok in evac_tok:
                    p.wait(vec, nm, tok)
                p.wait(vec, "s", prev_exp_s)
                p.inc(
                    vec.reduce_max(sm_sb[:, 0:1], st_sb[:, i0:SEQ], axis=AX.X, negate=True),
                    "v",
                )
                for nm, tok in evac_tok:
                    p.wait(sca, nm, tok)
                p.wait(sca, "v")
                prev_exp_s = p.inc(
                    sca.activation(
                        P_ap(big_sb, off, wc),
                        st_sb[:, i0:SEQ],
                        AF.Exp,
                        bias=sm_sb[:, 0:1],
                        accum_out=sm_sb[:, 1:2],
                    ),
                    "s",
                )
                p.wait(vec, "s", prev_exp_s)
                p.inc(vec.reciprocal(sm_sb[:, 2:3], sm_sb[:, 1:2]), "v")
                p.wait(vec, "v")
                p.inc(
                    vec.tensor_scalar(
                        v_sb[:, jt * DM : (jt + 1) * DM],
                        v_sb[:, jt * DM : (jt + 1) * DM],
                        sm_sb[:, 2:3],
                        None,
                        ALU.mult,
                    ),
                    "v",
                )
            p.barrier()

            ofree = [(0, 0, 0)] * 2
            for it in range(NTS):
                if it % 2 == 0:
                    p.pump(1)
                b = it % 2
                vf, sf, ef = ofree[b]
                p.wait(syn, "v", vf)
                x2s = x2st_sb[:, b * DM : (b + 1) * DM]
                dres = p.inc(syn.dma_start(x2s, xseq[it * 128 : (it + 1) * 128, :]), "r%d" % b, 16)
                p.wait(ten, "v", vf)
                p.wait(ten, "s", sf)
                pso = ps_o[b]
                for eb in range(2):
                    first = True
                    last = None
                    for jt in range(it + 1):
                        off, i0, wc = p_off[jt]
                        col = it * 128 - i0
                        last = ten.matmul(
                            pso[:, eb * 512 : eb * 512 + 512],
                            P_ap(big_sb, off + col, 128),
                            v_sb[:, jt * DM + eb * 512 : jt * DM + eb * 512 + 512],
                            start=first,
                            stop=(jt == it),
                        )
                        first = False
                    if eb == 1:
                        ttok = p.inc(last, "t")
                p.wait(vec, "t", ttok)
                vtok, stok, rstd, negmur = emit_ln_stats(p, lb, 2 + b, pso, DM)
                tmps = tmp_sb[:, b * DM : (b + 1) * DM]
                outs = out_sb[:, b * DM : (b + 1) * DM]
                p.wait(sca, "v", vtok)
                p.wait(sca, "s", stok)
                stok2 = p.inc(
                    sca.activation(tmps, pso[:], AF.Identity, bias=negmur, scale=rstd), "s"
                )
                lb.free_s[2 + b] = stok2
                p.wait(vec, "s", stok2)
                p.wait(vec, "r%d" % b, dres)
                p.wait(vec, "e%d" % b, ef)
                vtok2 = p.inc(vec.tensor_tensor(outs, tmps, x2s, ALU.add), "v")
                p.wait(syn, "v", vtok2)
                etok = p.inc(syn.dma_start(yseq[it * 128 : (it + 1) * 128, :], outs), "e%d" % b, 16)
                ofree[b] = (vtok2, stok2, etok)
            p.barrier()

        p.wait(syn, "e0")
        p.wait(syn, "e1")
        p.wait(syn, "d")



def emit_l0_ffn(nc, ctx, p, x, y):
    k0 = dram_cached(nc, "k0", [2 * DM, 4096], U8, kind="ExternalInput").ap()
    k1 = dram_cached(nc, "k1", [4096, 4096], U8, kind="ExternalInput").ap()
    k2 = dram_cached(nc, "k2", [4096, DM], U8, kind="ExternalInput").ap()
    s0 = dram_cached(nc, "s0", [1, 1], F32, kind="ExternalInput").ap()
    s1 = dram_cached(nc, "s1", [1, 1], F32, kind="ExternalInput").ap()
    s2 = dram_cached(nc, "s2", [1, 1], F32, kind="ExternalInput").ap()
    ones = dram_cached(nc, "ones", [1, 128], F32, kind="ExternalInput").ap()
    w0d = dram_cached(nc, "w0d", [DM, 4096], BF16, kind="Internal").ap()
    w1d = dram_cached(nc, "w1d", [4096, 4096], BF16, kind="Internal").ap()
    w2d = dram_cached(nc, "w2d", [4096, DM], BF16, kind="Internal").ap()
    xnd = dram_cached(nc, "xnd_b", [TOK, DM], BF16, kind="Internal").ap()

    with ExitStack() as ctx:
        E = ctx.enter_context
        ten, vec, sca, gps, syn = p.ten, p.vec, p.sca, p.gps, p.syn
        lb = LNBufs(nc, ctx, 4, "b_")
        lb.init(p)

        ones_sb = E(nc.sbuf_tensor("b_ones_sb", [1, 128], F32))
        svec_sb = E(nc.sbuf_tensor("b_svec_sb", [1, 4], F32))
        sc_sb = E(nc.sbuf_tensor("b_sc_sb", [128, 4], F32))
        xst_sb = E(nc.sbuf_tensor("b_xst_sb", [128, 2 * DM], F32))
        xnst_sb = E(nc.sbuf_tensor("b_xnst_sb", [128, 2 * DM], BF16))
        x2st_sb = E(nc.sbuf_tensor("b_x2st_sb", [128, 2 * DM], F32))
        tmp_sb = E(nc.sbuf_tensor("b_tmp_sb", [128, 2 * DM], F32))
        out_sb = E(nc.sbuf_tensor("b_out_sb", [128, 2 * DM], F32))
        y_sb = E(nc.sbuf_tensor("b_y_sb", [128, 2 * DM], F32))
        xnT_sb = E(nc.sbuf_tensor("b_xnT_sb", [128, 2 * NE * TC], BF16))
        h1_sb = E(nc.sbuf_tensor("b_h1_sb", [128, NF1 * TC], BF16))
        h2_sb = E(nc.sbuf_tensor("b_h2_sb", [128, NF2 * TC], BF16))
        cvt_sb = h1_sb
        cvo_sb = h2_sb
        col_sb = E(nc.sbuf_tensor("b_col_sb", [128, 4 * NF1 * 128], BF16))
        w2t_sb = E(nc.sbuf_tensor("b_w2t_sb", [128, 4 * DM], BF16))

        ps = [E(nc.psum_tensor(f"b_ps{i}", [128, 512], F32)) for i in range(8)]

        p.inc(syn.dma_start(ones_sb[:], ones), "d", 16)
        p.inc(syn.dma_start(svec_sb[:, 0:1], s0), "d", 16)
        p.inc(syn.dma_start(svec_sb[:, 1:2], s1), "d", 16)
        p.inc(syn.dma_start(svec_sb[:, 2:3], s2), "d", 16)
        p.wait(ten, "d")
        last = ten.matmul(ps[0][:, 0:3], ones_sb[:], svec_sb[:, 0:3], start=True, stop=True)
        p.inc(last, "t")
        p.wait(sca, "t")
        p.inc(sca.activation(sc_sb[:, 0:1], ps[0][:, 0:1], AF.Identity, scale=4.0), "s")
        p.inc(sca.activation(sc_sb[:, 1:2], ps[0][:, 1:2], AF.Identity, scale=2.0), "s")
        p.inc(sca.activation(sc_sb[:, 2:3], ps[0][:, 2:3], AF.Identity), "s")

        cvt8 = cvt_sb.bitcast(U8)
        conv_jobs = []
        for et in range(NE):
            conv_jobs.append(("w0", et))
        for r in range(NF1):
            conv_jobs.append(("k1", r))
        for r in range(NF2):
            conv_jobs.append(("k2", r))

        slot_tt = [("v", 0)] * 4
        slot_st = [0] * 4
        st_sem = ["k0", "k1", "k2", "k3"]
        ld_sem = ["d0", "d1", "r0", "r1"]
        milestones = {}

        def conv_round(ridx, kind, arg):
            sl = ridx % 4
            eng = vec if ridx % 2 == 0 else gps
            ename = "v" if ridx % 2 == 0 else "gc"
            ci = cvt8[:, sl * 8192 : sl * 8192 + 4096]
            co = cvo_sb[:, sl * 4096 : sl * 4096 + 4096]
            nm, tok = slot_tt[sl]
            p.wait(syn, nm, tok)
            p.wait(syn, ld_sem[sl])
            if kind == "w0":
                et = arg
                t0 = cvt8[:, sl * 8192 : sl * 8192 + 4096]
                t1 = cvt8[:, sl * 8192 + 4096 : sl * 8192 + 8192]
                p.inc(syn.dma_start(t0, k0[et * 128 : et * 128 + 128, :]), ld_sem[sl], 16)
                ltok = p.inc(
                    syn.dma_start(t1, k0[DM + et * 128 : DM + et * 128 + 128, :]),
                    ld_sem[sl], 16,
                )
            else:
                kk, r = (k1, arg) if kind == "k1" else (k2, arg)
                cols = 4096 if kind == "k1" else DM
                nblk = 4096 // cols
                if nblk == 1:
                    ap_in = kk[r * 128 : (r + 1) * 128, :]
                else:
                    ap_in = custom_ap(kk, r * nblk * 128 * cols,
                                      [[cols, 128], [128 * cols, nblk], [1, cols]])
                ltok = p.inc(syn.dma_start(ci, ap_in), ld_sem[sl], 16)
            p.wait(eng, ld_sem[sl], ltok)
            p.wait(eng, st_sem[sl], slot_st[sl])
            if kind == "w0":
                t0 = cvt8[:, sl * 8192 : sl * 8192 + 4096]
                t1 = cvt8[:, sl * 8192 + 4096 : sl * 8192 + 8192]
                vtok = p.inc(eng.tensor_tensor(co, t0, t1, ALU.subtract), ename)
            else:
                vtok = p.inc(
                    eng.tensor_scalar(co, ci, 2.0, -1.0, ALU.mult, ALU.add), ename
                )
            slot_tt[sl] = (ename, vtok)
            p.wait(sca, ename, vtok)
            p.wait(sca, st_sem[sl])
            if kind == "w0":
                dst = w0d[arg * 128 : (arg + 1) * 128, :]
            elif kind == "k1":
                dst = w1d[arg * 128 : (arg + 1) * 128, :]
            else:
                r = arg
                nblk = 4
                dst = custom_ap(w2d, r * nblk * 128 * DM,
                                [[DM, 128], [128 * DM, nblk], [1, DM]])
            slot_st[sl] = p.inc(sca.dma_start(dst, co), st_sem[sl], 16)

        conv_jobs = []
        for et in range(NE):
            conv_jobs.append(("w0", et))
        for r in range(NF1):
            conv_jobs.append(("k1", r))
        for r in range(NF2 // 4):
            conv_jobs.append(("k2", r))

        stage_free = [(0, 0, 0)] * 2
        era_store_tok = {}

        def emit_era_tile(i):
            b = i % 2
            vf, sf, ef = stage_free[b]
            p.wait(syn, "v", vf)
            p.wait(syn, "s", sf)
            p.wait(syn, "e%d" % b, ef)
            xs = xst_sb[:, b * DM : (b + 1) * DM]
            xns = xnst_sb[:, b * DM : (b + 1) * DM]
            row = i * 128
            p.wait(syn, "d%d" % b)
            dtok = p.inc(syn.dma_start(xs, x[row : row + 128, :]), "d%d" % b, 16)
            p.wait(vec, "d%d" % b, dtok)
            vtok, stok, rstd, negmur = emit_ln_stats(p, lb, b, xs, DM)
            p.wait(vec, "v")
            p.wait(vec, "e%d" % b, ef)
            aptok = p.inc(
                vec.tensor_scalar(xns, xs, rstd, negmur, ALU.mult, ALU.add), "v"
            )
            lb.free_v[b] = aptok
            p.wait(syn, "v", aptok)
            p.wait(syn, "e%d" % b)
            etok = p.inc(syn.dma_start(xnd[row : row + 128, :], xns), "e%d" % b, 16)
            stage_free[b] = (aptok, stok, etok)
            ch = i // (TC // 128)
            era_store_tok[ch] = (p.val["e0"], p.val["e1"])


        ps_rd = [("s", 0)] * 8
        t_z0_prev = [0, 0]
        t_z1_prev = 0
        t_z2_prev = 0
        xtok = [0, 0]
        xsem = ["x", "xs"]
        colfree = [0, 0, 0, 0]
        ctok = [0, 0, 0, 0]
        w2free = [0, 0, 0, 0]
        utok = [0, 0, 0, 0]
        col_q = [syn, gps, syn, gps]
        col_sem = ["w0", "w1", "w2", "w3"]
        w2_q = [syn, gps, syn, gps]
        w2_sem = ["u0", "u1", "u2", "u3"]
        issued = [0]

        NSTR = 8 + NF2

        def stream_src(gidx):
            k = gidx % NSTR
            if k < 8:
                return custom_ap(w0d, k * 512, [[4096, 128], [4096 * 128, 8], [1, 512]]), NE * 512
            k -= 8
            return custom_ap(w1d, k * 128, [[4096, 128], [4096 * 128, 32], [1, 128]]), NF1 * 128

        def ensure_issued(upto):
            while issued[0] <= min(upto, NCH * NSTR - 1):
                gidx = issued[0]
                sl = gidx % 4
                q = col_q[sl]
                srcap, n = stream_src(gidx)
                p.wait(q, "t", colfree[sl])
                p.wait(q, col_sem[sl])
                if gidx < NSTR:
                    ms = milestones["w0_st" if gidx % NSTR < 8 else "k1_st"]
                    for snm, stok_ in zip(st_sem, ms):
                        p.wait(q, snm, stok_)
                ctok[sl] = p.inc(
                    q.dma_start(col_sb[:, sl * NF1 * 128 : sl * NF1 * 128 + n], srcap),
                    col_sem[sl],
                    16,
                )
                issued[0] += 1

        def issue_transposes(c):
            bx = c % 2
            tok0 = c * TC
            p.wait(sca, "t", t_z0_prev[bx])
            e0t, e1t = era_store_tok[c]
            p.wait(sca, "e0", e0t)
            p.wait(sca, "e1", e1t)
            for et in range(NE):
                p.inc(
                    sca.dma_start(
                        xnT_sb[:, bx * NE * TC + et * TC : bx * NE * TC + (et + 1) * TC],
                        xnd[tok0 : tok0 + TC, et * 128 : (et + 1) * 128],
                        transpose=True,
                    ),
                    xsem[bx],
                    16,
                )
            xtok[bx] = p.val[xsem[bx]]

        def issue_w2(gidx):
            sl = gidx % 4
            idx = gidx % NF2
            q = w2_q[sl]
            p.wait(q, "t", w2free[sl])
            if gidx < NF2:
                for snm, stok_ in zip(st_sem, milestones["all_st"]):
                    p.wait(q, snm, stok_)
            utok[sl] = p.inc(
                q.dma_start(w2t_sb[:, sl * DM : (sl + 1) * DM], w2d[idx * 128 : (idx + 1) * 128, :]),
                w2_sem[sl],
                16,
            )

        era_next = [0]

        def emit_some_era(n):
            while n > 0 and era_next[0] < TOK // 128:
                emit_era_tile(era_next[0])
                era_next[0] += 1
                n -= 1

        for ridx, (kind, arg) in enumerate(conv_jobs):
            conv_round(ridx, kind, arg)
            if ridx % 6 == 5:
                emit_some_era(1)
            if kind == "w0" and arg == NE - 1:
                milestones["w0_st"] = list(slot_st)
            if kind == "k1" and arg == NF1 - 1:
                milestones["k1_st"] = list(slot_st)
        milestones["all_st"] = list(slot_st)
        milestones["tt_v"] = p.val["v"]
        milestones["tt_gc"] = p.val["gc"]
        p.wait(syn, "d0")
        p.wait(syn, "d1")
        p.wait(syn, "r0")
        p.wait(syn, "r1")
        emit_some_era(max(0, 8 - era_next[0]))
        issue_transposes(0)
        ensure_issued(1)
        for c in range(NCH):
            bx = c % 2
            tok0 = c * TC
            if c + 1 < NCH:
                issue_transposes(c + 1)

            xbase = bx * NE * TC
            for f1t in range(NF1):
                gidx = c * NSTR + f1t // 4
                sl = gidx % 4
                loc = f1t % 4
                if f1t % 4 == 0:
                    p.pump(1)
                ensure_issued(gidx + 3)
                bank = f1t % 4
                pz = ps[bank]
                nm, tok = ps_rd[bank]
                p.wait(ten, nm, tok)
                if f1t == 0:
                    p.wait(ten, xsem[bx], xtok[bx])
                if loc == 0:
                    p.wait(ten, col_sem[sl], ctok[sl])
                last = None
                for et in range(NE):
                    last = ten.matmul(
                        pz[:],
                        col_sb[:, sl * NF1 * 128 + et * 512 + loc * 128 : sl * NF1 * 128 + et * 512 + loc * 128 + 128],
                        xnT_sb[:, xbase + et * TC : xbase + (et + 1) * TC],
                        start=(et == 0),
                        stop=(et == NE - 1),
                    )
                ttok = p.inc(last, "t")
                if loc == 3:
                    colfree[sl] = ttok
                if f1t == NF1 - 1:
                    t_z0_prev[bx] = ttok
                p.wait(sca, "t", ttok)
                if f1t == 0:
                    p.wait(sca, "t", t_z1_prev)
                if c == 0 and f1t == 0:
                    p.wait(sca, "v", milestones["tt_v"])
                    p.wait(sca, "gc", milestones["tt_gc"])
                ps_rd[bank] = (
                    "s",
                    p.inc(
                        sca.activation(
                            h1_sb[:, f1t * TC : (f1t + 1) * TC], pz[:], AF.Relu,
                            scale=sc_sb[:, 0:1],
                        ),
                        "s",
                    ),
                )
            s_h1_done = p.val["s"]
            emit_some_era(TC // 128)

            p.wait(ten, "s", s_h1_done)
            for f2t in range(NF2):
                gidx = c * NSTR + 8 + f2t
                sl = gidx % 4
                if f2t % 4 == 0:
                    p.pump(1)
                ensure_issued(gidx + 3)
                bank = 4 + f2t % 2
                pz = ps[bank]
                nm, tok = ps_rd[bank]
                p.wait(ten, nm, tok)
                p.wait(ten, col_sem[sl], ctok[sl])
                last = None
                for f1t in range(NF1):
                    last = ten.matmul(
                        pz[:],
                        col_sb[:, sl * NF1 * 128 + f1t * 128 : sl * NF1 * 128 + (f1t + 1) * 128],
                        h1_sb[:, f1t * TC : (f1t + 1) * TC],
                        start=(f1t == 0),
                        stop=(f1t == NF1 - 1),
                    )
                ttok = p.inc(last, "t")
                colfree[sl] = ttok
                if f2t == NF2 - 1:
                    t_z1_prev = ttok
                p.wait(sca, "t", ttok)
                if f2t == 0:
                    p.wait(sca, "t", t_z2_prev)
                if c == 0 and f2t == 0:
                    for snm, stok_ in zip(st_sem, milestones["all_st"]):
                        p.wait(sca, snm, stok_)
                ps_rd[bank] = (
                    "s",
                    p.inc(
                        sca.activation(
                            h2_sb[:, f2t * TC : (f2t + 1) * TC], pz[:], AF.Relu,
                            scale=sc_sb[:, 1:2],
                        ),
                        "s",
                    ),
                )
            s_h2_done = p.val["s"]

            if c == 0:
                for k in range(3):
                    issue_w2(k)
            p.wait(ten, "s", s_h2_done)
            gz_base = c * NF2
            for bank in range(8):
                nm, tok = ps_rd[bank]
                p.wait(ten, nm, tok)
            for f2t in range(NF2):
                g = gz_base + f2t
                sl = g % 4
                if g + 3 < NCH * NF2:
                    issue_w2(g + 3)
                p.wait(ten, w2_sem[sl], utok[sl])
                last = None
                for tt in range(TC // 128):
                    for dh in range(2):
                        last = ten.matmul(
                            ps[tt * 2 + dh][:],
                            h2_sb[:, f2t * TC + tt * 128 : f2t * TC + tt * 128 + 128],
                            w2t_sb[:, sl * DM + dh * 512 : sl * DM + dh * 512 + 512],
                            start=(f2t == 0),
                            stop=(f2t == NF2 - 1),
                        )
                ttok = p.inc(last, "t")
                w2free[sl] = ttok
                if f2t == NF2 - 1:
                    t_z2_prev = ttok
            tz2 = p.val["t"]

            evac_free = [(0, 0)] * 2
            if c > 0:
                evac_free = prev_evac_free
            for tt in range(TC // 128):
                b = tt % 2
                row = tok0 + tt * 128
                vf, sf = evac_free[b]
                p.wait(syn, "v", vf)
                x2s = x2st_sb[:, b * DM : (b + 1) * DM]
                dres = p.inc(syn.dma_start(x2s, x[row : row + 128, :]), "r%d" % b, 16)
                ys = y_sb[:, b * DM : (b + 1) * DM]
                p.wait(sca, "t", tz2)
                p.wait(sca, "v", vf)
                p.wait(sca, "s", sf)
                p.inc(sca.activation(ys[:, 0:512], ps[tt * 2][:], AF.Identity, scale=sc_sb[:, 2:3]), "s")
                p.wait(sca, "s")
                stok0 = p.inc(
                    sca.activation(ys[:, 512:1024], ps[tt * 2 + 1][:], AF.Identity, scale=sc_sb[:, 2:3]),
                    "s",
                )
                ps_rd[tt * 2] = ("s", stok0)
                ps_rd[tt * 2 + 1] = ("s", stok0)
                p.wait(vec, "s", stok0)
                vtok, stok, rstd, negmur = emit_ln_stats(p, lb, 2 + b, ys, DM)
                tmps = tmp_sb[:, b * DM : (b + 1) * DM]
                outs = out_sb[:, b * DM : (b + 1) * DM]
                p.wait(sca, "v", vtok)
                p.wait(sca, "s", stok)
                stok2 = p.inc(
                    sca.activation(tmps, ys, AF.Identity, bias=negmur, scale=rstd), "s"
                )
                lb.free_s[2 + b] = stok2
                prev_ys_rd = (vtok, stok2)
                p.d_prev_ys_rd = prev_ys_rd
                p.wait(vec, "s", stok2)
                p.wait(vec, "r%d" % b, dres)
                p.wait(vec, "e%d" % b)
                vtok2 = p.inc(vec.tensor_tensor(outs, tmps, x2s, ALU.add), "v")
                p.wait(syn, "v", vtok2)
                p.inc(syn.dma_start(y[row : row + 128, :], outs), "e%d" % b, 16)
                evac_free[b] = (vtok2, stok2)
            prev_evac_free = evac_free

        p.wait(syn, "e0")
        p.wait(syn, "e1")
        p.wait(syn, "d")

def emit_l1_attn(nc, ctx, p, x, y):
    W = dram_cached(nc, "W", [32, DM], F32, kind="ExternalInput").ap()
    bb = dram_cached(nc, "bb", [1, 32], F32, kind="ExternalInput").ap()
    ones = dram_cached(nc, "ones", [1, 128], F32, kind="ExternalInput").ap()
    idb = dram_cached(nc, "idb", [128, 128], BF16, kind="ExternalInput").ap()
    wstage = dram_cached(nc, "wstage_c", [32, DM], BF16, kind="Internal").ap()
    xnd = dram_cached(nc, "xnd_c", [SEQ_PER_CORE, SEQ, DM], BF16, kind="Internal").ap()
    WPROWS = SEQ + 2
    wpad_n = ((WPROWS * RPAD + 127) // 128) * 128
    wpad = dram_cached(nc, "wpad", [SEQ_PER_CORE, wpad_n], BF16, kind="Internal").ap()

    with ExitStack() as ctx:
        E = ctx.enter_context
        ten, vec, sca, gps, syn = p.ten, p.vec, p.sca, p.gps, p.syn
        lb = LNBufs(nc, ctx, 4, "c_")
        lb.init(p)

        ones_sb = E(nc.sbuf_tensor("c_" + "ones_sb", [1, 128], F32))
        idb_sb = E(nc.sbuf_tensor("c_" + "idb_sb", [128, 128], BF16))
        brow_sb = E(nc.sbuf_tensor("c_" + "brow_sb", [1, 32], F32))
        wT_sb = E(nc.sbuf_tensor("c_" + "wT_sb", [128, NE * 32], BF16))
        sm_sb = E(nc.sbuf_tensor("c_" + "sm_sb", [128, 8], F32))
        w_sb = E(nc.sbuf_tensor("c_" + "w_sb", [128, NTS * 32], F32))
        zer_sb = E(nc.sbuf_tensor("c_" + "zer_sb", [128, 2576], BF16))
        xst_sb = E(nc.sbuf_tensor("c_" + "xst_sb", [128, 2 * DM], F32))
        x2st_sb = E(nc.sbuf_tensor("c_" + "x2st_sb", [128, 2 * DM], F32))
        tmp_sb = E(nc.sbuf_tensor("c_" + "tmp_sb", [128, 2 * DM], F32))
        out_sb = E(nc.sbuf_tensor("c_" + "out_sb", [128, 2 * DM], F32))
        xn_sb = E(nc.sbuf_tensor("c_" + "xn_sb", [128, 2 * NTS * DM], BF16))
        xnT_sb = E(nc.sbuf_tensor("c_" + "xnT_sb", [128, NE * SEQ], BF16))
        PT_sb = E(nc.sbuf_tensor("c_" + "PT_sb", [128, 4 * 256], BF16))
        P_sb = E(nc.sbuf_tensor("c_" + "P_sb", [128, 2 * 256], BF16))

        ps_l = E(nc.psum_tensor("c_" + "ps_l", [128, 512], F32))
        psD = [E(nc.psum_tensor("c_" + f"psD{i}", [128, 1024], BF16)) for i in range(2)]
        ps_o = [E(nc.psum_tensor("c_" + f"ps_oc{i}", [128, DM], F32)) for i in range(2)]


        p.inc(syn.dma_start(ones_sb[:], ones), "d", 16)
        p.inc(syn.dma_start(idb_sb[:], idb), "d", 16)
        p.inc(syn.dma_start(brow_sb[:], bb), "d", 16)
        p.inc(gps.dma_start(wstage, W), "g", 16)
        p.wait(syn, "g")
        for et in range(NE):
            p.inc(
                syn.dma_start(
                    wT_sb[:, et * 32 : (et + 1) * 32],
                    wstage[0:32, et * 128 : (et + 1) * 128],
                    transpose=True,
                ),
                "x",
                16,
            )
        p.inc(vec.memset(zer_sb[:], 0.0), "v")
        p.inc(vec.memset(PT_sb[:], 0.0), "v")
        p.wait(syn, "v")
        for s in range(SEQ_PER_CORE):
            p.inc(
                syn.dma_start(
                    wpad[s].rearrange("(a b) -> a b", a=128), zer_sb[:, 0 : wpad_n // 128]
                ),
                "d",
                16,
            )
        p.barrier()

        c1_stage = [(0, 0) for _ in range(2)]
        c1_tok = {}

        def emit_c1_tile(s_, i_):
            b = i_ % 2
            vf, sf = c1_stage[b]
            p.wait(syn, "v", vf)
            p.wait(syn, "s", sf)
            xs = xst_sb[:, b * DM : (b + 1) * DM]
            p.wait(syn, "d%d" % b)
            dtok = p.inc(
                syn.dma_start(xs, x[s_ * SEQ + i_ * 128 : s_ * SEQ + (i_ + 1) * 128, :]),
                "d%d" % b, 16,
            )
            p.wait(vec, "d%d" % b, dtok)
            vtok, stok, rstd, negmur = emit_ln_stats(p, lb, b, xs, DM)
            xns = xn_sb[:, (s_ * NTS + i_) * DM : (s_ * NTS + i_ + 1) * DM]
            p.wait(vec, "v")
            aptok = p.inc(
                vec.tensor_scalar(xns, xs, rstd, negmur, ALU.mult, ALU.add), "v"
            )
            lb.free_v[b] = aptok
            p.wait(syn, "v", aptok)
            p.inc(syn.dma_start(xnd[s_, i_ * 128 : (i_ + 1) * 128, :], xns), "xs", 16)
            c1_stage[b] = (aptok, stok)
            c1_tok[s_] = p.val["xs"]

        for i_ in range(NTS):
            emit_c1_tile(0, i_)
        p.wait(syn, "xs")

        for s in range(SEQ_PER_CORE):
            xseq = x[s * SEQ : (s + 1) * SEQ, :]
            yseq = y[s * SEQ : (s + 1) * SEQ, :]

            p.wait(syn, "xs", c1_tok[s])
            for et in range(NE):
                p.inc(
                    syn.dma_start(
                        xnT_sb[:, et * SEQ : (et + 1) * SEQ],
                        xnd[s, :, et * 128 : (et + 1) * 128],
                        transpose=True,
                    ),
                    "x",
                    16,
                )
            p.barrier()

            prev_exp_s = 0
            psl_free = 0
            for tt in range(NTS):
                p.wait(ten, "s", psl_free)
                last = None
                for et in range(NE):
                    last = ten.matmul(
                        ps_l[:, 0:32],
                        xnT_sb[:, et * SEQ + tt * 128 : et * SEQ + tt * 128 + 128],
                        wT_sb[:, et * 32 : (et + 1) * 32],
                        start=(et == 0),
                        stop=False,
                    )
                last = ten.matmul(
                    ps_l[:, 0:32], ones_sb[:], brow_sb[:], start=False, stop=True
                )
                ttok = p.inc(last, "t")
                p.wait(vec, "t", ttok)
                p.wait(vec, "s", prev_exp_s)
                p.inc(vec.reduce_max(sm_sb[:, 0:1], ps_l[:, 0:32], axis=AX.X, negate=True), "v")
                p.wait(sca, "v")
                prev_exp_s = p.inc(
                    sca.activation(
                        w_sb[:, tt * 32 : (tt + 1) * 32],
                        ps_l[:, 0:32],
                        AF.Exp,
                        bias=sm_sb[:, 0:1],
                        accum_out=sm_sb[:, 1:2],
                    ),
                    "s",
                )
                psl_free = prev_exp_s
                p.wait(vec, "s", prev_exp_s)
                p.inc(vec.reciprocal(sm_sb[:, 2:3], sm_sb[:, 1:2]), "v")
                p.wait(vec, "v")
                wtok = p.inc(
                    vec.tensor_scalar(
                        w_sb[:, tt * 32 : (tt + 1) * 32],
                        w_sb[:, tt * 32 : (tt + 1) * 32],
                        sm_sb[:, 2:3],
                        None,
                        ALU.mult,
                    ),
                    "v",
                )
                p.wait(gps, "v", wtok)
                dst = custom_ap(
                    wpad[s], (1 + tt * 128) * RPAD, [[RPAD, 128], [1, 32]]
                )
                p.inc(gps.dma_start(dst, w_sb[:, tt * 32 : (tt + 1) * 32]), "wp", 16)
            p.barrier()

            ofree = [(0, 0) for _ in range(2)]
            ptfree = [0, 0, 0, 0]
            pfree = [0, 0]
            psd_free = [0, 0]
            ktok = [0, 0, 0, 0]

            def issue_skew(lt):
                sl = lt % 4
                p.wait(syn, "t", ptfree[sl])
                srcap = custom_ap(
                    wpad[s], (1 + lt * 128) * RPAD, [[RPAD - 1, 128], [1, 159]]
                )
                ktok[sl] = p.inc(
                    syn.dma_start(PT_sb[:, sl * 256 : sl * 256 + 159], srcap), "k%d" % sl, 16
                )

            issue_skew(0)
            issue_skew(1)
            for lt in range(NTS):
                if s == 0:
                    emit_c1_tile(1, lt)
                p.pump(1)
                b = lt % 2
                sl = lt % 4
                if lt + 2 < NTS:
                    issue_skew(lt + 2)
                p.wait(ten, "k%d" % sl, ktok[sl])
                p.wait(ten, "s", psd_free[b])
                ten.transpose(
                    psD[b][:, 0:128], PT_sb[:, sl * 256 : sl * 256 + 128], idb_sb[:]
                )
                ttr = p.inc(
                    ten.transpose(
                        psD[b][:, 128:256], PT_sb[:, sl * 256 + 128 : sl * 256 + 256], idb_sb[:]
                    ),
                    "t",
                )
                ptfree[sl] = ttr
                p.wait(vec, "t", ttr)
                p.wait(vec, "t", pfree[b])
                p.wait(sca, "t", ttr)
                p.wait(sca, "t", pfree[b])
                ptok = p.inc(
                    sca.activation(P_sb[:, b * 256 : (b + 1) * 256], psD[b][:, 0:256], AF.Identity), "s"
                )
                psd_free[b] = ptok
                x2s = x2st_sb[:, b * DM : (b + 1) * DM]
                p.wait(syn, "gc", ofree[b][0])
                dres = p.inc(
                    syn.dma_start(x2s, xseq[lt * 128 : (lt + 1) * 128, :]), "r%d" % b, 16
                )
                vf, sf = ofree[b]
                p.wait(ten, "s", max(ptok, sf))
                p.wait(ten, "gc", vf)
                pso = ps_o[b]
                tmix = None
                for eb in range(2):
                    nsup = 2 if lt + 1 < NTS else 1
                    for kk in range(nsup):
                        lhs = P_sb[:, b * 256 + kk * 128 : b * 256 + kk * 128 + 128]
                        rhs = xn_sb[:, (s * NTS + lt + kk) * DM + eb * 512 : (s * NTS + lt + kk) * DM + eb * 512 + 512]
                        tmix = ten.matmul(
                            pso[:, eb * 512 : eb * 512 + 512], lhs, rhs,
                            start=(kk == 0), stop=(kk == nsup - 1),
                        )
                tmixtok = p.inc(tmix, "t")
                pfree[b] = tmixtok
                p.wait(vec, "t", tmixtok)
                vtok, stok, rstd, negmur = emit_ln_stats(p, lb, 2 + b, pso, DM)
                tmps = tmp_sb[:, b * DM : (b + 1) * DM]
                outs = out_sb[:, b * DM : (b + 1) * DM]
                p.wait(sca, "v", vtok)
                p.wait(sca, "s", stok)
                stok2 = p.inc(
                    sca.activation(tmps, pso[:], AF.Identity, bias=negmur, scale=rstd), "s"
                )
                lb.free_s[2 + b] = stok2
                p.wait(gps, "s", stok2)
                p.wait(gps, "r%d" % b, dres)
                p.wait(gps, "e%d" % b)
                vtok2 = p.inc(gps.tensor_tensor(outs, tmps, x2s, ALU.add), "gc")
                p.wait(syn, "gc", vtok2)
                p.inc(syn.dma_start(yseq[lt * 128 : (lt + 1) * 128, :], outs), "e%d" % b, 16)
                ofree[b] = (vtok2, stok2)
            p.barrier()

        p.wait(syn, "e0")
        p.wait(syn, "e1")



def emit_l1_ffn(nc, ctx, p, x, y):
    gw = dram_cached(nc, "gw", [NEXP, DM], F32, kind="ExternalInput").ap()
    gb = dram_cached(nc, "gb", [1, NEXP], F32, kind="ExternalInput").ap()
    rmsw = dram_cached(nc, "rmsw", [1, NEXP], F32, kind="ExternalInput").ap()
    w1 = dram_cached(nc, "w1", [NEXP, DM, 2048], F32, kind="ExternalInput").ap()
    b1 = dram_cached(nc, "b1", [NEXP * 2048], F32, kind="ExternalInput").ap()
    w2 = dram_cached(nc, "w2", [NEXP * 2048, DM], F32, kind="ExternalInput").ap()
    b2 = dram_cached(nc, "b2", [NEXP, DM], F32, kind="ExternalInput").ap()
    ones = dram_cached(nc, "ones", [1, 128], F32, kind="ExternalInput").ap()
    onescol = dram_cached(nc, "onescol", [128, 1], F32, kind="ExternalInput").ap()
    idf = dram_cached(nc, "idf", [128, 128], F32, kind="ExternalInput").ap()
    w1d = dram_cached(nc, "w1dd", [NEXP * NH * 128, DM], BF16, kind="Internal").ap()
    w2d = dram_cached(nc, "w2dd", [NEXP * 2048, DM], BF16, kind="Internal").ap()
    xnd = dram_cached(nc, "xnd_d", [TOK, DM], BF16, kind="Internal").ap()

    with ExitStack() as ctx:
        E = ctx.enter_context
        ten, vec, sca, gps, syn = p.ten, p.vec, p.sca, p.gps, p.syn
        lb = LNBufs(nc, ctx, 4, "d_")
        lb.init(p)

        ones_sb = E(nc.sbuf_tensor("d_ones_sb", [1, 128], F32))
        onescol_sb = E(nc.sbuf_tensor("d_onescol_sb", [128, 1], F32))
        idf_sb = E(nc.sbuf_tensor("d_idf_sb", [128, 128], F32))
        gw2_sb = E(nc.sbuf_tensor("d_gw2_sb", [128, NE * 16], BF16))
        gb2row_sb = E(nc.sbuf_tensor("d_gb2row_sb", [1, 16], F32))
        eps_sb = E(nc.sbuf_tensor("d_eps_sb", [128, 1], F32))
        gbrow_sb = E(nc.sbuf_tensor("d_gbrow_sb", [1, NEXP], F32))
        rmsrow_sb = E(nc.sbuf_tensor("d_rmsrow_sb", [1, NEXP], F32))
        rmsbc_sb = E(nc.sbuf_tensor("d_rmsbc_sb", [128, NEXP], F32))
        b1raw_sb = E(nc.sbuf_tensor("d_b1raw_sb", [128, 128], F32))
        b1col_sb = E(nc.sbuf_tensor("d_b1col_sb", [128, 128], F32))
        B2bc_sb = E(nc.sbuf_tensor("d_B2bc_sb", [128, DM], F32))
        gb_sb = E(nc.sbuf_tensor("d_gb_sb", [128, 2 * 16], F32))
        gb2_sb = E(nc.sbuf_tensor("d_gb2_sb", [128, 2 * 16], F32))
        ng_sb = E(nc.sbuf_tensor("d_ng_sb", [128, 32], F32))
        xst_sb = E(nc.sbuf_tensor("d_xst_sb", [128, 2 * DM], F32))
        xnst_sb = E(nc.sbuf_tensor("d_xnst_sb", [128, DM], BF16))
        x2st_sb = E(nc.sbuf_tensor("d_x2st_sb", [128, 2 * DM], F32))
        tmp_sb = E(nc.sbuf_tensor("d_tmp_sb", [128, 2 * DM], F32))
        out_sb = E(nc.sbuf_tensor("d_out_sb", [128, 2 * DM], F32))
        y_sb = E(nc.sbuf_tensor("d_y_sb", [128, DM], F32))
        xnT_sb = E(nc.sbuf_tensor("d_xnT_sb", [128, 2 * NE * TC2], BF16))
        col_sb = E(nc.sbuf_tensor("d_col_sb", [128, 4 * NE * 128], BF16))
        w2t_sb = E(nc.sbuf_tensor("d_w2t_sb", [128, 4 * DM], BF16))
        hall_sb = E(nc.sbuf_tensor("d_hall_sb", [128, NK2 * TC2], BF16))

        ps8 = [E(nc.psum_tensor(f"d_ps8_{i}", [128, 512], F32)) for i in range(8)]
        b2_sb = tmp_sb[0:8, 0:DM]
        b2row_sb = y_sb[0:1, 0:DM]
        graw_sb = xst_sb[0:8, 0:DM]


        p.inc(syn.dma_start(ones_sb[:], ones), "d", 16)
        p.inc(syn.dma_start(onescol_sb[:], onescol), "d", 16)
        p.inc(syn.dma_start(idf_sb[:], idf), "d", 16)
        p.inc(syn.dma_start(gbrow_sb[:], gb), "d", 16)
        p.inc(syn.dma_start(rmsrow_sb[:], rmsw), "d", 16)
        p.inc(syn.dma_start(b1raw_sb[:], b1.rearrange("(a b) -> a b", a=128)), "d", 16)
        p.inc(syn.dma_start(b2_sb, b2), "d", 16)
        p.inc(syn.dma_start(graw_sb, gw), "d", 16)
        p.pump(10**9)
        p.wait(ten, "d")
        p.inc(ten.matmul(ps8[3][:, 0:8], ones_sb[:], rmsrow_sb[:], start=True, stop=True), "t")
        p.wait(sca, "t")
        p.inc(sca.activation(rmsbc_sb[:], ps8[3][:, 0:8], AF.Identity), "s")
        p.inc(vec.memset(eps_sb[:], EPS32), "v")
        p.wait(vec, "d")
        p.inc(vec.tensor_copy(gb2row_sb[0:1, 0:8], gbrow_sb[0:1, 0:8]), "v")
        p.inc(vec.tensor_mul(gb2row_sb[0:1, 8:16], gbrow_sb[0:1, 0:8], rmsrow_sb[0:1, 0:8]), "v")
        for et in range(NE):
            p.inc(
                ten.transpose(
                    ps8[4][:, 0:8], graw_sb[0:8, et * 128 : (et + 1) * 128], idf_sb[0:8, 0:8]
                ),
                "t",
            )
            p.wait(sca, "t")
            p.wait(sca, "s")
            stok_g = p.inc(
                sca.activation(gw2_sb[:, et * 16 : et * 16 + 8], ps8[4][:, 0:8], AF.Identity),
                "s",
            )
            p.wait(vec, "s", stok_g)
            p.inc(
                vec.tensor_tensor(
                    gw2_sb[:, et * 16 + 8 : et * 16 + 16],
                    gw2_sb[:, et * 16 : et * 16 + 8],
                    rmsbc_sb[:],
                    ALU.mult,
                ),
                "v",
            )
            p.wait(ten, "s", stok_g)
        p.inc(ten.transpose(ps8[0][:, 0:128], b1raw_sb[:], idf_sb[:]), "t")
        p.wait(sca, "t")
        p.inc(sca.activation(b1col_sb[:], ps8[0][:, 0:128], AF.Identity), "s")
        p.wait(ten, "s")
        for dh in range(2):
            p.inc(
                ten.matmul(ps8[1][0:1, 0:512], onescol_sb[0:8, :], b2_sb[0:8, dh * 512 : dh * 512 + 512], start=True, stop=True),
                "t",
            )
            p.wait(sca, "t")
            p.inc(sca.activation(B2bc_sb[0:1, dh * 512 : dh * 512 + 512], ps8[1][0:1, 0:512], AF.Identity), "s")
            p.wait(ten, "s")
        p.barrier()

        stage_free = [(0, 0, 0)] * 2
        era_store_tok = {}
        era_next = [0]

        def emit_era_tile(i):
            b = i % 2
            vf, sf, ef = stage_free[b]
            p.wait(syn, "v", vf)
            p.wait(syn, "s", sf)
            p.wait(syn, "e%d" % b, ef)
            xs = xst_sb[:, b * DM : (b + 1) * DM]
            xns = xnst_sb[:, 0:DM]
            row = i * 128
            p.wait(syn, "d%d" % b)
            dtok = p.inc(syn.dma_start(xs, x[row : row + 128, :]), "d%d" % b, 16)
            p.wait(vec, "d%d" % b, dtok)
            vtok, stok, rstd, negmur = emit_ln_stats(p, lb, b, xs, DM)
            p.wait(vec, "v")
            p.wait(vec, "e0")
            p.wait(vec, "e1")
            aptok = p.inc(
                vec.tensor_scalar(xns, xs, rstd, negmur, ALU.mult, ALU.add), "v"
            )
            lb.free_v[b] = aptok
            p.wait(syn, "v", aptok)
            p.wait(syn, "e%d" % b)
            etok = p.inc(syn.dma_start(xnd[row : row + 128, :], xns), "e%d" % b, 16)
            stage_free[b] = (aptok, stok, etok)
            era_store_tok[i // (TC2 // 128)] = (p.val["e0"], p.val["e1"])

        def emit_some_era(n):
            while n > 0 and era_next[0] < TOK // 128:
                emit_era_tile(era_next[0])
                era_next[0] += 1
                n -= 1

        emit_some_era(2 * (TC2 // 128))

        ps_rd = [("s", 0)] * 8
        t_xnT_prev = [0, 0]
        t_mm2_prev = 0
        xtok = [0, 0]
        xsem = ["x", "xs"]
        colfree = [0] * 4
        ctok = [0] * 4
        w2free = [0] * 4
        utok = [0] * 4
        col_q = [syn, gps, syn, gps]
        col_sem = ["c0", "c1", "c2", "c3"]
        w2_q = [syn, gps, syn, gps]
        w2_sem = ["u0", "u1", "u2", "u3"]

        def issue_transposes(c):
            bx = c % 2
            tok0 = c * TC2
            p.wait(syn, "t", t_xnT_prev[bx])
            e0t, e1t = era_store_tok[c]
            p.wait(syn, "e0", e0t)
            p.wait(syn, "e1", e1t)
            for et in range(NE):
                p.inc(
                    syn.dma_start(
                        xnT_sb[:, bx * NE * TC2 + et * TC2 : bx * NE * TC2 + (et + 1) * TC2],
                        xnd[tok0 : tok0 + TC2, et * 128 : (et + 1) * 128],
                        transpose=True,
                    ),
                    xsem[bx],
                    16,
                )
            xtok[bx] = p.val[xsem[bx]]

        def issue_col(gidx):
            sl = gidx % 4
            kk = gidx % NK2
            q = col_q[sl]
            p.wait(q, "t", colfree[sl])
            ctok[sl] = p.inc(
                q.dma_start(
                    col_sb[:, sl * 1024 : (sl + 1) * 1024],
                    w1d[kk * 128 : (kk + 1) * 128, :],
                ),
                col_sem[sl],
                16,
            )

        def issue_w2(gidx):
            sl = gidx % 4
            kk = gidx % NK2
            q = w2_q[sl]
            p.wait(q, "t", w2free[sl])
            utok[sl] = p.inc(
                q.dma_start(w2t_sb[:, sl * DM : (sl + 1) * DM], w2d[kk * 128 : (kk + 1) * 128, :]),
                w2_sem[sl],
                16,
            )

        p.wait(syn, "dc")
        p.wait(gps, "dc")
        p.wait(sca, "dc")
        gate_slot_free = [(0, 0)] * 2
        issue_transposes(0)
        for k in range(3):
            issue_col(k)
        for k in range(3):
            issue_w2(k)
        prev_evac_free = [(0, 0)] * 2
        for c in range(NCH2):
            bx = c % 2
            xbase = bx * NE * TC2
            tok0 = c * TC2
            if c + 1 < NCH2:
                issue_transposes(c + 1)

            gslot_free = gate_slot_free

            def emit_gate_tile(i):
                gbi = i % 2
                gbank = 4 + i
                lg16 = gb_sb[:, gbi * 16 : gbi * 16 + 16]
                u2 = gb2_sb[:, gbi * 16 : gbi * 16 + 8]
                msq = gb2_sb[:, gbi * 16 + 8 : gbi * 16 + 9]
                negm2 = gb2_sb[:, gbi * 16 + 9 : gbi * 16 + 10]
                ssum = gb2_sb[:, gbi * 16 + 10 : gbi * 16 + 11]
                t2 = gb2_sb[:, gbi * 16 + 11 : gbi * 16 + 12]
                rr = gb2_sb[:, gbi * 16 + 12 : gbi * 16 + 13]
                nm, tok = ps_rd[gbank]
                p.wait(ten, nm, tok)
                if i == 0:
                    p.wait(ten, xsem[bx], xtok[bx])
                last = None
                for et in range(NE):
                    last = ten.matmul(
                        ps8[gbank][:, 0:16],
                        xnT_sb[:, xbase + et * TC2 + i * 128 : xbase + et * TC2 + i * 128 + 128],
                        gw2_sb[:, et * 16 : (et + 1) * 16],
                        start=(et == 0),
                        stop=False,
                    )
                last = ten.matmul(ps8[gbank][:, 0:16], ones_sb[:], gb2row_sb[:], start=False, stop=True)
                ttok = p.inc(last, "t")
                vfree, sfree = gslot_free[gbi]
                p.wait(sca, "t", ttok)
                p.wait(sca, "v", vfree)
                p.wait(sca, "s", sfree)
                p.inc(sca.activation(lg16, ps8[gbank][:, 0:16], AF.Identity), "s")
                p.wait(sca, "s")
                stok_sq = p.inc(
                    sca.activation(u2, lg16[:, 0:8], AF.Square, accum_out=msq), "s"
                )
                ps_rd[gbank] = ("s", stok_sq - 1)
                p.wait(sca, "s")
                p.inc(sca.activation(rr, msq, AF.Ln, scale=1.0 / 8.0, bias=eps_sb[:]), "s")
                p.wait(sca, "s")
                stok_r = p.inc(sca.activation(rr, rr, AF.Exp, scale=-0.5), "s")
                p.wait(vec, "s", stok_r)
                p.inc(vec.tensor_scalar(u2, lg16[:, 8:16], rr, 2.0, ALU.mult, ALU.mult), "v")
                p.wait(vec, "v")
                vtok_m = p.inc(vec.reduce_max(negm2, u2, axis=AX.X, negate=True), "v")
                p.wait(sca, "v", vtok_m)
                stok_e = p.inc(
                    sca.activation(u2, u2, AF.Exp, bias=negm2, accum_out=ssum), "s"
                )
                p.wait(vec, "s", stok_e)
                p.inc(vec.tensor_scalar(t2, ssum, 1e-6, 1.0, ALU.mult, ALU.add), "v")
                p.wait(vec, "v")
                vtok_n = p.inc(vec.reciprocal(ng_sb[:, (c % 2) * 4 + i : (c % 2) * 4 + i + 1], t2), "v")
                gslot_free[gbi] = (vtok_n, stok_e)

            for kk in range(NK2):
                if kk in (12, 24, 36, 48):
                    emit_gate_tile(kk // 12 - 1)
                gidx = c * NK2 + kk
                sl = gidx % 4
                if gidx + 3 < NCH2 * NK2:
                    issue_col(gidx + 3)
                bank = kk % 4
                nm, tok = ps_rd[bank]
                p.wait(ten, nm, tok)
                p.wait(ten, col_sem[sl], ctok[sl])
                if kk == 0:
                    p.wait(ten, xsem[bx], xtok[bx])
                last = None
                for et in range(NE):
                    last = ten.matmul(
                        ps8[bank][:],
                        col_sb[:, sl * 1024 + et * 128 : sl * 1024 + et * 128 + 128],
                        xnT_sb[:, xbase + et * TC2 : xbase + (et + 1) * TC2],
                        start=(et == 0),
                        stop=(et == NE - 1),
                    )
                ttok = p.inc(last, "t")
                colfree[sl] = ttok
                if kk == NK2 - 1:
                    t_xnT_prev[bx] = ttok
                p.wait(sca, "t", ttok)
                if kk == 0:
                    p.wait(sca, "t", t_mm2_prev)
                ps_rd[bank] = (
                    "s",
                    p.inc(
                        sca.activation(
                            hall_sb[:, kk * TC2 : (kk + 1) * TC2],
                            ps8[bank][:],
                            AF.Gelu,
                            bias=b1col_sb[:, kk : kk + 1],
                        ),
                        "s",
                    ),
                )
            s_hall_done = p.val["s"]
            emit_some_era(TC2 // 128)

            p.wait(ten, "s", s_hall_done)
            for bank in range(8):
                nm, tok = ps_rd[bank]
                p.wait(ten, nm, tok)
            for kk in range(NK2):
                gidx = c * NK2 + kk
                sl = gidx % 4
                if gidx + 3 < NCH2 * NK2:
                    issue_w2(gidx + 3)
                p.wait(ten, w2_sem[sl], utok[sl])
                last = None
                for tt in range(TC2 // 128):
                    for dh in range(2):
                        last = ten.matmul(
                            ps8[tt * 2 + dh][:],
                            hall_sb[:, kk * TC2 + tt * 128 : kk * TC2 + tt * 128 + 128],
                            w2t_sb[:, sl * DM + dh * 512 : sl * DM + dh * 512 + 512],
                            start=(kk == 0),
                            stop=False,
                        )
                ttok = p.inc(last, "t")
                w2free[sl] = ttok
                if kk == NK2 - 1:
                    t_mm2_prev = ttok
            last = None
            for tt in range(TC2 // 128):
                for dh in range(2):
                    last = ten.matmul(
                        ps8[tt * 2 + dh][:],
                        ones_sb[:],
                        B2bc_sb[0:1, dh * 512 : dh * 512 + 512],
                        start=False,
                        stop=True,
                    )
            p.inc(last, "t")
            tz2 = p.val["t"]

            evac_free = prev_evac_free
            prev_ys_rd = getattr(p, "d_prev_ys_rd", (0, 0))
            for tt in range(TC2 // 128):
                b = tt % 2
                gi = (c % 2) * 4 + tt
                row = tok0 + tt * 128
                vf, sf = evac_free[b]
                p.wait(syn, "v", vf)
                x2s = x2st_sb[:, b * DM : (b + 1) * DM]
                dres = p.inc(syn.dma_start(x2s, x[row : row + 128, :]), "r%d" % b, 16)
                ys = y_sb[:, 0:DM]
                p.wait(sca, "t", tz2)
                p.wait(sca, "v", prev_ys_rd[0])
                p.wait(sca, "s", prev_ys_rd[1])
                p.wait(sca, "v", vf)
                p.wait(sca, "s", sf)
                stk = 0
                for dh in range(2):
                    stk = p.inc(
                        sca.activation(
                            ys[:, dh * 512 : dh * 512 + 512],
                            ps8[tt * 2 + dh][:],
                            AF.Identity,
                            scale=ng_sb[:, gi : gi + 1],
                        ),
                        "s",
                    )
                    ps_rd[tt * 2 + dh] = ("s", stk)
                p.wait(vec, "s", stk)
                vtok, stok, rstd, negmur = emit_ln_stats(p, lb, 2 + b, ys, DM)
                tmps = tmp_sb[:, b * DM : (b + 1) * DM]
                outs = out_sb[:, b * DM : (b + 1) * DM]
                p.wait(sca, "v", vtok)
                p.wait(sca, "s", stok)
                stok2 = p.inc(
                    sca.activation(tmps, ys, AF.Identity, bias=negmur, scale=rstd), "s"
                )
                lb.free_s[2 + b] = stok2
                prev_ys_rd = (vtok, stok2)
                p.d_prev_ys_rd = prev_ys_rd
                p.wait(vec, "s", stok2)
                p.wait(vec, "r%d" % b, dres)
                p.wait(vec, "e%d" % b)
                vtok2 = p.inc(vec.tensor_tensor(outs, tmps, x2s, ALU.add), "v")
                p.wait(syn, "v", vtok2)
                p.inc(syn.dma_start(y[row : row + 128, :], outs), "e%d" % b, 16)
                evac_free[b] = (vtok2, stok2)
            prev_evac_free = evac_free

        p.wait(syn, "e0")
        p.wait(syn, "e1")

def build_fused():
    nc = bass.Bass("TRN2", target_bir_lowering=False)
    xin = nc.dram_tensor("xin", [TOK, DM], F32, kind="ExternalInput").ap()
    yout = nc.dram_tensor("yout", [TOK, DM], F32, kind="ExternalOutput").ap()
    x1d = nc.dram_tensor("x1d", [TOK, DM], F32, kind="Internal").ap()
    x2d = nc.dram_tensor("x2d", [TOK, DM], F32, kind="Internal").ap()
    x3d = nc.dram_tensor("x3d", [TOK, DM], F32, kind="Internal").ap()
    with ExitStack() as octx:
        p = Prog(nc, octx, extra=("w0", "w1", "w2", "w3", "u0", "u1", "u2", "u3",
                                  "g0", "g1", "k0", "k1", "k2", "k3", "xs", "wp",
                                  "c0", "c1", "c2", "c3", "c4", "c5", "dc", "gc"))
        blk = octx.enter_context(nc.Block())
        Wq_in = dram_cached(nc, "Wq", [DM, DM], F32, kind="ExternalInput").ap()
        Wk_in = dram_cached(nc, "Wk", [DM, DM], F32, kind="ExternalInput").ap()
        Wv_in = dram_cached(nc, "Wv", [DM, DM], F32, kind="ExternalInput").ap()
        wstage = dram_cached(nc, "wstage", [3, DM, DM], BF16, kind="Internal").ap()
        for m, W in enumerate((Wq_in, Wk_in, Wv_in)):
            p.inc(p.gps.dma_start(wstage[m], W), "g", 16)
        w1_in = dram_cached(nc, "w1", [NEXP, DM, 2048], F32, kind="ExternalInput").ap()
        w2_in = dram_cached(nc, "w2", [NEXP * 2048, DM], F32, kind="ExternalInput").ap()
        w1dd = dram_cached(nc, "w1dd", [NEXP * NH * 128, DM], BF16, kind="Internal").ap()
        w2dd = dram_cached(nc, "w2dd", [NEXP * 2048, DM], BF16, kind="Internal").ap()
        pieces = []
        for e in range(NEXP):
            for ht in range(NH):
                dstap = custom_ap(
                    w1dd,
                    (e * NH + ht) * 128 * DM,
                    [[DM, 128], [128, NE], [1, 128]],
                )
                srcap = custom_ap(
                    w1_in,
                    e * DM * 2048 + ht * 128,
                    [[2048, 128], [128 * 2048, NE], [1, 128]],
                )
                pieces.append((dstap, srcap))
        for r in range(0, NEXP * 2048, 512):
            pieces.append((w2dd[r : r + 512, :], w2_in[r : r + 512, :]))
        p.cast_pieces = pieces
        p.cast_idx = 0

        def pump(n=1):
            while n > 0 and p.cast_idx < len(p.cast_pieces):
                dst, srcp = p.cast_pieces[p.cast_idx]
                p.inc(p.gps.dma_start(dst, srcp), "dc", 16)
                p.cast_idx += 1
                n -= 1

        p.pump = pump
        emit_l0_attn(nc, octx, p, xin, x1d)
        p.barrier()
        emit_l0_ffn(nc, octx, p, x1d, x2d)
        p.barrier()
        emit_l1_attn(nc, octx, p, x2d, x3d)
        p.barrier()
        emit_l1_ffn(nc, octx, p, x3d, yout)
        p.barrier()
    return nc


def kernel(
    L0_attn_X,
    L0_attn_Wq,
    L0_attn_bq,
    L0_attn_Wk,
    L0_attn_bk,
    L0_attn_Wv,
    L0_attn_bv,
    L0_ffn_k0,
    L0_ffn_k1,
    L0_ffn_k2,
    L0_ffn_s0,
    L0_ffn_s1,
    L0_ffn_s2,
    L1_attn_W,
    L1_attn_b,
    L1_ffn_gate_w,
    L1_ffn_gate_b,
    L1_ffn_rms_w,
    L1_ffn_w1,
    L1_ffn_b1,
    L1_ffn_w2,
    L1_ffn_b2,
):
    X = np.asarray(L0_attn_X, dtype=np.float32)
    ones, idf, cmask = host_consts()
    idb = np.eye(128, dtype=np.float32).astype(ml_dtypes.bfloat16)
    onescol = np.ones((128, 1), dtype=np.float32)
    nc = build_fused()
    in_maps = []
    for c in range(NCORES):
        xs = np.ascontiguousarray(
            X[c * SEQ_PER_CORE : (c + 1) * SEQ_PER_CORE].reshape(TOK, DM)
        )
        in_maps.append(
            {
                "xin": xs,
                "Wq": np.asarray(L0_attn_Wq), "Wk": np.asarray(L0_attn_Wk),
                "Wv": np.asarray(L0_attn_Wv),
                "bq": np.asarray(L0_attn_bq), "bk": np.asarray(L0_attn_bk),
                "bv": np.asarray(L0_attn_bv),
                "ones": ones, "idf": idf, "cmask": cmask, "idb": idb,
                "onescol": onescol,
                "k0": np.asarray(L0_ffn_k0).view(np.uint8),
                "k1": np.asarray(L0_ffn_k1).view(np.uint8),
                "k2": np.asarray(L0_ffn_k2).view(np.uint8),
                "s0": np.asarray(L0_ffn_s0, dtype=np.float32).reshape(1, 1),
                "s1": np.asarray(L0_ffn_s1, dtype=np.float32).reshape(1, 1),
                "s2": np.asarray(L0_ffn_s2, dtype=np.float32).reshape(1, 1),
                "W": np.asarray(L1_attn_W),
                "bb": np.asarray(L1_attn_b).reshape(1, 32),
                "gw": np.asarray(L1_ffn_gate_w),
                "gb": np.asarray(L1_ffn_gate_b).reshape(1, 8),
                "rmsw": np.asarray(L1_ffn_rms_w).reshape(1, 8),
                "w1": np.asarray(L1_ffn_w1),
                "b1": np.asarray(L1_ffn_b1).reshape(-1),
                "w2": np.asarray(L1_ffn_w2).reshape(-1, DM),
                "b2": np.asarray(L1_ffn_b2),
            }
        )
    res = run_bass_kernel_spmd(nc, in_maps, core_ids=list(range(NCORES)))
    out = np.concatenate(
        [r["yout"].reshape(SEQ_PER_CORE, SEQ, DM) for r in res.results], axis=0
    )
    return np.ascontiguousarray(out.astype(np.float32))


# version 21
# speedup vs baseline: 1.1347x; 1.0220x over previous
import numpy as np
import ml_dtypes
import bass_rust
import concourse.bass as bass
import concourse.mybir as mybir
from concourse.bass_utils import run_bass_kernel_spmd
from contextlib import ExitStack

F32 = mybir.dt.float32
BF16 = mybir.dt.bfloat16
U8 = mybir.dt.uint8
AF = mybir.ActivationFunctionType
ALU = mybir.AluOpType
AX = mybir.AxisListType

NCORES = 8
SEQ = 2048
DM = 1024
SEQ_PER_CORE = 2
TOK = SEQ * SEQ_PER_CORE
NTS = SEQ // 128
NE = DM // 128
EPS = 1e-5
NEG = -1.0e30


def custom_ap(base_ap, offset_elems, pairs):
    a = base_ap.copy()
    a.offset = offset_elems
    a.ap = bass_rust.VecI64Pair([list(x) for x in pairs])
    return a


class Prog:

    def __init__(self, nc, ctx, extra=()):
        self.nc = nc
        self.ctx = ctx
        self.ten = nc.tensor
        self.vec = nc.vector
        self.sca = nc.scalar
        self.gps = nc.gpsimd
        self.syn = nc.sync
        self.sem = {}
        self.val = {}
        for name in ("t", "v", "s", "g", "d", "d0", "d1", "r0", "r1", "x", "e0", "e1") + tuple(extra):
            self.sem[name] = ctx.enter_context(nc.semaphore("sem_" + name))
            self.val[name] = 0

    def inc(self, instr, name, step=1):
        instr.then_inc(self.sem[name], step)
        self.val[name] += step
        return self.val[name]

    def wait(self, engine, name, v=None):
        if v is None:
            v = self.val[name]
        if v > 0:
            engine.wait_ge(self.sem[name], v)

    def barrier(self, skip=("dc",)):
        for eng in (self.ten, self.vec, self.sca, self.gps, self.syn):
            for name in self.sem:
                if name in skip:
                    continue
                self.wait(eng, name)


class LNBufs:

    def __init__(self, nc, ctx, nslots, pf=""):
        E = ctx.enter_context
        self.stats = E(nc.sbuf_tensor(pf + "ln_stats", [128, 12], F32))
        self.mv = E(nc.sbuf_tensor(pf + "ln_mv", [128, 3 * nslots], F32))
        self.rstd = E(nc.sbuf_tensor(pf + "ln_rstd", [128, nslots], F32))
        self.negmur = E(nc.sbuf_tensor(pf + "ln_negmur", [128, nslots], F32))
        self.free_s = [0] * nslots
        self.free_v = [0] * nslots
        self.eps = E(nc.sbuf_tensor(pf + "ln_eps", [128, 1], F32))

    def init(self, p):
        p.inc(p.vec.memset(self.eps[:], EPS), "v")


def emit_ln_stats(p, lb, slot, src_ap, w):
    vec, sca = p.vec, p.sca
    mv = lb.mv[:, 3 * slot : 3 * slot + 3]
    rstd = lb.rstd[:, slot : slot + 1]
    negmur = lb.negmur[:, slot : slot + 1]
    p.wait(vec, "s", lb.free_s[slot])
    p.wait(sca, "v", lb.free_v[slot])
    nchunk = (w + 511) // 512
    ins = None
    for c in range(nchunk):
        lo = c * 512
        hi = min(w, lo + 512)
        ins = vec.bn_stats(lb.stats[:, 6 * c : 6 * c + 6], src_ap[:, lo:hi])
    p.inc(ins, "v")
    p.wait(vec, "v")
    p.inc(vec.bn_aggr(mv[:, 0:2], lb.stats[:, 0 : 6 * nchunk]), "v")
    p.wait(sca, "v")
    p.inc(sca.activation(rstd, mv[:, 1:2], AF.Ln, bias=lb.eps[:]), "s")
    p.wait(sca, "s")
    s_tok = p.inc(sca.activation(rstd, rstd, AF.Exp, scale=-0.5), "s")
    p.wait(vec, "s", s_tok)
    v_tok = p.inc(
        vec.tensor_scalar(negmur, mv[:, 0:1], rstd, -1.0, ALU.mult, ALU.mult), "v"
    )
    lb.free_s[slot] = s_tok
    lb.free_v[slot] = v_tok
    return v_tok, s_tok, rstd, negmur



TC = 512
NCH = TOK // TC
NF1 = 4096 // 128
NF2 = 4096 // 128
RPAD = 160
TC2 = 512
NCH2 = TOK // TC2
NEXP = 8
NH = 2048 // 128
NK2 = NEXP * NH
EPS32 = float(np.finfo(np.float32).eps)


def P_ap(big_sb, off, w):
    return big_sb[:, off : off + w]


def host_consts():
    ones = np.ones((1, 128), dtype=np.float32)
    idf = np.eye(128, dtype=np.float32)
    cmask = np.zeros((4, 128, 512), dtype=np.float32)
    for r in range(4):
        jl = np.arange(128)[:, None]
        il = np.arange(512)[None, :]
        cmask[r][il < jl + 128 * r] = NEG
    return ones, idf, cmask


_DRAM_CACHE = {}


def dram_cached(nc, name, shape, dtype, kind="Internal"):
    key = (id(nc), name)
    if key not in _DRAM_CACHE:
        _DRAM_CACHE[key] = nc.dram_tensor(name, shape, dtype, kind=kind)
    return _DRAM_CACHE[key]


def emit_l0_attn(nc, ctx, p, x, y):
    Wq = dram_cached(nc, "Wq", [DM, DM], F32, kind="ExternalInput").ap()
    Wk = dram_cached(nc, "Wk", [DM, DM], F32, kind="ExternalInput").ap()
    Wv = dram_cached(nc, "Wv", [DM, DM], F32, kind="ExternalInput").ap()
    bq = dram_cached(nc, "bq", [DM], F32, kind="ExternalInput").ap()
    bk = dram_cached(nc, "bk", [DM], F32, kind="ExternalInput").ap()
    bv = dram_cached(nc, "bv", [DM], F32, kind="ExternalInput").ap()
    ones = dram_cached(nc, "ones", [1, 128], F32, kind="ExternalInput").ap()
    idf = dram_cached(nc, "idf", [128, 128], F32, kind="ExternalInput").ap()
    cmask = dram_cached(nc, "cmask", [4, 128, 512], F32, kind="ExternalInput").ap()
    wstage = dram_cached(nc, "wstage", [3, DM, DM], BF16, kind="Internal").ap()
    xnd = dram_cached(nc, "xnd", [SEQ_PER_CORE, SEQ, DM], BF16, kind="Internal").ap()

    p_off = []
    acc = 0
    for jt in range(NTS):
        bi0 = (128 * jt) // 512
        p_off.append((acc, 512 * bi0, SEQ - 512 * bi0))
        acc += SEQ - 512 * bi0

    with ExitStack() as ctx:
        E = ctx.enter_context
        ten, vec, sca, gps, syn = p.ten, p.vec, p.sca, p.gps, p.syn
        lb = LNBufs(nc, ctx, 4, "a_")
        lb.init(p)

        ones_sb = E(nc.sbuf_tensor("a_" + "ones_sb", [1, 128], F32))
        id8_sb = E(nc.sbuf_tensor("a_" + "id8_sb", [8, 8], F32))
        braw_sb = E(nc.sbuf_tensor("a_" + "braw_sb", [8, 256], F32))
        bcol_sb = E(nc.sbuf_tensor("a_" + "bcol_sb", [128, 16], F32))
        bvrow_sb = E(nc.sbuf_tensor("a_" + "bvrow_sb", [1, DM], F32))
        mask_sb = E(nc.sbuf_tensor("a_" + "mask_sb", [128, 4 * 512], BF16))
        sm_sb = E(nc.sbuf_tensor("a_" + "sm_sb", [128, 8], F32))
        xst_sb = E(nc.sbuf_tensor("a_" + "xst_sb", [128, 2 * DM], F32))
        xnst_sb = E(nc.sbuf_tensor("a_" + "xnst_sb", [128, 2 * DM], BF16))
        x2st_sb = E(nc.sbuf_tensor("a_" + "x2st_sb", [128, 2 * DM], F32))
        tmp_sb = E(nc.sbuf_tensor("a_" + "tmp_sb", [128, 2 * DM], F32))
        out_sb = E(nc.sbuf_tensor("a_" + "out_sb", [128, 2 * DM], F32))
        qT_sb = E(nc.sbuf_tensor("a_" + "qT_sb", [128, NE * SEQ], BF16))
        kT_sb = E(nc.sbuf_tensor("a_" + "kT_sb", [128, NE * SEQ], BF16))
        v_sb = E(nc.sbuf_tensor("a_" + "v_sb", [128, NTS * DM], BF16))
        st_sb = E(nc.sbuf_tensor("a_" + "st_sb", [128, 2 * SEQ], F32))
        big_sb = E(nc.sbuf_tensor("a_" + "big_sb", [128, 24576], BF16))
        xnT = big_sb[:, 0:16384]
        wTf = big_sb[:, 16384:24576]

        ps_a = E(nc.psum_tensor("a_" + "ps_a", [128, 512], F32))
        ps_b = E(nc.psum_tensor("a_" + "ps_b", [128, 512], F32))
        ps_c = E(nc.psum_tensor("a_" + "ps_c", [128, 512], F32))
        ps_o = [E(nc.psum_tensor("a_" + f"ps_o{i}", [128, DM], F32)) for i in range(2)]


        p.inc(syn.dma_start(ones_sb[:], ones), "d", 16)
        p.inc(syn.dma_start(id8_sb[:], idf[0:8, 0:8]), "d", 16)
        p.inc(syn.dma_start(braw_sb[:, 0:128], bq.rearrange("(a b) -> a b", a=8)), "d", 16)
        p.inc(syn.dma_start(braw_sb[:, 128:256], bk.rearrange("(a b) -> a b", a=8)), "d", 16)
        p.inc(syn.dma_start(bvrow_sb[:], bv.rearrange("(a b) -> a b", a=1)), "d", 16)
        for r in range(4):
            p.inc(gps.dma_start(mask_sb[:, r * 512 : (r + 1) * 512], cmask[r]), "g", 16)
        p.wait(ten, "d")
        p.inc(ten.transpose(ps_c[:, 0:8], braw_sb[0:8, 0:128], id8_sb[:]), "t")
        p.wait(sca, "t")
        p.inc(sca.activation(bcol_sb[:, 0:8], ps_c[:, 0:8], AF.Identity, scale=1.0 / 32.0), "s")
        p.wait(ten, "s")
        p.inc(ten.transpose(ps_c[:, 8:16], braw_sb[0:8, 128:256], id8_sb[:]), "t")
        p.wait(sca, "t")
        p.inc(sca.activation(bcol_sb[:, 8:16], ps_c[:, 8:16], AF.Identity), "s")
        p.barrier()

        a_stage = [(0, 0, 0)] * 2
        a_era_tok = {}
        a_era_half = {}

        def emit_era_tile_A(s_, i_):
            b = i_ % 2
            vf, sf, ef = a_stage[b]
            p.wait(syn, "v", vf)
            p.wait(syn, "s", sf)
            p.wait(syn, "e%d" % b, ef)
            xs = xst_sb[:, b * DM : (b + 1) * DM]
            xns = xnst_sb[:, b * DM : (b + 1) * DM]
            p.wait(syn, "d%d" % b)
            dtok = p.inc(
                syn.dma_start(xs, x[s_ * SEQ + i_ * 128 : s_ * SEQ + (i_ + 1) * 128, :]),
                "d%d" % b, 16,
            )
            p.wait(vec, "d%d" % b, dtok)
            vtok, stok, rstd, negmur = emit_ln_stats(p, lb, b, xs, DM)
            p.wait(vec, "v")
            p.wait(vec, "e%d" % b, ef)
            aptok = p.inc(
                vec.tensor_scalar(xns, xs, rstd, negmur, ALU.mult, ALU.add), "v"
            )
            lb.free_v[b] = aptok
            p.wait(syn, "v", aptok)
            p.wait(syn, "e%d" % b)
            etok = p.inc(
                syn.dma_start(xnd[s_, i_ * 128 : (i_ + 1) * 128, :], xns), "e%d" % b, 16
            )
            a_stage[b] = (aptok, stok, etok)
            a_era_tok[s_] = (p.val["e0"], p.val["e1"])
            if i_ == 7 or i_ == 15:
                a_era_half[(s_, i_ // 8)] = (p.val["e0"], p.val["e1"])

        for i_ in range(NTS):
            emit_era_tile_A(0, i_)

        for s in range(SEQ_PER_CORE):
            xseq = x[s * SEQ : (s + 1) * SEQ, :]
            yseq = y[s * SEQ : (s + 1) * SEQ, :]

            xhalf = [0, 0]
            hsem = ["x", "xs"]
            for half in range(2):
                e0t, e1t = a_era_half[(s, half)]
                p.wait(syn, "e0", e0t)
                p.wait(syn, "e1", e1t)
                p.wait(syn, hsem[half])
                for et in range(NE):
                    p.inc(
                        syn.dma_start(
                            xnT[:, et * SEQ + half * 1024 : et * SEQ + half * 1024 + 1024],
                            xnd[s, half * 1024 : (half + 1) * 1024, et * 128 : (et + 1) * 128],
                            transpose=True,
                        ),
                        hsem[half],
                        16,
                    )
                xhalf[half] = p.val[hsem[half]]

            t_qkv_prev = 0
            evac_parity = {}
            for m in range(3):
                p.wait(syn, "t", t_qkv_prev)
                for et in range(NE):
                    p.inc(
                        syn.dma_start(
                            wTf[:, et * DM : (et + 1) * DM],
                            wstage[m, :, et * 128 : (et + 1) * 128],
                            transpose=True,
                        ),
                        "u0",
                        16,
                    )
                u_m = p.val["u0"]
                if m < 2:
                    dst = qT_sb if m == 0 else kT_sb
                    for tb in range(SEQ // 512):
                        for ft in range(NE):
                            key = (ft * 4 + tb) % 2
                            ps = ps_a if key == 0 else ps_b
                            if key in evac_parity:
                                p.wait(ten, "s", evac_parity[key])
                            if ft == 0:
                                p.wait(ten, ["x", "xs"][tb // 2], xhalf[tb // 2])
                                p.wait(ten, "u0", u_m)
                            last = None
                            for et in range(NE):
                                last = ten.matmul(
                                    ps[:],
                                    wTf[:, et * DM + ft * 128 : et * DM + ft * 128 + 128],
                                    xnT[:, et * SEQ + tb * 512 : et * SEQ + tb * 512 + 512],
                                    start=(et == 0),
                                    stop=(et == NE - 1),
                                )
                            ttok = p.inc(last, "t")
                            p.wait(sca, "t", ttok)
                            kw = (
                                dict(scale=1.0 / 32.0, bias=bcol_sb[:, ft : ft + 1])
                                if m == 0
                                else dict(bias=bcol_sb[:, 8 + ft : 9 + ft])
                            )
                            ins = sca.activation(
                                dst[:, ft * SEQ + tb * 512 : ft * SEQ + tb * 512 + 512],
                                ps[:],
                                AF.Identity,
                                **kw,
                            )
                            evac_parity[key] = p.inc(ins, "s")
                    t_qkv_prev = p.val["t"]
                else:
                    for tt in range(NTS):
                        for fb in range(2):
                            key = (tt * 2 + fb) % 2
                            ps = ps_a if key == 0 else ps_b
                            if key in evac_parity:
                                p.wait(ten, "s", evac_parity[key])
                            if fb == 0:
                                p.wait(ten, ["x", "xs"][tt // 8], xhalf[tt // 8])
                                p.wait(ten, "u0", u_m)
                            for et in range(NE):
                                ten.matmul(
                                    ps[:],
                                    xnT[:, et * SEQ + tt * 128 : et * SEQ + tt * 128 + 128],
                                    wTf[:, et * DM + fb * 512 : et * DM + fb * 512 + 512],
                                    start=(et == 0),
                                    stop=False,
                                )
                            last = ten.matmul(
                                ps[:],
                                ones_sb[:],
                                bvrow_sb[:, fb * 512 : fb * 512 + 512],
                                start=False,
                                stop=True,
                            )
                            ttok = p.inc(last, "t")
                            p.wait(sca, "t", ttok)
                            ins = sca.activation(
                                v_sb[:, tt * DM + fb * 512 : tt * DM + fb * 512 + 512],
                                ps[:],
                                AF.Identity,
                            )
                            evac_parity[key] = p.inc(ins, "s")
                    t_qkv_prev = p.val["t"]
            p.wait(ten, "s")

            prev_exp = [0, 0]
            ps_free = {0: ("s", 0), 1: ("s", 0)}
            for jt in range(NTS):
                jb = jt % 2
                stj = st_sb[:, jb * SEQ : (jb + 1) * SEQ]
                smb = 4 * jb
                if s == 0:
                    emit_era_tile_A(1, jt)
                if jt % 2 == 0:
                    p.pump(1)
                off, i0, wc = p_off[jt]
                bi0 = i0 // 512
                evac_tok = []
                for ib in range(bi0, 4):
                    key = ib % 2
                    ps = ps_a if key == 0 else ps_b
                    nm, tok = ps_free[key]
                    p.wait(ten, nm, tok)
                    last = None
                    for et in range(NE):
                        last = ten.matmul(
                            ps[:],
                            kT_sb[:, et * SEQ + jt * 128 : et * SEQ + jt * 128 + 128],
                            qT_sb[:, et * SEQ + ib * 512 : et * SEQ + ib * 512 + 512],
                            start=(et == 0),
                            stop=(et == NE - 1),
                        )
                    ttok = p.inc(last, "t")
                    if ib == bi0:
                        r = jt - 4 * bi0
                        p.wait(vec, "t", ttok)
                        p.wait(vec, "s", prev_exp[jb])
                        tok = p.inc(
                            vec.tensor_tensor(
                                stj[:, ib * 512 : ib * 512 + 512],
                                ps[:],
                                mask_sb[:, r * 512 : r * 512 + 512],
                                ALU.add,
                            ),
                            "v",
                        )
                        evac_tok.append(("v", tok))
                        ps_free[key] = ("v", tok)
                    else:
                        p.wait(sca, "t", ttok)
                        p.wait(sca, "s", prev_exp[jb])
                        tok = p.inc(
                            sca.activation(
                                stj[:, ib * 512 : ib * 512 + 512], ps[:], AF.Identity
                            ),
                            "s",
                        )
                        evac_tok.append(("s", tok))
                        ps_free[key] = ("s", tok)
                for nm, tok in evac_tok:
                    p.wait(vec, nm, tok)
                p.wait(vec, "s", prev_exp[jb])
                p.inc(
                    vec.reduce_max(
                        sm_sb[:, smb : smb + 1], stj[:, i0:SEQ], axis=AX.X, negate=True
                    ),
                    "v",
                )
                for nm, tok in evac_tok:
                    p.wait(sca, nm, tok)
                p.wait(sca, "v")
                prev_exp[jb] = p.inc(
                    sca.activation(
                        P_ap(big_sb, off, wc),
                        stj[:, i0:SEQ],
                        AF.Exp,
                        bias=sm_sb[:, smb : smb + 1],
                        accum_out=sm_sb[:, smb + 1 : smb + 2],
                    ),
                    "s",
                )
                p.wait(vec, "s", prev_exp[jb])
                p.inc(vec.reciprocal(sm_sb[:, smb + 2 : smb + 3], sm_sb[:, smb + 1 : smb + 2]), "v")
                p.wait(vec, "v")
                p.inc(
                    vec.tensor_scalar(
                        v_sb[:, jt * DM : (jt + 1) * DM],
                        v_sb[:, jt * DM : (jt + 1) * DM],
                        sm_sb[:, smb + 2 : smb + 3],
                        None,
                        ALU.mult,
                    ),
                    "v",
                )
            p.barrier()

            ofree = [(0, 0, 0)] * 2
            for it in range(NTS):
                if it % 2 == 0:
                    p.pump(1)
                b = it % 2
                vf, sf, ef = ofree[b]
                p.wait(syn, "v", vf)
                x2s = x2st_sb[:, b * DM : (b + 1) * DM]
                dres = p.inc(syn.dma_start(x2s, xseq[it * 128 : (it + 1) * 128, :]), "r%d" % b, 16)
                p.wait(ten, "v", vf)
                p.wait(ten, "s", sf)
                pso = ps_o[b]
                for eb in range(2):
                    first = True
                    last = None
                    for jt in range(it + 1):
                        off, i0, wc = p_off[jt]
                        col = it * 128 - i0
                        last = ten.matmul(
                            pso[:, eb * 512 : eb * 512 + 512],
                            P_ap(big_sb, off + col, 128),
                            v_sb[:, jt * DM + eb * 512 : jt * DM + eb * 512 + 512],
                            start=first,
                            stop=(jt == it),
                        )
                        first = False
                    if eb == 1:
                        ttok = p.inc(last, "t")
                p.wait(vec, "t", ttok)
                vtok, stok, rstd, negmur = emit_ln_stats(p, lb, 2 + b, pso, DM)
                tmps = tmp_sb[:, b * DM : (b + 1) * DM]
                outs = out_sb[:, b * DM : (b + 1) * DM]
                p.wait(sca, "v", vtok)
                p.wait(sca, "s", stok)
                stok2 = p.inc(
                    sca.activation(tmps, pso[:], AF.Identity, bias=negmur, scale=rstd), "s"
                )
                lb.free_s[2 + b] = stok2
                p.wait(vec, "s", stok2)
                p.wait(vec, "r%d" % b, dres)
                p.wait(vec, "e%d" % b, ef)
                vtok2 = p.inc(vec.tensor_tensor(outs, tmps, x2s, ALU.add), "v")
                p.wait(syn, "v", vtok2)
                etok = p.inc(syn.dma_start(yseq[it * 128 : (it + 1) * 128, :], outs), "e%d" % b, 16)
                ofree[b] = (vtok2, stok2, etok)
            p.barrier()

        p.wait(syn, "e0")
        p.wait(syn, "e1")
        p.wait(syn, "d")



def emit_l0_ffn(nc, ctx, p, x, y):
    k0 = dram_cached(nc, "k0", [2 * DM, 4096], U8, kind="ExternalInput").ap()
    k1 = dram_cached(nc, "k1", [4096, 4096], U8, kind="ExternalInput").ap()
    k2 = dram_cached(nc, "k2", [4096, DM], U8, kind="ExternalInput").ap()
    s0 = dram_cached(nc, "s0", [1, 1], F32, kind="ExternalInput").ap()
    s1 = dram_cached(nc, "s1", [1, 1], F32, kind="ExternalInput").ap()
    s2 = dram_cached(nc, "s2", [1, 1], F32, kind="ExternalInput").ap()
    ones = dram_cached(nc, "ones", [1, 128], F32, kind="ExternalInput").ap()
    w0d = dram_cached(nc, "w0d", [DM, 4096], BF16, kind="Internal").ap()
    w1d = dram_cached(nc, "w1d", [4096, 4096], BF16, kind="Internal").ap()
    w2d = dram_cached(nc, "w2d", [4096, DM], BF16, kind="Internal").ap()
    xnd = dram_cached(nc, "xnd_b", [TOK, DM], BF16, kind="Internal").ap()

    with ExitStack() as ctx:
        E = ctx.enter_context
        ten, vec, sca, gps, syn = p.ten, p.vec, p.sca, p.gps, p.syn
        lb = LNBufs(nc, ctx, 4, "b_")
        lb.init(p)

        ones_sb = E(nc.sbuf_tensor("b_ones_sb", [1, 128], F32))
        svec_sb = E(nc.sbuf_tensor("b_svec_sb", [1, 4], F32))
        sc_sb = E(nc.sbuf_tensor("b_sc_sb", [128, 4], F32))
        xst_sb = E(nc.sbuf_tensor("b_xst_sb", [128, 2 * DM], F32))
        xnst_sb = E(nc.sbuf_tensor("b_xnst_sb", [128, 2 * DM], BF16))
        x2st_sb = E(nc.sbuf_tensor("b_x2st_sb", [128, 2 * DM], F32))
        tmp_sb = E(nc.sbuf_tensor("b_tmp_sb", [128, 2 * DM], F32))
        out_sb = E(nc.sbuf_tensor("b_out_sb", [128, 2 * DM], F32))
        y_sb = E(nc.sbuf_tensor("b_y_sb", [128, 2 * DM], F32))
        xnT_sb = E(nc.sbuf_tensor("b_xnT_sb", [128, 2 * NE * TC], BF16))
        h1_sb = E(nc.sbuf_tensor("b_h1_sb", [128, NF1 * TC], BF16))
        h2_sb = E(nc.sbuf_tensor("b_h2_sb", [128, NF2 * TC], BF16))
        cvt_sb = h1_sb
        cvo_sb = h2_sb
        col_sb = E(nc.sbuf_tensor("b_col_sb", [128, 4 * NF1 * 128], BF16))
        w2t_sb = E(nc.sbuf_tensor("b_w2t_sb", [128, 4 * DM], BF16))

        ps = [E(nc.psum_tensor(f"b_ps{i}", [128, 512], F32)) for i in range(8)]

        p.inc(syn.dma_start(ones_sb[:], ones), "d", 16)
        p.inc(syn.dma_start(svec_sb[:, 0:1], s0), "d", 16)
        p.inc(syn.dma_start(svec_sb[:, 1:2], s1), "d", 16)
        p.inc(syn.dma_start(svec_sb[:, 2:3], s2), "d", 16)
        p.wait(ten, "d")
        last = ten.matmul(ps[0][:, 0:3], ones_sb[:], svec_sb[:, 0:3], start=True, stop=True)
        p.inc(last, "t")
        p.wait(sca, "t")
        p.inc(sca.activation(sc_sb[:, 0:1], ps[0][:, 0:1], AF.Identity, scale=4.0), "s")
        p.inc(sca.activation(sc_sb[:, 1:2], ps[0][:, 1:2], AF.Identity, scale=2.0), "s")
        p.inc(sca.activation(sc_sb[:, 2:3], ps[0][:, 2:3], AF.Identity), "s")

        cvt8 = cvt_sb.bitcast(U8)
        conv_jobs = []
        for et in range(NE):
            conv_jobs.append(("w0", et))
        for r in range(NF1):
            conv_jobs.append(("k1", r))
        for r in range(NF2):
            conv_jobs.append(("k2", r))

        slot_tt = [("v", 0)] * 4
        slot_st = [0] * 4
        st_sem = ["k0", "k1", "k2", "k3"]
        ld_sem = ["d0", "d1", "r0", "r1"]
        milestones = {}

        def conv_round(ridx, kind, arg):
            sl = ridx % 4
            eng = vec if ridx % 2 == 0 else gps
            ename = "v" if ridx % 2 == 0 else "gc"
            ci = cvt8[:, sl * 8192 : sl * 8192 + 4096]
            co = cvo_sb[:, sl * 4096 : sl * 4096 + 4096]
            nm, tok = slot_tt[sl]
            p.wait(syn, nm, tok)
            p.wait(syn, ld_sem[sl])
            if kind == "w0":
                et = arg
                t0 = cvt8[:, sl * 8192 : sl * 8192 + 4096]
                t1 = cvt8[:, sl * 8192 + 4096 : sl * 8192 + 8192]
                p.inc(syn.dma_start(t0, k0[et * 128 : et * 128 + 128, :]), ld_sem[sl], 16)
                ltok = p.inc(
                    syn.dma_start(t1, k0[DM + et * 128 : DM + et * 128 + 128, :]),
                    ld_sem[sl], 16,
                )
            else:
                kk, r = (k1, arg) if kind == "k1" else (k2, arg)
                cols = 4096 if kind == "k1" else DM
                nblk = 4096 // cols
                if nblk == 1:
                    ap_in = kk[r * 128 : (r + 1) * 128, :]
                else:
                    ap_in = custom_ap(kk, r * nblk * 128 * cols,
                                      [[cols, 128], [128 * cols, nblk], [1, cols]])
                ltok = p.inc(syn.dma_start(ci, ap_in), ld_sem[sl], 16)
            p.wait(eng, ld_sem[sl], ltok)
            p.wait(eng, st_sem[sl], slot_st[sl])
            if kind == "w0":
                t0 = cvt8[:, sl * 8192 : sl * 8192 + 4096]
                t1 = cvt8[:, sl * 8192 + 4096 : sl * 8192 + 8192]
                vtok = p.inc(eng.tensor_tensor(co, t0, t1, ALU.subtract), ename)
            else:
                vtok = p.inc(
                    eng.tensor_scalar(co, ci, 2.0, -1.0, ALU.mult, ALU.add), ename
                )
            slot_tt[sl] = (ename, vtok)
            p.wait(sca, ename, vtok)
            p.wait(sca, st_sem[sl])
            if kind == "w0":
                dst = w0d[arg * 128 : (arg + 1) * 128, :]
            elif kind == "k1":
                dst = w1d[arg * 128 : (arg + 1) * 128, :]
            else:
                r = arg
                nblk = 4
                dst = custom_ap(w2d, r * nblk * 128 * DM,
                                [[DM, 128], [128 * DM, nblk], [1, DM]])
            slot_st[sl] = p.inc(sca.dma_start(dst, co), st_sem[sl], 16)

        conv_jobs = []
        for et in range(NE):
            conv_jobs.append(("w0", et))
        for r in range(NF1):
            conv_jobs.append(("k1", r))
        for r in range(NF2 // 4):
            conv_jobs.append(("k2", r))

        stage_free = [(0, 0, 0)] * 2
        era_store_tok = {}

        def emit_era_tile(i):
            b = i % 2
            vf, sf, ef = stage_free[b]
            p.wait(syn, "v", vf)
            p.wait(syn, "s", sf)
            p.wait(syn, "e%d" % b, ef)
            xs = xst_sb[:, b * DM : (b + 1) * DM]
            xns = xnst_sb[:, b * DM : (b + 1) * DM]
            row = i * 128
            p.wait(syn, "d%d" % b)
            dtok = p.inc(syn.dma_start(xs, x[row : row + 128, :]), "d%d" % b, 16)
            p.wait(vec, "d%d" % b, dtok)
            vtok, stok, rstd, negmur = emit_ln_stats(p, lb, b, xs, DM)
            p.wait(vec, "v")
            p.wait(vec, "e%d" % b, ef)
            aptok = p.inc(
                vec.tensor_scalar(xns, xs, rstd, negmur, ALU.mult, ALU.add), "v"
            )
            lb.free_v[b] = aptok
            p.wait(syn, "v", aptok)
            p.wait(syn, "e%d" % b)
            etok = p.inc(syn.dma_start(xnd[row : row + 128, :], xns), "e%d" % b, 16)
            stage_free[b] = (aptok, stok, etok)
            ch = i // (TC // 128)
            era_store_tok[ch] = (p.val["e0"], p.val["e1"])


        ps_rd = [("s", 0)] * 8
        t_z0_prev = [0, 0]
        t_z1_prev = 0
        t_z2_prev = 0
        xtok = [0, 0]
        xsem = ["x", "xs"]
        colfree = [0, 0, 0, 0]
        ctok = [0, 0, 0, 0]
        w2free = [0, 0, 0, 0]
        utok = [0, 0, 0, 0]
        col_q = [syn, gps, syn, gps]
        col_sem = ["w0", "w1", "w2", "w3"]
        w2_q = [syn, gps, syn, gps]
        w2_sem = ["u0", "u1", "u2", "u3"]
        issued = [0]

        NSTR = 8 + NF2

        def stream_src(gidx):
            k = gidx % NSTR
            if k < 8:
                return custom_ap(w0d, k * 512, [[4096, 128], [4096 * 128, 8], [1, 512]]), NE * 512
            k -= 8
            return custom_ap(w1d, k * 128, [[4096, 128], [4096 * 128, 32], [1, 128]]), NF1 * 128

        def ensure_issued(upto):
            while issued[0] <= min(upto, NCH * NSTR - 1):
                gidx = issued[0]
                sl = gidx % 4
                q = col_q[sl]
                srcap, n = stream_src(gidx)
                p.wait(q, "t", colfree[sl])
                p.wait(q, col_sem[sl])
                if gidx < NSTR:
                    ms = milestones["w0_st" if gidx % NSTR < 8 else "k1_st"]
                    for snm, stok_ in zip(st_sem, ms):
                        p.wait(q, snm, stok_)
                ctok[sl] = p.inc(
                    q.dma_start(col_sb[:, sl * NF1 * 128 : sl * NF1 * 128 + n], srcap),
                    col_sem[sl],
                    16,
                )
                issued[0] += 1

        def issue_transposes(c):
            bx = c % 2
            tok0 = c * TC
            p.wait(sca, "t", t_z0_prev[bx])
            e0t, e1t = era_store_tok[c]
            p.wait(sca, "e0", e0t)
            p.wait(sca, "e1", e1t)
            for et in range(NE):
                p.inc(
                    sca.dma_start(
                        xnT_sb[:, bx * NE * TC + et * TC : bx * NE * TC + (et + 1) * TC],
                        xnd[tok0 : tok0 + TC, et * 128 : (et + 1) * 128],
                        transpose=True,
                    ),
                    xsem[bx],
                    16,
                )
            xtok[bx] = p.val[xsem[bx]]

        def issue_w2(gidx):
            sl = gidx % 4
            idx = gidx % NF2
            q = w2_q[sl]
            p.wait(q, "t", w2free[sl])
            if gidx < NF2:
                for snm, stok_ in zip(st_sem, milestones["all_st"]):
                    p.wait(q, snm, stok_)
            utok[sl] = p.inc(
                q.dma_start(w2t_sb[:, sl * DM : (sl + 1) * DM], w2d[idx * 128 : (idx + 1) * 128, :]),
                w2_sem[sl],
                16,
            )

        era_next = [0]

        def emit_some_era(n):
            while n > 0 and era_next[0] < TOK // 128:
                emit_era_tile(era_next[0])
                era_next[0] += 1
                n -= 1

        for ridx, (kind, arg) in enumerate(conv_jobs):
            conv_round(ridx, kind, arg)
            if ridx % 6 == 5:
                emit_some_era(1)
            if kind == "w0" and arg == NE - 1:
                milestones["w0_st"] = list(slot_st)
            if kind == "k1" and arg == NF1 - 1:
                milestones["k1_st"] = list(slot_st)
        milestones["all_st"] = list(slot_st)
        milestones["tt_v"] = p.val["v"]
        milestones["tt_gc"] = p.val["gc"]
        p.wait(syn, "d0")
        p.wait(syn, "d1")
        p.wait(syn, "r0")
        p.wait(syn, "r1")
        emit_some_era(max(0, 8 - era_next[0]))
        issue_transposes(0)
        ensure_issued(1)
        for c in range(NCH):
            bx = c % 2
            tok0 = c * TC
            if c + 1 < NCH:
                issue_transposes(c + 1)

            xbase = bx * NE * TC
            for f1t in range(NF1):
                gidx = c * NSTR + f1t // 4
                sl = gidx % 4
                loc = f1t % 4
                if f1t % 4 == 0:
                    p.pump(1)
                ensure_issued(gidx + 3)
                bank = f1t % 3
                pz = ps[bank]
                nm, tok = ps_rd[bank]
                p.wait(ten, nm, tok)
                if f1t == 0:
                    p.wait(ten, xsem[bx], xtok[bx])
                if loc == 0:
                    p.wait(ten, col_sem[sl], ctok[sl])
                last = None
                for et in range(NE):
                    last = ten.matmul(
                        pz[:],
                        col_sb[:, sl * NF1 * 128 + et * 512 + loc * 128 : sl * NF1 * 128 + et * 512 + loc * 128 + 128],
                        xnT_sb[:, xbase + et * TC : xbase + (et + 1) * TC],
                        start=(et == 0),
                        stop=(et == NE - 1),
                    )
                ttok = p.inc(last, "t")
                if loc == 3:
                    colfree[sl] = ttok
                if f1t == NF1 - 1:
                    t_z0_prev[bx] = ttok
                p.wait(sca, "t", ttok)
                if f1t == 0:
                    p.wait(sca, "t", t_z1_prev)
                if c == 0 and f1t == 0:
                    p.wait(sca, "v", milestones["tt_v"])
                    p.wait(sca, "gc", milestones["tt_gc"])
                ps_rd[bank] = (
                    "s",
                    p.inc(
                        sca.activation(
                            h1_sb[:, f1t * TC : (f1t + 1) * TC], pz[:], AF.Relu,
                            scale=sc_sb[:, 0:1],
                        ),
                        "s",
                    ),
                )
            s_h1_done = p.val["s"]
            emit_some_era(TC // 128)

            p.wait(ten, "s", s_h1_done)
            for f2t in range(NF2):
                gidx = c * NSTR + 8 + f2t
                sl = gidx % 4
                if f2t % 4 == 0:
                    p.pump(1)
                ensure_issued(gidx + 3)
                bank = 3 + f2t % 2
                pz = ps[bank]
                nm, tok = ps_rd[bank]
                p.wait(ten, nm, tok)
                p.wait(ten, col_sem[sl], ctok[sl])
                last = None
                for f1t in range(NF1):
                    last = ten.matmul(
                        pz[:],
                        col_sb[:, sl * NF1 * 128 + f1t * 128 : sl * NF1 * 128 + (f1t + 1) * 128],
                        h1_sb[:, f1t * TC : (f1t + 1) * TC],
                        start=(f1t == 0),
                        stop=(f1t == NF1 - 1),
                    )
                ttok = p.inc(last, "t")
                colfree[sl] = ttok
                if f2t == NF2 - 1:
                    t_z1_prev = ttok
                p.wait(sca, "t", ttok)
                if f2t == 0:
                    p.wait(sca, "t", t_z2_prev)
                if c == 0 and f2t == 0:
                    for snm, stok_ in zip(st_sem, milestones["all_st"]):
                        p.wait(sca, snm, stok_)
                ps_rd[bank] = (
                    "s",
                    p.inc(
                        sca.activation(
                            h2_sb[:, f2t * TC : (f2t + 1) * TC], pz[:], AF.Relu,
                            scale=sc_sb[:, 1:2],
                        ),
                        "s",
                    ),
                )
            s_h2_done = p.val["s"]

            if c == 0:
                for k in range(3):
                    issue_w2(k)
            p.wait(ten, "s", s_h2_done)
            gz_base = c * NF2
            for bank in range(8):
                nm, tok = ps_rd[bank]
                p.wait(ten, nm, tok)
            for f2t in range(NF2):
                g = gz_base + f2t
                sl = g % 4
                if g + 3 < NCH * NF2:
                    issue_w2(g + 3)
                p.wait(ten, w2_sem[sl], utok[sl])
                last = None
                for tt in range(TC // 128):
                    for dh in range(2):
                        last = ten.matmul(
                            ps[tt * 2 + dh][:],
                            h2_sb[:, f2t * TC + tt * 128 : f2t * TC + tt * 128 + 128],
                            w2t_sb[:, sl * DM + dh * 512 : sl * DM + dh * 512 + 512],
                            start=(f2t == 0),
                            stop=(f2t == NF2 - 1),
                        )
                ttok = p.inc(last, "t")
                w2free[sl] = ttok
                if f2t == NF2 - 1:
                    t_z2_prev = ttok
            tz2 = p.val["t"]

            evac_free = [(0, 0)] * 2
            if c > 0:
                evac_free = prev_evac_free
            for tt in range(TC // 128):
                b = tt % 2
                row = tok0 + tt * 128
                vf, sf = evac_free[b]
                p.wait(syn, "v", vf)
                x2s = x2st_sb[:, b * DM : (b + 1) * DM]
                dres = p.inc(syn.dma_start(x2s, x[row : row + 128, :]), "r%d" % b, 16)
                ys = y_sb[:, b * DM : (b + 1) * DM]
                p.wait(sca, "t", tz2)
                p.wait(sca, "v", vf)
                p.wait(sca, "s", sf)
                p.inc(sca.activation(ys[:, 0:512], ps[tt * 2][:], AF.Identity, scale=sc_sb[:, 2:3]), "s")
                p.wait(sca, "s")
                stok0 = p.inc(
                    sca.activation(ys[:, 512:1024], ps[tt * 2 + 1][:], AF.Identity, scale=sc_sb[:, 2:3]),
                    "s",
                )
                ps_rd[tt * 2] = ("s", stok0)
                ps_rd[tt * 2 + 1] = ("s", stok0)
                p.wait(vec, "s", stok0)
                vtok, stok, rstd, negmur = emit_ln_stats(p, lb, 2 + b, ys, DM)
                tmps = tmp_sb[:, b * DM : (b + 1) * DM]
                outs = out_sb[:, b * DM : (b + 1) * DM]
                p.wait(sca, "v", vtok)
                p.wait(sca, "s", stok)
                stok2 = p.inc(
                    sca.activation(tmps, ys, AF.Identity, bias=negmur, scale=rstd), "s"
                )
                lb.free_s[2 + b] = stok2
                prev_ys_rd = (vtok, stok2)
                p.d_prev_ys_rd = prev_ys_rd
                p.wait(vec, "s", stok2)
                p.wait(vec, "r%d" % b, dres)
                p.wait(vec, "e%d" % b)
                vtok2 = p.inc(vec.tensor_tensor(outs, tmps, x2s, ALU.add), "v")
                p.wait(syn, "v", vtok2)
                p.inc(syn.dma_start(y[row : row + 128, :], outs), "e%d" % b, 16)
                evac_free[b] = (vtok2, stok2)
            prev_evac_free = evac_free

        p.wait(syn, "e0")
        p.wait(syn, "e1")
        p.wait(syn, "d")

def emit_l1_attn(nc, ctx, p, x, y):
    W = dram_cached(nc, "W", [32, DM], F32, kind="ExternalInput").ap()
    bb = dram_cached(nc, "bb", [1, 32], F32, kind="ExternalInput").ap()
    ones = dram_cached(nc, "ones", [1, 128], F32, kind="ExternalInput").ap()
    idb = dram_cached(nc, "idb", [128, 128], BF16, kind="ExternalInput").ap()
    wstage = dram_cached(nc, "wstage_c", [32, DM], BF16, kind="Internal").ap()
    xnd = dram_cached(nc, "xnd_c", [SEQ_PER_CORE, SEQ, DM], BF16, kind="Internal").ap()
    WPROWS = SEQ + 2
    wpad_n = ((WPROWS * RPAD + 127) // 128) * 128
    wpad = dram_cached(nc, "wpad", [SEQ_PER_CORE, wpad_n], BF16, kind="Internal").ap()

    with ExitStack() as ctx:
        E = ctx.enter_context
        ten, vec, sca, gps, syn = p.ten, p.vec, p.sca, p.gps, p.syn
        lb = LNBufs(nc, ctx, 4, "c_")
        lb.init(p)

        ones_sb = E(nc.sbuf_tensor("c_" + "ones_sb", [1, 128], F32))
        idb_sb = E(nc.sbuf_tensor("c_" + "idb_sb", [128, 128], BF16))
        brow_sb = E(nc.sbuf_tensor("c_" + "brow_sb", [1, 32], F32))
        wT_sb = E(nc.sbuf_tensor("c_" + "wT_sb", [128, NE * 32], BF16))
        sm_sb = E(nc.sbuf_tensor("c_" + "sm_sb", [128, 8], F32))
        w_sb = E(nc.sbuf_tensor("c_" + "w_sb", [128, NTS * 32], F32))
        zer_sb = E(nc.sbuf_tensor("c_" + "zer_sb", [128, 2576], BF16))
        xst_sb = E(nc.sbuf_tensor("c_" + "xst_sb", [128, 2 * DM], F32))
        x2st_sb = E(nc.sbuf_tensor("c_" + "x2st_sb", [128, 2 * DM], F32))
        tmp_sb = E(nc.sbuf_tensor("c_" + "tmp_sb", [128, 2 * DM], F32))
        out_sb = E(nc.sbuf_tensor("c_" + "out_sb", [128, 2 * DM], F32))
        xn_sb = E(nc.sbuf_tensor("c_" + "xn_sb", [128, 2 * NTS * DM], BF16))
        xnT_sb = E(nc.sbuf_tensor("c_" + "xnT_sb", [128, NE * SEQ], BF16))
        PT_sb = E(nc.sbuf_tensor("c_" + "PT_sb", [128, 4 * 256], BF16))
        P_sb = E(nc.sbuf_tensor("c_" + "P_sb", [128, 2 * 256], BF16))

        ps_l = E(nc.psum_tensor("c_" + "ps_l", [128, 512], F32))
        psD = [E(nc.psum_tensor("c_" + f"psD{i}", [128, 1024], BF16)) for i in range(2)]
        ps_o = [E(nc.psum_tensor("c_" + f"ps_oc{i}", [128, DM], F32)) for i in range(2)]


        p.inc(syn.dma_start(ones_sb[:], ones), "d", 16)
        p.inc(syn.dma_start(idb_sb[:], idb), "d", 16)
        p.inc(syn.dma_start(brow_sb[:], bb), "d", 16)
        p.inc(gps.dma_start(wstage, W), "g", 16)
        p.wait(syn, "g")
        for et in range(NE):
            p.inc(
                syn.dma_start(
                    wT_sb[:, et * 32 : (et + 1) * 32],
                    wstage[0:32, et * 128 : (et + 1) * 128],
                    transpose=True,
                ),
                "x",
                16,
            )
        p.inc(vec.memset(zer_sb[:], 0.0), "v")
        p.inc(vec.memset(PT_sb[:], 0.0), "v")
        p.wait(syn, "v")
        for s in range(SEQ_PER_CORE):
            p.inc(
                syn.dma_start(
                    wpad[s].rearrange("(a b) -> a b", a=128), zer_sb[:, 0 : wpad_n // 128]
                ),
                "d",
                16,
            )
        p.barrier()

        c1_stage = [(0, 0) for _ in range(2)]
        c1_tok = {}

        def emit_c1_tile(s_, i_):
            b = i_ % 2
            vf, sf = c1_stage[b]
            p.wait(syn, "v", vf)
            p.wait(syn, "s", sf)
            xs = xst_sb[:, b * DM : (b + 1) * DM]
            p.wait(syn, "d%d" % b)
            dtok = p.inc(
                syn.dma_start(xs, x[s_ * SEQ + i_ * 128 : s_ * SEQ + (i_ + 1) * 128, :]),
                "d%d" % b, 16,
            )
            p.wait(vec, "d%d" % b, dtok)
            vtok, stok, rstd, negmur = emit_ln_stats(p, lb, b, xs, DM)
            xns = xn_sb[:, (s_ * NTS + i_) * DM : (s_ * NTS + i_ + 1) * DM]
            p.wait(vec, "v")
            aptok = p.inc(
                vec.tensor_scalar(xns, xs, rstd, negmur, ALU.mult, ALU.add), "v"
            )
            lb.free_v[b] = aptok
            p.wait(syn, "v", aptok)
            p.inc(syn.dma_start(xnd[s_, i_ * 128 : (i_ + 1) * 128, :], xns), "xs", 16)
            c1_stage[b] = (aptok, stok)
            c1_tok[s_] = p.val["xs"]

        for i_ in range(NTS):
            emit_c1_tile(0, i_)
        p.wait(syn, "xs")

        for s in range(SEQ_PER_CORE):
            xseq = x[s * SEQ : (s + 1) * SEQ, :]
            yseq = y[s * SEQ : (s + 1) * SEQ, :]

            p.wait(syn, "xs", c1_tok[s])
            for et in range(NE):
                p.inc(
                    syn.dma_start(
                        xnT_sb[:, et * SEQ : (et + 1) * SEQ],
                        xnd[s, :, et * 128 : (et + 1) * 128],
                        transpose=True,
                    ),
                    "x",
                    16,
                )
            p.barrier()

            prev_exp_s = 0
            psl_free = 0
            for tt in range(NTS):
                p.wait(ten, "s", psl_free)
                last = None
                for et in range(NE):
                    last = ten.matmul(
                        ps_l[:, 0:32],
                        xnT_sb[:, et * SEQ + tt * 128 : et * SEQ + tt * 128 + 128],
                        wT_sb[:, et * 32 : (et + 1) * 32],
                        start=(et == 0),
                        stop=False,
                    )
                last = ten.matmul(
                    ps_l[:, 0:32], ones_sb[:], brow_sb[:], start=False, stop=True
                )
                ttok = p.inc(last, "t")
                p.wait(vec, "t", ttok)
                p.wait(vec, "s", prev_exp_s)
                p.inc(vec.reduce_max(sm_sb[:, 0:1], ps_l[:, 0:32], axis=AX.X, negate=True), "v")
                p.wait(sca, "v")
                prev_exp_s = p.inc(
                    sca.activation(
                        w_sb[:, tt * 32 : (tt + 1) * 32],
                        ps_l[:, 0:32],
                        AF.Exp,
                        bias=sm_sb[:, 0:1],
                        accum_out=sm_sb[:, 1:2],
                    ),
                    "s",
                )
                psl_free = prev_exp_s
                p.wait(vec, "s", prev_exp_s)
                p.inc(vec.reciprocal(sm_sb[:, 2:3], sm_sb[:, 1:2]), "v")
                p.wait(vec, "v")
                wtok = p.inc(
                    vec.tensor_scalar(
                        w_sb[:, tt * 32 : (tt + 1) * 32],
                        w_sb[:, tt * 32 : (tt + 1) * 32],
                        sm_sb[:, 2:3],
                        None,
                        ALU.mult,
                    ),
                    "v",
                )
                p.wait(gps, "v", wtok)
                dst = custom_ap(
                    wpad[s], (1 + tt * 128) * RPAD, [[RPAD, 128], [1, 32]]
                )
                p.inc(gps.dma_start(dst, w_sb[:, tt * 32 : (tt + 1) * 32]), "wp", 16)
            p.barrier()

            ofree = [(0, 0) for _ in range(2)]
            ptfree = [0, 0, 0, 0]
            pfree = [0, 0]
            psd_free = [0, 0]
            ktok = [0, 0, 0, 0]

            def issue_skew(lt):
                sl = lt % 4
                p.wait(syn, "t", ptfree[sl])
                srcap = custom_ap(
                    wpad[s], (1 + lt * 128) * RPAD, [[RPAD - 1, 128], [1, 159]]
                )
                ktok[sl] = p.inc(
                    syn.dma_start(PT_sb[:, sl * 256 : sl * 256 + 159], srcap), "k%d" % sl, 16
                )

            issue_skew(0)
            issue_skew(1)
            for lt in range(NTS):
                if s == 0:
                    emit_c1_tile(1, lt)
                p.pump(1)
                b = lt % 2
                sl = lt % 4
                if lt + 2 < NTS:
                    issue_skew(lt + 2)
                p.wait(ten, "k%d" % sl, ktok[sl])
                p.wait(ten, "s", psd_free[b])
                ten.transpose(
                    psD[b][:, 0:128], PT_sb[:, sl * 256 : sl * 256 + 128], idb_sb[:]
                )
                ttr = p.inc(
                    ten.transpose(
                        psD[b][:, 128:256], PT_sb[:, sl * 256 + 128 : sl * 256 + 256], idb_sb[:]
                    ),
                    "t",
                )
                ptfree[sl] = ttr
                p.wait(vec, "t", ttr)
                p.wait(vec, "t", pfree[b])
                p.wait(sca, "t", ttr)
                p.wait(sca, "t", pfree[b])
                ptok = p.inc(
                    sca.activation(P_sb[:, b * 256 : (b + 1) * 256], psD[b][:, 0:256], AF.Identity), "s"
                )
                psd_free[b] = ptok
                x2s = x2st_sb[:, b * DM : (b + 1) * DM]
                p.wait(syn, "gc", ofree[b][0])
                dres = p.inc(
                    syn.dma_start(x2s, xseq[lt * 128 : (lt + 1) * 128, :]), "r%d" % b, 16
                )
                vf, sf = ofree[b]
                p.wait(ten, "s", max(ptok, sf))
                p.wait(ten, "gc", vf)
                pso = ps_o[b]
                tmix = None
                for eb in range(2):
                    nsup = 2 if lt + 1 < NTS else 1
                    for kk in range(nsup):
                        lhs = P_sb[:, b * 256 + kk * 128 : b * 256 + kk * 128 + 128]
                        rhs = xn_sb[:, (s * NTS + lt + kk) * DM + eb * 512 : (s * NTS + lt + kk) * DM + eb * 512 + 512]
                        tmix = ten.matmul(
                            pso[:, eb * 512 : eb * 512 + 512], lhs, rhs,
                            start=(kk == 0), stop=(kk == nsup - 1),
                        )
                tmixtok = p.inc(tmix, "t")
                pfree[b] = tmixtok
                p.wait(vec, "t", tmixtok)
                vtok, stok, rstd, negmur = emit_ln_stats(p, lb, 2 + b, pso, DM)
                tmps = tmp_sb[:, b * DM : (b + 1) * DM]
                outs = out_sb[:, b * DM : (b + 1) * DM]
                p.wait(sca, "v", vtok)
                p.wait(sca, "s", stok)
                stok2 = p.inc(
                    sca.activation(tmps, pso[:], AF.Identity, bias=negmur, scale=rstd), "s"
                )
                lb.free_s[2 + b] = stok2
                p.wait(gps, "s", stok2)
                p.wait(gps, "r%d" % b, dres)
                p.wait(gps, "e%d" % b)
                vtok2 = p.inc(gps.tensor_tensor(outs, tmps, x2s, ALU.add), "gc")
                p.wait(syn, "gc", vtok2)
                p.inc(syn.dma_start(yseq[lt * 128 : (lt + 1) * 128, :], outs), "e%d" % b, 16)
                ofree[b] = (vtok2, stok2)
            p.barrier()

        p.wait(syn, "e0")
        p.wait(syn, "e1")



def emit_l1_ffn(nc, ctx, p, x, y):
    gw = dram_cached(nc, "gw", [NEXP, DM], F32, kind="ExternalInput").ap()
    gb = dram_cached(nc, "gb", [1, NEXP], F32, kind="ExternalInput").ap()
    rmsw = dram_cached(nc, "rmsw", [1, NEXP], F32, kind="ExternalInput").ap()
    w1 = dram_cached(nc, "w1", [NEXP, DM, 2048], F32, kind="ExternalInput").ap()
    b1 = dram_cached(nc, "b1", [NEXP * 2048], F32, kind="ExternalInput").ap()
    w2 = dram_cached(nc, "w2", [NEXP * 2048, DM], F32, kind="ExternalInput").ap()
    b2 = dram_cached(nc, "b2", [NEXP, DM], F32, kind="ExternalInput").ap()
    ones = dram_cached(nc, "ones", [1, 128], F32, kind="ExternalInput").ap()
    onescol = dram_cached(nc, "onescol", [128, 1], F32, kind="ExternalInput").ap()
    idf = dram_cached(nc, "idf", [128, 128], F32, kind="ExternalInput").ap()
    w1d = dram_cached(nc, "w1dd", [NEXP * NH * 128, DM], BF16, kind="Internal").ap()
    w2d = dram_cached(nc, "w2dd", [NEXP * 2048, DM], BF16, kind="Internal").ap()
    xnd = dram_cached(nc, "xnd_d", [TOK, DM], BF16, kind="Internal").ap()

    with ExitStack() as ctx:
        E = ctx.enter_context
        ten, vec, sca, gps, syn = p.ten, p.vec, p.sca, p.gps, p.syn
        lb = LNBufs(nc, ctx, 4, "d_")
        lb.init(p)

        ones_sb = E(nc.sbuf_tensor("d_ones_sb", [1, 128], F32))
        onescol_sb = E(nc.sbuf_tensor("d_onescol_sb", [128, 1], F32))
        idf_sb = E(nc.sbuf_tensor("d_idf_sb", [128, 128], F32))
        gw2_sb = E(nc.sbuf_tensor("d_gw2_sb", [128, NE * 16], BF16))
        gb2row_sb = E(nc.sbuf_tensor("d_gb2row_sb", [1, 16], F32))
        eps_sb = E(nc.sbuf_tensor("d_eps_sb", [128, 1], F32))
        gbrow_sb = E(nc.sbuf_tensor("d_gbrow_sb", [1, NEXP], F32))
        rmsrow_sb = E(nc.sbuf_tensor("d_rmsrow_sb", [1, NEXP], F32))
        rmsbc_sb = E(nc.sbuf_tensor("d_rmsbc_sb", [128, NEXP], F32))
        b1raw_sb = E(nc.sbuf_tensor("d_b1raw_sb", [128, 128], F32))
        b1col_sb = E(nc.sbuf_tensor("d_b1col_sb", [128, 128], F32))
        B2bc_sb = E(nc.sbuf_tensor("d_B2bc_sb", [128, DM], F32))
        gb_sb = E(nc.sbuf_tensor("d_gb_sb", [128, 2 * 16], F32))
        gb2_sb = E(nc.sbuf_tensor("d_gb2_sb", [128, 2 * 16], F32))
        ng_sb = E(nc.sbuf_tensor("d_ng_sb", [128, 32], F32))
        xst_sb = E(nc.sbuf_tensor("d_xst_sb", [128, 2 * DM], F32))
        xnst_sb = E(nc.sbuf_tensor("d_xnst_sb", [128, DM], BF16))
        x2st_sb = E(nc.sbuf_tensor("d_x2st_sb", [128, 2 * DM], F32))
        tmp_sb = E(nc.sbuf_tensor("d_tmp_sb", [128, 2 * DM], F32))
        out_sb = E(nc.sbuf_tensor("d_out_sb", [128, 2 * DM], F32))
        y_sb = E(nc.sbuf_tensor("d_y_sb", [128, DM], F32))
        xnT_sb = E(nc.sbuf_tensor("d_xnT_sb", [128, 2 * NE * TC2], BF16))
        col_sb = E(nc.sbuf_tensor("d_col_sb", [128, 4 * NE * 128], BF16))
        w2t_sb = E(nc.sbuf_tensor("d_w2t_sb", [128, 4 * DM], BF16))
        hall_sb = E(nc.sbuf_tensor("d_hall_sb", [128, NK2 * TC2], BF16))

        ps8 = [E(nc.psum_tensor(f"d_ps8_{i}", [128, 512], F32)) for i in range(8)]
        b2_sb = tmp_sb[0:8, 0:DM]
        b2row_sb = y_sb[0:1, 0:DM]
        graw_sb = xst_sb[0:8, 0:DM]


        p.inc(syn.dma_start(ones_sb[:], ones), "d", 16)
        p.inc(syn.dma_start(onescol_sb[:], onescol), "d", 16)
        p.inc(syn.dma_start(idf_sb[:], idf), "d", 16)
        p.inc(syn.dma_start(gbrow_sb[:], gb), "d", 16)
        p.inc(syn.dma_start(rmsrow_sb[:], rmsw), "d", 16)
        p.inc(syn.dma_start(b1raw_sb[:], b1.rearrange("(a b) -> a b", a=128)), "d", 16)
        p.inc(syn.dma_start(b2_sb, b2), "d", 16)
        p.inc(syn.dma_start(graw_sb, gw), "d", 16)
        p.pump(10**9)
        p.wait(ten, "d")
        p.inc(ten.matmul(ps8[3][:, 0:8], ones_sb[:], rmsrow_sb[:], start=True, stop=True), "t")
        p.wait(sca, "t")
        p.inc(sca.activation(rmsbc_sb[:], ps8[3][:, 0:8], AF.Identity), "s")
        p.inc(vec.memset(eps_sb[:], EPS32), "v")
        p.wait(vec, "d")
        p.inc(vec.tensor_copy(gb2row_sb[0:1, 0:8], gbrow_sb[0:1, 0:8]), "v")
        p.inc(vec.tensor_mul(gb2row_sb[0:1, 8:16], gbrow_sb[0:1, 0:8], rmsrow_sb[0:1, 0:8]), "v")
        for et in range(NE):
            p.inc(
                ten.transpose(
                    ps8[4][:, 0:8], graw_sb[0:8, et * 128 : (et + 1) * 128], idf_sb[0:8, 0:8]
                ),
                "t",
            )
            p.wait(sca, "t")
            p.wait(sca, "s")
            stok_g = p.inc(
                sca.activation(gw2_sb[:, et * 16 : et * 16 + 8], ps8[4][:, 0:8], AF.Identity),
                "s",
            )
            p.wait(vec, "s", stok_g)
            p.inc(
                vec.tensor_tensor(
                    gw2_sb[:, et * 16 + 8 : et * 16 + 16],
                    gw2_sb[:, et * 16 : et * 16 + 8],
                    rmsbc_sb[:],
                    ALU.mult,
                ),
                "v",
            )
            p.wait(ten, "s", stok_g)
        p.inc(ten.transpose(ps8[0][:, 0:128], b1raw_sb[:], idf_sb[:]), "t")
        p.wait(sca, "t")
        p.inc(sca.activation(b1col_sb[:], ps8[0][:, 0:128], AF.Identity), "s")
        p.wait(ten, "s")
        for dh in range(2):
            p.inc(
                ten.matmul(ps8[1][0:1, 0:512], onescol_sb[0:8, :], b2_sb[0:8, dh * 512 : dh * 512 + 512], start=True, stop=True),
                "t",
            )
            p.wait(sca, "t")
            p.inc(sca.activation(B2bc_sb[0:1, dh * 512 : dh * 512 + 512], ps8[1][0:1, 0:512], AF.Identity), "s")
            p.wait(ten, "s")
        p.barrier()

        stage_free = [(0, 0, 0)] * 2
        era_store_tok = {}
        era_next = [0]

        def emit_era_tile(i):
            b = i % 2
            vf, sf, ef = stage_free[b]
            p.wait(syn, "v", vf)
            p.wait(syn, "s", sf)
            p.wait(syn, "e%d" % b, ef)
            xs = xst_sb[:, b * DM : (b + 1) * DM]
            xns = xnst_sb[:, 0:DM]
            row = i * 128
            p.wait(syn, "d%d" % b)
            dtok = p.inc(syn.dma_start(xs, x[row : row + 128, :]), "d%d" % b, 16)
            p.wait(vec, "d%d" % b, dtok)
            vtok, stok, rstd, negmur = emit_ln_stats(p, lb, b, xs, DM)
            p.wait(vec, "v")
            p.wait(vec, "e0")
            p.wait(vec, "e1")
            aptok = p.inc(
                vec.tensor_scalar(xns, xs, rstd, negmur, ALU.mult, ALU.add), "v"
            )
            lb.free_v[b] = aptok
            p.wait(syn, "v", aptok)
            p.wait(syn, "e%d" % b)
            etok = p.inc(syn.dma_start(xnd[row : row + 128, :], xns), "e%d" % b, 16)
            stage_free[b] = (aptok, stok, etok)
            era_store_tok[i // (TC2 // 128)] = (p.val["e0"], p.val["e1"])

        def emit_some_era(n):
            while n > 0 and era_next[0] < TOK // 128:
                emit_era_tile(era_next[0])
                era_next[0] += 1
                n -= 1

        emit_some_era(2 * (TC2 // 128))

        ps_rd = [("s", 0)] * 8
        t_xnT_prev = [0, 0]
        t_mm2_prev = 0
        xtok = [0, 0]
        xsem = ["x", "xs"]
        colfree = [0] * 4
        ctok = [0] * 4
        w2free = [0] * 4
        utok = [0] * 4
        col_q = [syn, gps, syn, gps]
        col_sem = ["c0", "c1", "c2", "c3"]
        w2_q = [syn, gps, syn, gps]
        w2_sem = ["u0", "u1", "u2", "u3"]

        def issue_transposes(c):
            bx = c % 2
            tok0 = c * TC2
            p.wait(syn, "t", t_xnT_prev[bx])
            e0t, e1t = era_store_tok[c]
            p.wait(syn, "e0", e0t)
            p.wait(syn, "e1", e1t)
            for et in range(NE):
                p.inc(
                    syn.dma_start(
                        xnT_sb[:, bx * NE * TC2 + et * TC2 : bx * NE * TC2 + (et + 1) * TC2],
                        xnd[tok0 : tok0 + TC2, et * 128 : (et + 1) * 128],
                        transpose=True,
                    ),
                    xsem[bx],
                    16,
                )
            xtok[bx] = p.val[xsem[bx]]

        def issue_col(gidx):
            sl = gidx % 4
            kk = gidx % NK2
            q = col_q[sl]
            p.wait(q, "t", colfree[sl])
            ctok[sl] = p.inc(
                q.dma_start(
                    col_sb[:, sl * 1024 : (sl + 1) * 1024],
                    w1d[kk * 128 : (kk + 1) * 128, :],
                ),
                col_sem[sl],
                16,
            )

        def issue_w2(gidx):
            sl = gidx % 4
            kk = gidx % NK2
            q = w2_q[sl]
            p.wait(q, "t", w2free[sl])
            utok[sl] = p.inc(
                q.dma_start(w2t_sb[:, sl * DM : (sl + 1) * DM], w2d[kk * 128 : (kk + 1) * 128, :]),
                w2_sem[sl],
                16,
            )

        p.wait(syn, "dc")
        p.wait(gps, "dc")
        p.wait(sca, "dc")
        gate_slot_free = [(0, 0)] * 2
        issue_transposes(0)
        for k in range(3):
            issue_col(k)
        for k in range(3):
            issue_w2(k)
        prev_evac_free = [(0, 0)] * 2
        for c in range(NCH2):
            bx = c % 2
            xbase = bx * NE * TC2
            tok0 = c * TC2
            if c + 1 < NCH2:
                issue_transposes(c + 1)

            gslot_free = gate_slot_free

            def emit_gate_tile(i):
                gbi = i % 2
                gbank = 4 + i
                lg16 = gb_sb[:, gbi * 16 : gbi * 16 + 16]
                u2 = gb2_sb[:, gbi * 16 : gbi * 16 + 8]
                msq = gb2_sb[:, gbi * 16 + 8 : gbi * 16 + 9]
                negm2 = gb2_sb[:, gbi * 16 + 9 : gbi * 16 + 10]
                ssum = gb2_sb[:, gbi * 16 + 10 : gbi * 16 + 11]
                t2 = gb2_sb[:, gbi * 16 + 11 : gbi * 16 + 12]
                rr = gb2_sb[:, gbi * 16 + 12 : gbi * 16 + 13]
                nm, tok = ps_rd[gbank]
                p.wait(ten, nm, tok)
                if i == 0:
                    p.wait(ten, xsem[bx], xtok[bx])
                last = None
                for et in range(NE):
                    last = ten.matmul(
                        ps8[gbank][:, 0:16],
                        xnT_sb[:, xbase + et * TC2 + i * 128 : xbase + et * TC2 + i * 128 + 128],
                        gw2_sb[:, et * 16 : (et + 1) * 16],
                        start=(et == 0),
                        stop=False,
                    )
                last = ten.matmul(ps8[gbank][:, 0:16], ones_sb[:], gb2row_sb[:], start=False, stop=True)
                ttok = p.inc(last, "t")
                vfree, sfree = gslot_free[gbi]
                p.wait(sca, "t", ttok)
                p.wait(sca, "v", vfree)
                p.wait(sca, "s", sfree)
                p.inc(sca.activation(lg16, ps8[gbank][:, 0:16], AF.Identity), "s")
                p.wait(sca, "s")
                stok_sq = p.inc(
                    sca.activation(u2, lg16[:, 0:8], AF.Square, accum_out=msq), "s"
                )
                ps_rd[gbank] = ("s", stok_sq - 1)
                p.wait(sca, "s")
                p.inc(sca.activation(rr, msq, AF.Ln, scale=1.0 / 8.0, bias=eps_sb[:]), "s")
                p.wait(sca, "s")
                stok_r = p.inc(sca.activation(rr, rr, AF.Exp, scale=-0.5), "s")
                p.wait(vec, "s", stok_r)
                p.inc(vec.tensor_scalar(u2, lg16[:, 8:16], rr, 2.0, ALU.mult, ALU.mult), "v")
                p.wait(vec, "v")
                vtok_m = p.inc(vec.reduce_max(negm2, u2, axis=AX.X, negate=True), "v")
                p.wait(sca, "v", vtok_m)
                stok_e = p.inc(
                    sca.activation(u2, u2, AF.Exp, bias=negm2, accum_out=ssum), "s"
                )
                p.wait(vec, "s", stok_e)
                p.inc(vec.tensor_scalar(t2, ssum, 1e-6, 1.0, ALU.mult, ALU.add), "v")
                p.wait(vec, "v")
                vtok_n = p.inc(vec.reciprocal(ng_sb[:, (c % 2) * 4 + i : (c % 2) * 4 + i + 1], t2), "v")
                gslot_free[gbi] = (vtok_n, stok_e)

            for kk in range(NK2):
                if kk in (12, 24, 36, 48):
                    emit_gate_tile(kk // 12 - 1)
                gidx = c * NK2 + kk
                sl = gidx % 4
                if gidx + 3 < NCH2 * NK2:
                    issue_col(gidx + 3)
                bank = kk % 3
                nm, tok = ps_rd[bank]
                p.wait(ten, nm, tok)
                p.wait(ten, col_sem[sl], ctok[sl])
                if kk == 0:
                    p.wait(ten, xsem[bx], xtok[bx])
                last = None
                for et in range(NE):
                    last = ten.matmul(
                        ps8[bank][:],
                        col_sb[:, sl * 1024 + et * 128 : sl * 1024 + et * 128 + 128],
                        xnT_sb[:, xbase + et * TC2 : xbase + (et + 1) * TC2],
                        start=(et == 0),
                        stop=(et == NE - 1),
                    )
                ttok = p.inc(last, "t")
                colfree[sl] = ttok
                if kk == NK2 - 1:
                    t_xnT_prev[bx] = ttok
                p.wait(sca, "t", ttok)
                if kk == 0:
                    p.wait(sca, "t", t_mm2_prev)
                ps_rd[bank] = (
                    "s",
                    p.inc(
                        sca.activation(
                            hall_sb[:, kk * TC2 : (kk + 1) * TC2],
                            ps8[bank][:],
                            AF.Gelu,
                            bias=b1col_sb[:, kk : kk + 1],
                        ),
                        "s",
                    ),
                )
            s_hall_done = p.val["s"]
            emit_some_era(TC2 // 128)

            p.wait(ten, "s", s_hall_done)
            for bank in range(8):
                nm, tok = ps_rd[bank]
                p.wait(ten, nm, tok)
            for kk in range(NK2):
                gidx = c * NK2 + kk
                sl = gidx % 4
                if gidx + 3 < NCH2 * NK2:
                    issue_w2(gidx + 3)
                p.wait(ten, w2_sem[sl], utok[sl])
                last = None
                for tt in range(TC2 // 128):
                    for dh in range(2):
                        last = ten.matmul(
                            ps8[tt * 2 + dh][:],
                            hall_sb[:, kk * TC2 + tt * 128 : kk * TC2 + tt * 128 + 128],
                            w2t_sb[:, sl * DM + dh * 512 : sl * DM + dh * 512 + 512],
                            start=(kk == 0),
                            stop=False,
                        )
                ttok = p.inc(last, "t")
                w2free[sl] = ttok
                if kk == NK2 - 1:
                    t_mm2_prev = ttok
            last = None
            for tt in range(TC2 // 128):
                for dh in range(2):
                    last = ten.matmul(
                        ps8[tt * 2 + dh][:],
                        ones_sb[:],
                        B2bc_sb[0:1, dh * 512 : dh * 512 + 512],
                        start=False,
                        stop=True,
                    )
            p.inc(last, "t")
            tz2 = p.val["t"]

            evac_free = prev_evac_free
            prev_ys_rd = getattr(p, "d_prev_ys_rd", (0, 0))
            for tt in range(TC2 // 128):
                b = tt % 2
                gi = (c % 2) * 4 + tt
                row = tok0 + tt * 128
                vf, sf = evac_free[b]
                p.wait(syn, "v", vf)
                x2s = x2st_sb[:, b * DM : (b + 1) * DM]
                dres = p.inc(syn.dma_start(x2s, x[row : row + 128, :]), "r%d" % b, 16)
                ys = y_sb[:, 0:DM]
                p.wait(sca, "t", tz2)
                p.wait(sca, "v", prev_ys_rd[0])
                p.wait(sca, "s", prev_ys_rd[1])
                p.wait(sca, "v", vf)
                p.wait(sca, "s", sf)
                stk = 0
                for dh in range(2):
                    stk = p.inc(
                        sca.activation(
                            ys[:, dh * 512 : dh * 512 + 512],
                            ps8[tt * 2 + dh][:],
                            AF.Identity,
                            scale=ng_sb[:, gi : gi + 1],
                        ),
                        "s",
                    )
                    ps_rd[tt * 2 + dh] = ("s", stk)
                p.wait(vec, "s", stk)
                vtok, stok, rstd, negmur = emit_ln_stats(p, lb, 2 + b, ys, DM)
                tmps = tmp_sb[:, b * DM : (b + 1) * DM]
                outs = out_sb[:, b * DM : (b + 1) * DM]
                p.wait(sca, "v", vtok)
                p.wait(sca, "s", stok)
                stok2 = p.inc(
                    sca.activation(tmps, ys, AF.Identity, bias=negmur, scale=rstd), "s"
                )
                lb.free_s[2 + b] = stok2
                prev_ys_rd = (vtok, stok2)
                p.d_prev_ys_rd = prev_ys_rd
                p.wait(vec, "s", stok2)
                p.wait(vec, "r%d" % b, dres)
                p.wait(vec, "e%d" % b)
                vtok2 = p.inc(vec.tensor_tensor(outs, tmps, x2s, ALU.add), "v")
                p.wait(syn, "v", vtok2)
                p.inc(syn.dma_start(y[row : row + 128, :], outs), "e%d" % b, 16)
                evac_free[b] = (vtok2, stok2)
            prev_evac_free = evac_free

        p.wait(syn, "e0")
        p.wait(syn, "e1")

def build_fused():
    nc = bass.Bass("TRN2", target_bir_lowering=False)
    xin = nc.dram_tensor("xin", [TOK, DM], F32, kind="ExternalInput").ap()
    yout = nc.dram_tensor("yout", [TOK, DM], F32, kind="ExternalOutput").ap()
    x1d = nc.dram_tensor("x1d", [TOK, DM], F32, kind="Internal").ap()
    x2d = nc.dram_tensor("x2d", [TOK, DM], F32, kind="Internal").ap()
    x3d = nc.dram_tensor("x3d", [TOK, DM], F32, kind="Internal").ap()
    with ExitStack() as octx:
        p = Prog(nc, octx, extra=("w0", "w1", "w2", "w3", "u0", "u1", "u2", "u3",
                                  "g0", "g1", "k0", "k1", "k2", "k3", "xs", "wp",
                                  "c0", "c1", "c2", "c3", "c4", "c5", "dc", "gc"))
        blk = octx.enter_context(nc.Block())
        Wq_in = dram_cached(nc, "Wq", [DM, DM], F32, kind="ExternalInput").ap()
        Wk_in = dram_cached(nc, "Wk", [DM, DM], F32, kind="ExternalInput").ap()
        Wv_in = dram_cached(nc, "Wv", [DM, DM], F32, kind="ExternalInput").ap()
        wstage = dram_cached(nc, "wstage", [3, DM, DM], BF16, kind="Internal").ap()
        for m, W in enumerate((Wq_in, Wk_in, Wv_in)):
            p.inc(p.gps.dma_start(wstage[m], W), "g", 16)
        w1_in = dram_cached(nc, "w1", [NEXP, DM, 2048], F32, kind="ExternalInput").ap()
        w2_in = dram_cached(nc, "w2", [NEXP * 2048, DM], F32, kind="ExternalInput").ap()
        w1dd = dram_cached(nc, "w1dd", [NEXP * NH * 128, DM], BF16, kind="Internal").ap()
        w2dd = dram_cached(nc, "w2dd", [NEXP * 2048, DM], BF16, kind="Internal").ap()
        pieces = []
        for e in range(NEXP):
            for ht in range(NH):
                dstap = custom_ap(
                    w1dd,
                    (e * NH + ht) * 128 * DM,
                    [[DM, 128], [128, NE], [1, 128]],
                )
                srcap = custom_ap(
                    w1_in,
                    e * DM * 2048 + ht * 128,
                    [[2048, 128], [128 * 2048, NE], [1, 128]],
                )
                pieces.append((dstap, srcap))
        for r in range(0, NEXP * 2048, 512):
            pieces.append((w2dd[r : r + 512, :], w2_in[r : r + 512, :]))
        p.cast_pieces = pieces
        p.cast_idx = 0

        def pump(n=1):
            while n > 0 and p.cast_idx < len(p.cast_pieces):
                dst, srcp = p.cast_pieces[p.cast_idx]
                p.inc(p.gps.dma_start(dst, srcp), "dc", 16)
                p.cast_idx += 1
                n -= 1

        p.pump = pump
        emit_l0_attn(nc, octx, p, xin, x1d)
        p.barrier()
        emit_l0_ffn(nc, octx, p, x1d, x2d)
        p.barrier()
        emit_l1_attn(nc, octx, p, x2d, x3d)
        p.barrier()
        emit_l1_ffn(nc, octx, p, x3d, yout)
        p.barrier()
    return nc


def kernel(
    L0_attn_X,
    L0_attn_Wq,
    L0_attn_bq,
    L0_attn_Wk,
    L0_attn_bk,
    L0_attn_Wv,
    L0_attn_bv,
    L0_ffn_k0,
    L0_ffn_k1,
    L0_ffn_k2,
    L0_ffn_s0,
    L0_ffn_s1,
    L0_ffn_s2,
    L1_attn_W,
    L1_attn_b,
    L1_ffn_gate_w,
    L1_ffn_gate_b,
    L1_ffn_rms_w,
    L1_ffn_w1,
    L1_ffn_b1,
    L1_ffn_w2,
    L1_ffn_b2,
):
    X = np.asarray(L0_attn_X, dtype=np.float32)
    ones, idf, cmask = host_consts()
    idb = np.eye(128, dtype=np.float32).astype(ml_dtypes.bfloat16)
    onescol = np.ones((128, 1), dtype=np.float32)
    nc = build_fused()
    in_maps = []
    for c in range(NCORES):
        xs = np.ascontiguousarray(
            X[c * SEQ_PER_CORE : (c + 1) * SEQ_PER_CORE].reshape(TOK, DM)
        )
        in_maps.append(
            {
                "xin": xs,
                "Wq": np.asarray(L0_attn_Wq), "Wk": np.asarray(L0_attn_Wk),
                "Wv": np.asarray(L0_attn_Wv),
                "bq": np.asarray(L0_attn_bq), "bk": np.asarray(L0_attn_bk),
                "bv": np.asarray(L0_attn_bv),
                "ones": ones, "idf": idf, "cmask": cmask, "idb": idb,
                "onescol": onescol,
                "k0": np.asarray(L0_ffn_k0).view(np.uint8),
                "k1": np.asarray(L0_ffn_k1).view(np.uint8),
                "k2": np.asarray(L0_ffn_k2).view(np.uint8),
                "s0": np.asarray(L0_ffn_s0, dtype=np.float32).reshape(1, 1),
                "s1": np.asarray(L0_ffn_s1, dtype=np.float32).reshape(1, 1),
                "s2": np.asarray(L0_ffn_s2, dtype=np.float32).reshape(1, 1),
                "W": np.asarray(L1_attn_W),
                "bb": np.asarray(L1_attn_b).reshape(1, 32),
                "gw": np.asarray(L1_ffn_gate_w),
                "gb": np.asarray(L1_ffn_gate_b).reshape(1, 8),
                "rmsw": np.asarray(L1_ffn_rms_w).reshape(1, 8),
                "w1": np.asarray(L1_ffn_w1),
                "b1": np.asarray(L1_ffn_b1).reshape(-1),
                "w2": np.asarray(L1_ffn_w2).reshape(-1, DM),
                "b2": np.asarray(L1_ffn_b2),
            }
        )
    res = run_bass_kernel_spmd(nc, in_maps, core_ids=list(range(NCORES)))
    out = np.concatenate(
        [r["yout"].reshape(SEQ_PER_CORE, SEQ, DM) for r in res.results], axis=0
    )
    return np.ascontiguousarray(out.astype(np.float32))
